# version 05
# baseline (speedup 1.0000x reference)
import contextlib
import numpy as np
import concourse.bass as bass
import concourse.mybir as mybir
from concourse.bass_utils import run_bass_kernel_spmd

F32 = mybir.dt.float32
BF16 = mybir.dt.bfloat16
I32 = mybir.dt.int32
AF = mybir.ActivationFunctionType
ALU = mybir.AluOpType
AX = mybir.AxisListType

TWO_PI = 6.283185307179586
INV_2PI = 1.0 / TWO_PI
CW1 = 6.25
CW2 = 68.0 / 2048.0
CW3 = float(np.float32(np.float64(TWO_PI) - CW1 - CW2))
MAGIC = 12582912.0
HALF_PI = float(np.float32(np.pi / 2))
PI_F = float(np.float32(np.pi))
TIE = 5e-11

T_SEQ = 4096
D = 1024
NTOK_ALL = 32768

_BLK = {"t": "tensor", "v": "vector", "s": "scalar", "g": "gpsimd", "q": "sync"}


class EngCtx:
    NAMES = ("t", "v", "s", "g", "q")

    def __init__(self, nc, st):
        self.nc = nc
        self.sem = {n: st.enter_context(nc.semaphore("pc_" + n)) for n in self.NAMES}
        self.val = {n: 0 for n in self.NAMES}
        self.last = {n: None for n in self.NAMES}

    def em(self, en, inst, dma=False):
        self.last[en] = [inst, False, 16 if dma else 1]
        if dma:
            self._seal(en)
        return inst

    def _seal(self, en):
        li = self.last[en]
        if li is None:
            return
        inst, inced, step = li
        if not inced:
            inst.then_inc(self.sem[en], step)
            self.val[en] += step
            li[1] = True

    def selfsync(self, en, q):
        self._seal(en)
        q.wait_ge(self.sem[en], self.val[en])

    def fence(self, dst_q, srcs):
        for s_en in srcs:
            if self.last[s_en] is None:
                continue
            self._seal(s_en)
            dst_q.wait_ge(self.sem[s_en], self.val[s_en])

    def mark(self, en):
        if self.last[en] is None:
            return (en, 0)
        self._seal(en)
        return (en, self.val[en])

    def wait(self, q, marks):
        for en, val in marks:
            if val > 0:
                q.wait_ge(self.sem[en], val)


def perch(v):
    return np.ascontiguousarray(np.asarray(v, np.float32).reshape(8, 128).T)


def build_host_inputs(inputs, core):
    x = np.asarray(inputs["L0_attn_x"], np.float32)
    d = {}
    d["xT"] = np.ascontiguousarray(x[core].T)
    d["xaT"] = np.ascontiguousarray(x.reshape(NTOK_ALL, D).T)
    wgT = np.asarray(inputs["L0_ffn_w_gate"], np.float32).T
    lhsA = np.zeros((128, 8, 9), np.float32)
    lhsA[:, :, :8] = wgT.reshape(8, 128, 8).transpose(1, 0, 2)
    lhsA[:, :, 8] = 1.0
    d["lhsA"] = np.ascontiguousarray(lhsA.reshape(128, 72))
    d["bias_row"] = np.asarray(inputs["L0_ffn_expert_bias"], np.float32).reshape(1, 8)
    for nm in ("qk_w", "v_w", "g_w", "o_w"):
        d[nm + "T"] = np.ascontiguousarray(np.asarray(inputs["L0_attn_" + nm], np.float32).T)
    d["gate_wT"] = np.ascontiguousarray(np.asarray(inputs["L0_attn_gate_w"], np.float32).T)
    d["up_wT"] = np.ascontiguousarray(np.asarray(inputs["L0_attn_up_w"], np.float32).T)
    d["down_wT"] = np.ascontiguousarray(np.asarray(inputs["L0_attn_down_w"], np.float32).T)
    for nm in ("ln1_w", "ln1_b", "conv_b", "g_b", "onorm_w", "onorm_b",
               "ln2_w", "ln2_b", "gamma1", "gamma2"):
        d[nm] = perch(inputs["L0_attn_" + nm])
    d["conv_w"] = np.ascontiguousarray(
        np.asarray(inputs["L0_attn_conv_w"], np.float32)[:, 0, :].reshape(8, 128, 4).transpose(1, 0, 2))
    d["gammaA"] = perch(np.repeat(np.asarray(inputs["L0_attn_gamma_attn"], np.float32).reshape(16), 64))
    for nm in ("sw1", "sw2", "sw3"):
        d[nm + "T"] = np.ascontiguousarray(np.asarray(inputs["L0_ffn_" + nm], np.float32).T)
    d["w1"] = np.ascontiguousarray(np.asarray(inputs["L0_ffn_w1"], np.float32))
    d["w3"] = np.ascontiguousarray(np.asarray(inputs["L0_ffn_w3"], np.float32))
    d["w2"] = np.ascontiguousarray(np.asarray(inputs["L0_ffn_w2"], np.float32))
    for nm in ("w_x", "w_y", "gi_w", "ga_w"):
        d[nm] = np.ascontiguousarray(np.asarray(inputs["L1_attn_" + nm], np.float32))
    for nm in ("b_x", "b_y", "conv_b", "gi_b", "ga_b", "a_param"):
        d["l1_" + nm] = perch(inputs["L1_attn_" + nm])
    d["conv1_w"] = np.ascontiguousarray(
        np.asarray(inputs["L1_attn_conv_w"], np.float32).T.reshape(8, 128, 4).transpose(1, 0, 2))
    d["gate1_w"] = np.ascontiguousarray(np.asarray(inputs["L1_ffn_gate_w"], np.float32))
    d["gate1_bc"] = np.asarray(inputs["L1_ffn_gate_b"], np.float32).reshape(8, 1)
    d["w1f"] = np.ascontiguousarray(np.asarray(inputs["L1_ffn_w1"], np.float32))
    d["w3f"] = np.ascontiguousarray(np.asarray(inputs["L1_ffn_w3"], np.float32))
    d["w2f"] = np.ascontiguousarray(np.asarray(inputs["L1_ffn_w2"], np.float32))
    d["mask0"] = np.full((128, 1), 1.0 if core == 0 else 0.0, np.float32)
    d["identf"] = np.eye(128, dtype=np.float32)
    idx = np.linspace(0.0, 1.0, 64).astype(np.float32)
    freqs = ((10.0 ** idx)[None, :] * np.arange(16, dtype=np.float32)[:, None]).astype(np.float32)
    d["freqs"] = perch(freqs.reshape(1024))
    hs = np.zeros((128, 8, 16), np.float32)
    for cc in range(8):
        for p in range(128):
            hs[p, cc, 2 * cc + (p >> 6)] = 1.0
    d["headsel"] = np.ascontiguousarray(hs)
    d["headexp"] = np.ascontiguousarray(np.transpose(hs, (2, 1, 0)))
    return d


DRAM_SHAPES = {
    "xT": ([D, T_SEQ], F32), "xaT": ([D, NTOK_ALL], F32),
    "lhsA": ([128, 72], F32), "bias_row": ([1, 8], F32),
    "qk_wT": ([D, D], F32), "v_wT": ([D, D], F32), "g_wT": ([D, D], F32),
    "o_wT": ([D, D], F32), "gate_wT": ([D, 4096], F32), "up_wT": ([D, 4096], F32),
    "down_wT": ([4096, D], F32),
    "ln1_w": ([128, 8], F32), "ln1_b": ([128, 8], F32), "conv_b": ([128, 8], F32),
    "g_b": ([128, 8], F32), "onorm_w": ([128, 8], F32), "onorm_b": ([128, 8], F32),
    "ln2_w": ([128, 8], F32), "ln2_b": ([128, 8], F32),
    "gamma1": ([128, 8], F32), "gamma2": ([128, 8], F32),
    "conv_w": ([128, 8, 4], F32), "gammaA": ([128, 8], F32),
    "sw1T": ([D, D], F32), "sw2T": ([D, D], F32), "sw3T": ([D, D], F32),
    "w1": ([8, D, 512], F32), "w3": ([8, D, 512], F32), "w2": ([8, 512, D], F32),
    "w_x": ([D, D], F32), "w_y": ([D, D], F32), "gi_w": ([D, D], F32), "ga_w": ([D, D], F32),
    "l1_b_x": ([128, 8], F32), "l1_b_y": ([128, 8], F32), "l1_conv_b": ([128, 8], F32),
    "l1_gi_b": ([128, 8], F32), "l1_ga_b": ([128, 8], F32), "l1_a_param": ([128, 8], F32),
    "conv1_w": ([128, 8, 4], F32),
    "gate1_w": ([D, 8], F32), "gate1_bc": ([8, 1], F32),
    "w1f": ([8, D, 4096], F32), "w3f": ([8, D, 4096], F32), "w2f": ([8, 4096, D], F32),
    "mask0": ([128, 1], F32), "identf": ([128, 128], F32),
    "freqs": ([128, 8], F32), "headsel": ([128, 8, 16], F32), "headexp": ([16, 8, 128], F32),
    "jslot": ([128, 512], F32), "wslot": ([128, 512], F32),
    "offs_d": ([1, 8], F32), "xn8T_d": ([D, 8], F32), "mrow_d": ([1, 512], F32),
    "u1tab": ([64, 512], F32), "u3tab": ([64, 512], F32), "hs_d": ([64, 512], F32),
    "acc1": ([D, T_SEQ], F32), "acc3": ([D, T_SEQ], F32), "h2_d": ([D, T_SEQ], BF16), "xnb_d": ([D, T_SEQ], BF16), "dbg": ([128, 96], F32), "dbg2": ([9, 512], F32),
    "xn1f_d": ([D, T_SEQ], F32), "xn1b_d": ([D, T_SEQ], BF16), "crow_d": ([8, T_SEQ], F32),
    "outT": ([D, T_SEQ], F32), "dbgA": ([D, T_SEQ], F32),
}


def dram_io(nc, which):
    kinds = {"in": "ExternalInput", "out": "ExternalOutput", "int": "Internal"}
    io = {}
    for nm, kind in which.items():
        sh, dt = DRAM_SHAPES[nm]
        io[nm] = nc.dram_tensor(nm, sh, dt, kind=kinds[kind]).ap()
    return io


P0_SKIP = set()
P0_DBG_TC = -1

def build_p0(nc, ec, block, io, st):
    S = lambda name, shape, dt=F32: st.enter_context(nc.sbuf_tensor(name, shape, dt))
    P = lambda name, shape: st.enter_context(nc.psum_tensor(name, shape, F32))
    em = ec.em

    xa = [S("p0_xa0", [128, 8, 512]), S("p0_xa1", [128, 8, 512])]
    sq = [S("p0_sq0", [128, 8, 512]), S("p0_sq1", [128, 8, 512])]
    lhsA = S("p0_lhsA", [128, 72])
    ones_r = S("p0_onesr", [1, 128])
    identf = S("p0_id", [128, 128])
    bias_b = S("p0_biasb", [128, 8])
    iota8 = S("p0_iota8", [128, 8])
    wgs = S("p0_wgs", [9, 1])
    m_row = S("p0_m", [1, 512])
    v_row = S("p0_v", [1, 512])
    rv_row = S("p0_rv", [1, 512])
    mb8 = S("p0_mb8", [8, 512])
    lt = S("p0_lt", [128, 4, 8])
    scores = S("p0_scores", [128, 4, 8])
    sp = S("p0_sp", [128, 4, 8])
    tmp8 = S("p0_tmp8", [128, 4, 8])
    keep = S("p0_keep", [128, 4, 8])
    k1 = S("p0_k1", [128, 4, 8])
    k2 = S("p0_k2", [128, 4, 8])
    gs = S("p0_gs", [128, 4, 4])
    gs2 = S("p0_gs2", [128, 4, 4])
    red = S("p0_red", [128, 4])
    t9 = S("p0_t9", [9, 512])
    cnt = S("p0_cnt", [128, 8])
    jstage = S("p0_jstage", [128, 512])
    wstage = S("p0_wstage", [128, 512])
    xn8 = S("p0_xn8", [128, 8, 8])
    mb128 = S("p0_mb128", [128, 8])
    rvb128 = S("p0_rvb128", [128, 8])
    cnt_row = S("p0_cntrow", [1, 8])
    dbgt = S("p0_dbgt", [128, 96])
    offs_r = S("p0_offsr", [1, 8])
    zero8 = S("p0_zero8", [1, 8])

    def redb(n):
        return red[:].rearrange("p (s o) -> p s o", o=1).broadcast_to([128, 4, n])

    psA = [P("p0_psA0", [9, 512]), P("p0_psA1", [9, 512])]
    psB = [P("p0_psB0", [1, 512]), P("p0_psB1", [1, 512])]
    psT = P("p0_psT", [128, 4, 8])
    psb = P("p0_psb", [9, 512])
    psb2 = P("p0_psb2", [8, 512])

    def q_setup(q):
        em("q", q.dma_start(out=lhsA[:], in_=io["lhsA"]), dma=True)
        em("q", q.dma_start(out=identf[:], in_=io["identf"]), dma=True)
        em("q", q.dma_start(out=bias_b[0:1, :], in_=io["bias_row"]), dma=True)
    block.sync(q_setup)

    def g_setup(g):
        em("g", g.iota(iota8[:], pattern=[[1, 8]], base=0, channel_multiplier=0,
                       allow_small_or_imprecise_dtypes=True))
        em("g", g.memset(cnt[:], 0.0))
        em("g", g.memset(zero8[:], 0.0))
        em("g", g.memset(ones_r[:], 1.0))
    block.gpsimd(g_setup)

    ones_c = S("p0_onesc", [128, 1])

    def g_setup2(g):
        em("g", g.memset(ones_c[:], 1.0))
    block.gpsimd(g_setup2)

    def t_setup2(t):
        ec.fence(t, ["q", "g"])
        for ci in range(8):
            em("t", t.matmul(psb[0:9, 0:1], lhsT=lhsA[:, 9 * ci:9 * ci + 9],
                             rhs=ones_c[:], start=(ci == 0), stop=(ci == 7)))
        em("t", t.matmul(psT[:, 0, :], lhsT=ones_r[:], rhs=bias_b[0:1, :],
                         start=True, stop=True))
    block.tensor(t_setup2)

    def v_setup(v):
        ec.fence(v, ["t"])
        em("v", v.tensor_copy(out=wgs[:], in_=psb[0:9, 0:1]))
        em("v", v.tensor_copy(out=bias_b[:], in_=psT[:, 0, :]))
    block.vector(v_setup)

    for tc in range(64):
        b = tc % 2
        xa_t, sq_t, pA, pB = xa[b], sq[b], psA[b], psB[b]

        def q_load(q, tc=tc, xa_t=xa_t):
            ec.fence(q, ["t", "s"] + (["v"] if tc == 2 else []))
            em("q", q.dma_start(
                out=xa_t[:],
                in_=io["xaT"][:, tc * 512:(tc + 1) * 512].rearrange(
                    "(c p) t -> p c t", p=128)), dma=True)
        block.sync(q_load)

        def s_sq(s, xa_t=xa_t, sq_t=sq_t):
            ec.fence(s, ["q", "t"])
            em("s", s.activation(out=sq_t[:], in_=xa_t[:], func=AF.Square))
        block.scalar(s_sq)

        def t_mm(t, xa_t=xa_t, sq_t=sq_t, pA=pA, pB=pB):
            ec.fence(t, ["q", "s", "v"])
            for ci in range(8):
                em("t", t.matmul(pA[:], lhsT=lhsA[:, 9 * ci:9 * ci + 9],
                                 rhs=xa_t[:, ci, :], start=(ci == 0), stop=(ci == 7)))
            for ci in range(8):
                em("t", t.matmul(pB[:], lhsT=ones_c[:], rhs=sq_t[:, ci, :],
                                 start=(ci == 0), stop=(ci == 7)))
        block.tensor(t_mm)

        def v_stat(v, pA=pA, pB=pB):
            ec.fence(v, ["t", "g"])
            em("v", v.tensor_copy(out=t9[:], in_=pA[:]))
        block.vector(v_stat)

        def g_mrow(g):
            ec.fence(g, ["v"])
            if "mrow" in P0_SKIP:
                em("g", g.memset(m_row[:], 0.001))
            else:
                em("g", g.dma_start(out=io["mrow_d"], in_=t9[8:9, :]), dma=True)
                ec.fence(g, ["g"])
                em("g", g.dma_start(out=m_row[:], in_=io["mrow_d"]), dma=True)
        block.gpsimd(g_mrow)

        def v_stat2(v, pB=pB):
            ec.fence(v, ["g"])
            em("v", v.tensor_scalar_mul(out=m_row[:], in0=m_row[:], scalar1=1.0 / D))
            ec.selfsync("v", v)
            em("v", v.tensor_mul(out=v_row[:], in0=m_row[:], in1=m_row[:]))
            ec.selfsync("v", v)
            em("v", v.scalar_tensor_tensor(out=v_row[:], in0=pB[:], scalar=1.0 / D,
                                           in1=v_row[:], op0=ALU.mult, op1=ALU.subtract))
        block.vector(v_stat2)

        def v_eps(v):
            em("v", v.tensor_scalar_add(out=v_row[:], in0=v_row[:], scalar1=1e-5))
        block.vector(v_eps)

        def s_rstd(s):
            ec.fence(s, ["v"])
            em("s", s.activation(out=rv_row[:], in_=v_row[:], func=AF.Sqrt))
        block.scalar(s_rstd)

        def v_rv(v):
            ec.fence(v, ["s"])
            em("v", v.reciprocal(out=rv_row[:], in_=rv_row[:]))
        block.vector(v_rv)

        def t_b8(t):
            ec.fence(t, ["v"])
            em("t", t.matmul(psb[0:8, :], lhsT=ones_r[0:1, 0:8], rhs=m_row[:],
                             start=True, stop=True))
            em("t", t.matmul(psb2[:], lhsT=ones_r[0:1, 0:8], rhs=rv_row[:],
                             start=True, stop=True))
        block.tensor(t_b8)

        def v_lg(v):
            ec.fence(v, ["t"])
            em("v", v.tensor_copy(out=mb8[:], in_=psb[0:8, :]))
            ec.selfsync("v", v)
            em("v", v.tensor_scalar_mul(out=mb8[:], in0=mb8[:], scalar1=wgs[0:8, :]))
            ec.selfsync("v", v)
            em("v", v.tensor_sub(out=t9[0:8, :], in0=t9[0:8, :], in1=mb8[:]))
            ec.selfsync("v", v)
            em("v", v.tensor_mul(out=t9[0:8, :], in0=t9[0:8, :], in1=psb2[:]))
        block.vector(v_lg)

        if tc == P0_DBG_TC:
            def g_dbg(g):
                ec.fence(g, ["v"])
                em("g", g.dma_start(out=io["dbg2"], in_=t9[:]), dma=True)
            block.gpsimd(g_dbg)

        if tc == 0:
            def t_x8(t):
                ec.fence(t, ["v"])
                em("t", t.matmul(psT[:, 0, :], lhsT=ones_r[:], rhs=m_row[0:1, 0:8],
                                 start=True, stop=True))
            block.tensor(t_x8)

            def v_x8(v):
                ec.fence(v, ["t"])
                em("v", v.tensor_copy(out=mb128[:], in_=psT[:, 0, :]))
            block.vector(v_x8)

            def t_x8b(t):
                ec.fence(t, ["v"])
                em("t", t.matmul(psT[:, 0, :], lhsT=ones_r[:], rhs=rv_row[0:1, 0:8],
                                 start=True, stop=True))
            block.tensor(t_x8b)

            def v_x8b(v, xa_t=xa_t):
                ec.fence(v, ["t"])
                em("v", v.tensor_copy(out=rvb128[:], in_=psT[:, 0, :]))
                ec.selfsync("v", v)
                for ci in range(8):
                    em("v", v.tensor_sub(out=xn8[:, ci, :], in0=xa_t[:, ci, 0:8],
                                         in1=mb128[:]))
                ec.selfsync("v", v)
                for ci in range(8):
                    em("v", v.tensor_mul(out=xn8[:, ci, :], in0=xn8[:, ci, :],
                                         in1=rvb128[:]))
            block.vector(v_x8b)

            def g_x8(g):
                ec.fence(g, ["v"])
                if "xn8" not in P0_SKIP:
                    em("g", g.dma_start(out=io["xn8T_d"].rearrange("(c p) e -> p c e", p=128),
                                        in_=xn8[:]), dma=True)
                else:
                    em("g", g.memset(xn8[:, 0, :], 0.0))
            block.gpsimd(g_x8)

        def t_tr(t):
            ec.fence(t, ["v"])
            for scn in range(4):
                em("t", t.transpose(psT[:, scn, :], in_=t9[0:8, scn * 128:(scn + 1) * 128],
                                    identity=identf[0:8, 0:8]))
        block.tensor(t_tr)

        def v_route(v, tc=tc):
            ec.fence(v, ["t"])
            em("v", v.tensor_copy(out=lt[:], in_=psT[:]))
            ec.selfsync("v", v)
            em("v", v.reduce_max(out=red[:], in_=lt[:], axis=AX.X))
            ec.selfsync("v", v)
            em("v", v.tensor_sub(out=sp[:], in0=lt[:], in1=redb(8)))
        block.vector(v_route)

        def s_exp(s):
            ec.fence(s, ["v"])
            em("s", s.activation(out=sp[:], in_=sp[:], func=AF.Exp))
        block.scalar(s_exp)

        def v_route2(v, tc=tc):
            ec.fence(v, ["s"])
            def Y():
                ec.selfsync("v", v)
            em("v", v.reduce_sum(out=red[:], in_=sp[:], axis=AX.X)); Y()
            em("v", v.reciprocal(out=red[:], in_=red[:])); Y()
            em("v", v.tensor_mul(out=scores[:], in0=sp[:], in1=redb(8))); Y()
            em("v", v.tensor_add(out=sp[:], in0=scores[:],
                                 in1=bias_b[:].rearrange("p (o e) -> p o e", o=1).broadcast_to([128, 4, 8]))); Y()
            em("v", v.scalar_tensor_tensor(out=sp[:], in0=iota8[:].rearrange("p (o e) -> p o e", o=1).broadcast_to([128, 4, 8]),
                                           scalar=-TIE, in1=sp[:], op0=ALU.mult, op1=ALU.add)); Y()
            spv = sp[:].rearrange("p s (g two) -> p s g two", two=2)
            em("v", v.tensor_add(out=gs[:], in0=spv[:, :, :, 0], in1=spv[:, :, :, 1])); Y()
            em("v", v.reduce_max(out=red[:], in_=gs[:], axis=AX.X)); Y()
            em("v", v.tensor_tensor(out=gs2[:], in0=gs[:], in1=redb(4), op=ALU.is_ge)); Y()
            em("v", v.scalar_tensor_tensor(out=gs[:], in0=gs2[:], scalar=-1e4,
                                           in1=gs[:], op0=ALU.mult, op1=ALU.add)); Y()
            em("v", v.reduce_max(out=red[:], in_=gs[:], axis=AX.X)); Y()
            em("v", v.tensor_tensor(out=gs[:], in0=gs[:], in1=redb(4), op=ALU.is_ge)); Y()
            em("v", v.tensor_add(out=gs[:], in0=gs[:], in1=gs2[:])); Y()
            kv = keep[:].rearrange("p s (g two) -> p s g two", two=2)
            em("v", v.tensor_copy(out=kv[:, :, :, 0], in_=gs[:]))
            em("v", v.tensor_copy(out=kv[:, :, :, 1], in_=gs[:])); Y()
            em("v", v.tensor_scalar_sub(out=tmp8[:], in0=keep[:], scalar1=1.0)); Y()
            em("v", v.scalar_tensor_tensor(out=tmp8[:], in0=tmp8[:], scalar=1e4,
                                           in1=sp[:], op0=ALU.mult, op1=ALU.add)); Y()
            em("v", v.reduce_max(out=red[:], in_=tmp8[:], axis=AX.X)); Y()
            em("v", v.tensor_tensor(out=k1[:], in0=tmp8[:], in1=redb(8), op=ALU.is_ge)); Y()
            em("v", v.scalar_tensor_tensor(out=tmp8[:], in0=k1[:], scalar=-2e4,
                                           in1=tmp8[:], op0=ALU.mult, op1=ALU.add)); Y()
            em("v", v.reduce_max(out=red[:], in_=tmp8[:], axis=AX.X)); Y()
            em("v", v.tensor_tensor(out=k2[:], in0=tmp8[:], in1=redb(8), op=ALU.is_ge)); Y()
            iob = iota8[:].rearrange("p (o e) -> p o e", o=1).broadcast_to([128, 4, 8])
            em("v", v.tensor_mul(out=tmp8[:], in0=k1[:], in1=iob)); Y()
            em("v", v.reduce_sum(out=jstage[:, 0 * 256 + tc * 4: 0 * 256 + tc * 4 + 4],
                                 in_=tmp8[:], axis=AX.X)); Y()
            em("v", v.tensor_mul(out=tmp8[:], in0=k2[:], in1=iob)); Y()
            em("v", v.reduce_sum(out=jstage[:, 256 + tc * 4: 256 + tc * 4 + 4],
                                 in_=tmp8[:], axis=AX.X)); Y()
            em("v", v.tensor_mul(out=tmp8[:], in0=k1[:], in1=scores[:])); Y()
            em("v", v.reduce_sum(out=wstage[:, 0 * 256 + tc * 4: 0 * 256 + tc * 4 + 4],
                                 in_=tmp8[:], axis=AX.X)); Y()
            em("v", v.tensor_mul(out=tmp8[:], in0=k2[:], in1=scores[:])); Y()
            em("v", v.reduce_sum(out=wstage[:, 256 + tc * 4: 256 + tc * 4 + 4],
                                 in_=tmp8[:], axis=AX.X)); Y()
            em("v", v.tensor_add(out=tmp8[:], in0=k1[:], in1=k2[:])); Y()
            for scn in range(4):
                em("v", v.tensor_add(out=cnt[:], in0=cnt[:], in1=tmp8[:, scn, :])); Y()
        block.vector(v_route2)

    def t_cnt(t):
        ec.fence(t, ["v"])
        em("t", t.matmul(psb[0:1, 0:8], lhsT=ones_c[:], rhs=cnt[:],
                         start=True, stop=True))
    block.tensor(t_cnt)

    def v_offs(v):
        ec.fence(v, ["t"])
        em("v", v.tensor_copy(out=cnt_row[:], in_=psb[0:1, 0:8]))
        ec.selfsync("v", v)
        em("v", v.tensor_tensor_scan(out=offs_r[:], data0=cnt_row[:], data1=zero8[:],
                                     initial=0.0, op0=ALU.add, op1=ALU.add))
        ec.selfsync("v", v)
    block.vector(v_offs)

    def g_fin(g):
        ec.fence(g, ["v"])
        if P0_DBG_TC >= 0:
            em("g", g.dma_start(out=io["dbg"], in_=dbgt[:]), dma=True)
        em("g", g.dma_start(out=io["offs_d"], in_=offs_r[:]), dma=True)
        em("g", g.dma_start(out=io["jslot"], in_=jstage[:]), dma=True)
        em("g", g.dma_start(out=io["wslot"], in_=wstage[:]), dma=True)
    block.gpsimd(g_fin)



def build_p1a(nc, ec, block, io, st):
    S = lambda name, shape, dt=F32: st.enter_context(nc.sbuf_tensor(name, shape, dt))
    P = lambda name, shape: st.enter_context(nc.psum_tensor(name, shape, F32))
    em = ec.em
    NG = 8

    xg = S("a_xg", [128, 8, 512])
    BS = S("a_bs", [128, 8, 512])
    acc = S("a_acc", [128, 8, 512])
    xnb = S("a_xnb", [128, 8, 512], BF16)
    h = S("a_h", [128, 8, 515], BF16)
    xc = S("a_xc", [128, 8, 512], BF16)
    qkb = S("a_qkb", [128, 8, 512], BF16)
    bsq = S("a_bsq", [128, 8, 512], BF16)
    kb = S("a_kb", [128, 8, 513], BF16)
    vb = S("a_vb", [128, 8, 512], BF16)
    gb = S("a_gb", [128, 8, 512], BF16)
    outm = S("a_outm", [128, 8, 512], BF16)
    aob = vb
    h2b = gb
    A = S("a_A", [128, 512]); B = S("a_B", [128, 512]); C = S("a_C", [128, 512])
    Dd = S("a_D", [128, 512]); E = S("a_E", [128, 512]); F = S("a_F", [128, 512])
    G = S("a_G", [128, 512])
    zeros512 = S("a_z512", [128, 512])
    tio = S("a_tio", [128, 512])
    wld = [S("a_wld0", [128, 8, 512]), S("a_wld1", [128, 8, 512])]
    wbf = [S("a_wbf0", [128, 8, 512], BF16), S("a_wbf1", [128, 8, 512], BF16)]
    mb = S("a_mb", [128, 512]); rvb = S("a_rvb", [128, 512])
    m_row = S("a_mrow", [1, 512]); v_row = S("a_vrow", [1, 512]); rv_row = S("a_rvrow", [1, 512])
    ones_c = S("a_onesc", [128, 1]); ones_bf = S("a_onesbf", [128, 1], BF16)
    ones_r = S("a_onesr", [1, 128])
    ns = S("a_ns", [16, 512]); rnb16 = S("a_rnb", [16, 512], BF16)
    carry_h = S("a_carryh", [128, 8, 3], BF16)
    carry_k = S("a_carryk", [128, 8, 1], BF16)
    carry_s = S("a_carrys", [128, 8])
    par = {}
    for nm in ("ln1_w", "ln1_b", "conv_b", "g_b", "onorm_w", "onorm_b",
               "ln2_w", "ln2_b", "gamma1", "gammaA", "freqs"):
        par[nm] = S("a_" + nm, [128, 8])
    conv_w = S("a_convw", [128, 8, 4])
    hsel = S("a_hsel", [128, 8, 16], BF16)
    hexp = S("a_hexp", [16, 8, 128], BF16)
    hself = S("a_hself", [128, 8, 16])
    hexpf = S("a_hexpf", [16, 8, 128])
    lng = S("a_lng", [128, 8])
    g1w = S("a_g1w", [128, 8]); g1b = S("a_g1b", [128, 8])
    t8a = S("a_t8a", [128, 8]); t8b = S("a_t8b", [128, 8]); t8c = S("a_t8c", [128, 8])

    pms = [P("a_pm0", [1, 512]), P("a_pm1", [1, 512])]
    pbc = [P("a_pbc0", [128, 512]), P("a_pbc1", [128, 512])]
    pmm = [P("a_pmm0", [128, 512]), P("a_pmm1", [128, 512])]
    pn16 = P("a_pn16", [16, 512])

    def Y(v):
        ec.selfsync("v", v)

    def q_setup(q):
        for nm in par:
            em("q", q.dma_start(out=par[nm][:], in_=io[nm]), dma=True)
        em("q", q.dma_start(out=conv_w[:], in_=io["conv_w"]), dma=True)
        em("q", q.dma_start(out=hself[:], in_=io["headsel"]), dma=True)
        em("q", q.dma_start(out=hexpf[:], in_=io["headexp"]), dma=True)
    block.sync(q_setup)

    def g_setup(g):
        em("g", g.memset(ones_c[:], 1.0))
        em("g", g.memset(ones_bf[:], 1.0))
        em("g", g.memset(ones_r[:], 1.0))
        em("g", g.memset(zeros512[:], 0.0))
        em("g", g.memset(carry_h[:], 0.0))
        em("g", g.memset(carry_k[:], 0.0))
        em("g", g.memset(carry_s[:], 0.0))
        ec.fence(g, ["q"])
        em("g", g.tensor_copy(out=hsel[:], in_=hself[:]))
        em("g", g.tensor_copy(out=hexp[:], in_=hexpf[:]))
    block.gpsimd(g_setup)

    def v_lng(v):
        ec.fence(v, ["q", "g"])
        em("v", v.tensor_scalar_sub(out=t8a[:], in0=par["gammaA"][:], scalar1=1.0)); Y(v)
        em("v", v.tensor_scalar_add(out=t8b[:], in0=par["gammaA"][:], scalar1=1.0)); Y(v)
        em("v", v.reciprocal(out=t8b[:], in_=t8b[:])); Y(v)
        em("v", v.tensor_mul(out=t8b[:], in0=t8a[:], in1=t8b[:])); Y(v)
        em("v", v.tensor_mul(out=t8c[:], in0=t8b[:], in1=t8b[:])); Y(v)
        em("v", v.tensor_scalar(out=lng[:], in0=t8c[:], scalar1=1.0 / 9.0, scalar2=1.0 / 7.0,
                                op0=ALU.mult, op1=ALU.add)); Y(v)
        em("v", v.scalar_tensor_tensor(out=lng[:], in0=lng[:], scalar=1.0, in1=t8c[:],
                                       op0=ALU.mult, op1=ALU.mult)); Y(v)
        em("v", v.tensor_scalar_add(out=lng[:], in0=lng[:], scalar1=1.0 / 5.0)); Y(v)
        em("v", v.tensor_mul(out=lng[:], in0=lng[:], in1=t8c[:])); Y(v)
        em("v", v.tensor_scalar_add(out=lng[:], in0=lng[:], scalar1=1.0 / 3.0)); Y(v)
        em("v", v.tensor_mul(out=lng[:], in0=lng[:], in1=t8c[:])); Y(v)
        em("v", v.tensor_scalar_add(out=lng[:], in0=lng[:], scalar1=1.0)); Y(v)
        em("v", v.tensor_mul(out=lng[:], in0=lng[:], in1=t8b[:])); Y(v)
        em("v", v.tensor_scalar_mul(out=lng[:], in0=lng[:], scalar1=2.0)); Y(v)
    block.vector(v_lng)

    def s_lng(s):
        ec.fence(s, ["v"])
        em("s", s.activation(out=t8b[:], in_=par["gammaA"][:], func=AF.Ln))
    block.scalar(s_lng)

    def v_lng2(v):
        ec.fence(v, ["s"])
        em("v", v.tensor_tensor(out=t8a[:], in0=t8a[:], in1=t8a[:], op=ALU.mult)); Y(v)
        em("v", v.tensor_scalar(out=t8a[:], in0=t8a[:], scalar1=0.16, scalar2=None,
                                op0=ALU.is_gt)); Y(v)
        em("v", v.tensor_mul(out=t8b[:], in0=t8b[:], in1=t8a[:])); Y(v)
        em("v", v.tensor_scalar(out=t8a[:], in0=t8a[:], scalar1=-1.0, scalar2=1.0,
                                op0=ALU.mult, op1=ALU.add)); Y(v)
        em("v", v.tensor_mul(out=lng[:], in0=lng[:], in1=t8a[:])); Y(v)
        em("v", v.tensor_add(out=lng[:], in0=lng[:], in1=t8b[:])); Y(v)
        em("v", v.tensor_mul(out=g1w[:], in0=par["gamma1"][:], in1=par["onorm_w"][:]))
        em("v", v.tensor_mul(out=g1b[:], in0=par["gamma1"][:], in1=par["onorm_b"][:])); Y(v)
    block.vector(v_lng2)

    for g_i in range(NG):
        t0 = g_i * 512

        def q_xg(q, t0=t0):
            ec.fence(q, ["v", "s"])
            em("q", q.dma_start(out=xg[:], in_=io["xT"][:, t0:t0 + 512].rearrange(
                "(c p) t -> p c t", p=128)), dma=True)
        block.sync(q_xg)

        def s_sq(s):
            ec.fence(s, ["q", "t"])
            em("s", s.activation(out=BS[:], in_=xg[:], func=AF.Square))
        block.scalar(s_sq)

        def t_stat(t):
            ec.fence(t, ["s", "v"])
            for ci in range(8):
                em("t", t.matmul(pms[0][:], lhsT=ones_c[:], rhs=xg[:, ci, :],
                                 start=(ci == 0), stop=(ci == 7)))
            for ci in range(8):
                em("t", t.matmul(pms[1][:], lhsT=ones_c[:], rhs=BS[:, ci, :],
                                 start=(ci == 0), stop=(ci == 7)))
        block.tensor(t_stat)

        def v_stat(v):
            ec.fence(v, ["t"])
            em("v", v.tensor_scalar_mul(out=m_row[:], in0=pms[0][:], scalar1=1.0 / D)); Y(v)
            em("v", v.tensor_mul(out=v_row[:], in0=m_row[:], in1=m_row[:])); Y(v)
            em("v", v.scalar_tensor_tensor(out=v_row[:], in0=pms[1][:], scalar=1.0 / D,
                                           in1=v_row[:], op0=ALU.mult, op1=ALU.subtract)); Y(v)
            em("v", v.tensor_scalar_add(out=v_row[:], in0=v_row[:], scalar1=1e-5))
        block.vector(v_stat)

        def s_rstd(s):
            ec.fence(s, ["v"])
            em("s", s.activation(out=rv_row[:], in_=v_row[:], func=AF.Sqrt))
        block.scalar(s_rstd)

        def v_rv(v):
            ec.fence(v, ["s"])
            em("v", v.reciprocal(out=rv_row[:], in_=rv_row[:]))
        block.vector(v_rv)

        def t_bc(t):
            ec.fence(t, ["v"])
            em("t", t.matmul(pbc[0][:], lhsT=ones_r[:], rhs=m_row[:], start=True, stop=True))
            em("t", t.matmul(pbc[1][:], lhsT=ones_r[:], rhs=rv_row[:], start=True, stop=True))
        block.tensor(t_bc)

        def v_xn(v):
            ec.fence(v, ["t"])
            em("v", v.tensor_copy(out=mb[:], in_=pbc[0][:]))
            em("v", v.tensor_copy(out=rvb[:], in_=pbc[1][:])); Y(v)
            for ci in range(8):
                em("v", v.tensor_sub(out=BS[:, ci, :], in0=xg[:, ci, :], in1=mb[:]))
            Y(v)
            for ci in range(8):
                em("v", v.tensor_mul(out=BS[:, ci, :], in0=BS[:, ci, :], in1=rvb[:]))
            Y(v)
            for ci in range(8):
                em("v", v.tensor_copy(out=xnb[:, ci, :], in_=BS[:, ci, :]))
            for ci in range(8):
                em("v", v.tensor_add(out=acc[:, ci, :], in0=xg[:, ci, :], in1=BS[:, ci, :]))
            Y(v)
        block.vector(v_xn)

        def g_xnb(g, t0=t0):
            ec.fence(g, ["v"])
            em("g", g.dma_start(out=io["xnb_d"][:, t0:t0 + 512].rearrange(
                "(c p) t -> p c t", p=128), in_=xnb[:]), dma=True)
        block.gpsimd(g_xnb)

        def ln_pass(src_bf, dst, wname, bname, halo, extra_scale=None, extra_bias=None,
                    dst_off=0):
            def s_sq2(s):
                ec.fence(s, ["v", "t"])
                em("s", s.activation(out=bsq[:], in_=src_bf[:], func=AF.Square))
            block.scalar(s_sq2)

            def t_st(t):
                ec.fence(t, ["s", "v"])
                for ci in range(8):
                    em("t", t.matmul(pms[0][:], lhsT=ones_bf[:], rhs=src_bf[:, ci, :],
                                     start=(ci == 0), stop=(ci == 7)))
                for ci in range(8):
                    em("t", t.matmul(pms[1][:], lhsT=ones_bf[:], rhs=bsq[:, ci, :],
                                     start=(ci == 0), stop=(ci == 7)))
            block.tensor(t_st)

            def v_st(v):
                ec.fence(v, ["t"])
                em("v", v.tensor_scalar_mul(out=m_row[:], in0=pms[0][:], scalar1=1.0 / D)); Y(v)
                em("v", v.tensor_mul(out=v_row[:], in0=m_row[:], in1=m_row[:])); Y(v)
                em("v", v.scalar_tensor_tensor(out=v_row[:], in0=pms[1][:], scalar=1.0 / D,
                                               in1=v_row[:], op0=ALU.mult, op1=ALU.subtract)); Y(v)
                em("v", v.tensor_scalar_add(out=v_row[:], in0=v_row[:], scalar1=1e-5))
            block.vector(v_st)

            def s_r(s):
                ec.fence(s, ["v"])
                em("s", s.activation(out=rv_row[:], in_=v_row[:], func=AF.Sqrt))
            block.scalar(s_r)

            def v_r(v):
                ec.fence(v, ["s"])
                em("v", v.reciprocal(out=rv_row[:], in_=rv_row[:]))
            block.vector(v_r)

            def t_b(t):
                ec.fence(t, ["v"])
                em("t", t.matmul(pbc[0][:], lhsT=ones_r[:], rhs=m_row[:], start=True, stop=True))
                em("t", t.matmul(pbc[1][:], lhsT=ones_r[:], rhs=rv_row[:], start=True, stop=True))
            block.tensor(t_b)

            def v_a(v):
                ec.fence(v, ["t"])
                em("v", v.tensor_copy(out=mb[:], in_=pbc[0][:]))
                em("v", v.tensor_copy(out=rvb[:], in_=pbc[1][:])); Y(v)
                for ci in range(8):
                    em("v", v.tensor_sub(out=BS[:, ci, :], in0=src_bf[:, ci, :], in1=mb[:]))
                Y(v)
                for ci in range(8):
                    em("v", v.tensor_mul(out=BS[:, ci, :], in0=BS[:, ci, :], in1=rvb[:]))
                Y(v)
                wA = par[wname] if extra_scale is None else extra_scale
                bA = par[bname] if extra_bias is None else extra_bias
                for ci in range(8):
                    em("v", v.tensor_scalar(out=dst[:, ci, dst_off:dst_off + 512],
                                            in0=BS[:, ci, :],
                                            scalar1=wA[:, ci:ci + 1], scalar2=bA[:, ci:ci + 1],
                                            op0=ALU.mult, op1=ALU.add))
                Y(v)
            block.vector(v_a)

        ln_pass(xnb, h, "ln1_w", "ln1_b", halo=True, dst_off=3)

        def v_conv(v, g_i=g_i):
            for ci in range(8):
                em("v", v.tensor_copy(out=h[:, ci, 0:3], in_=carry_h[:, ci, :]))
            Y(v)
            for ci in range(8):
                em("v", v.tensor_copy(out=carry_h[:, ci, :], in_=h[:, ci, 512:515]))
            Y(v)
            for ci in range(8):
                em("v", v.tensor_scalar_mul(out=BS[:, ci, :], in0=h[:, ci, 0:512],
                                            scalar1=conv_w[:, ci, 0:1]))
            Y(v)
            for i in range(1, 4):
                for ci in range(8):
                    em("v", v.scalar_tensor_tensor(out=BS[:, ci, :], in0=h[:, ci, i:i + 512],
                                                   scalar=conv_w[:, ci, i:i + 1],
                                                   in1=BS[:, ci, :], op0=ALU.mult, op1=ALU.add))
                Y(v)
            for ci in range(8):
                em("v", v.tensor_scalar(out=xc[:, ci, :], in0=BS[:, ci, :],
                                        scalar1=par["conv_b"][:, ci:ci + 1], scalar2=None,
                                        op0=ALU.add))
            Y(v)
        block.vector(v_conv)

        def proj(wkey, out_tile, kind, g_i=g_i):
            for ob in range(2):
                wb = wld[ob % 2]; wf = wbf[ob % 2]

                def q_w(q, wkey=wkey, ob=ob, wb=wb):
                    ec.fence(q, ["g"])
                    em("q", q.dma_start(out=wb[:], in_=io[wkey][:, ob * 512:(ob + 1) * 512].rearrange(
                        "(c p) o -> p c o", p=128)), dma=True)
                block.sync(q_w)

                def g_cast(g, wb=wb, wf=wf):
                    ec.fence(g, ["q", "t"])
                    em("g", g.tensor_copy(out=wf[:], in_=wb[:]))
                block.gpsimd(g_cast)

                for oc in range(4):
                    pp = pmm[oc % 2]

                    def t_p(t, wf=wf, oc=oc, pp=pp):
                        ec.fence(t, ["g", "v", "s"])
                        for ci in range(8):
                            em("t", t.matmul(pp[:], lhsT=wf[:, ci, oc * 128:(oc + 1) * 128],
                                             rhs=xc[:, ci, :], start=(ci == 0), stop=(ci == 7)))
                    block.tensor(t_p)

                    occ = ob * 4 + oc
                    if kind == "qk":
                        def s_e(s, pp=pp, occ=occ):
                            ec.fence(s, ["t"])
                            em("s", s.activation(out=bsq[:, occ, :], in_=pp[:], func=AF.Square))
                        block.scalar(s_e)

                        def v_e(v, pp=pp, occ=occ):
                            ec.fence(v, ["s"])
                            em("v", v.tensor_copy(out=qkb[:, occ, :], in_=pp[:]))
                        block.vector(v_e)
                    elif kind == "v":
                        def v_e(v, pp=pp, occ=occ):
                            ec.fence(v, ["t"])
                            em("v", v.tensor_copy(out=vb[:, occ, :], in_=pp[:]))
                        block.vector(v_e)
                    else:
                        def s_e(s, pp=pp, occ=occ):
                            ec.fence(s, ["t"])
                            em("s", s.activation(out=gb[:, occ, :], in_=pp[:], func=AF.Sigmoid,
                                                 bias=par["g_b"][:, occ:occ + 1]))
                        block.scalar(s_e)

        proj("qk_wT", qkb, "qk")
        proj("v_wT", vb, "v")
        proj("g_wT", gb, "g")

        def t_nrm(t):
            ec.fence(t, ["s", "v"])
            for ci in range(8):
                em("t", t.matmul(pn16[:], lhsT=hsel[:, ci, :], rhs=bsq[:, ci, :],
                                 start=(ci == 0), stop=(ci == 7)))
        block.tensor(t_nrm)

        def s_nrm(s):
            ec.fence(s, ["t"])
            em("s", s.activation(out=ns[:], in_=pn16[:], func=AF.Sqrt))
        block.scalar(s_nrm)

        def v_nrm(v):
            ec.fence(v, ["s"])
            em("v", v.tensor_scalar_max(out=ns[:], in0=ns[:], scalar1=1e-12)); Y(v)
            em("v", v.reciprocal(out=ns[:], in_=ns[:])); Y(v)
            em("v", v.tensor_copy(out=rnb16[:], in_=ns[:])); Y(v)
        block.vector(v_nrm)

        def v_kcar(v):
            for ci in range(8):
                em("v", v.tensor_copy(out=kb[:, ci, 0:1], in_=carry_k[:, ci, :]))
            Y(v)
        block.vector(v_kcar)

        for ci in range(8):
            pp = pmm[ci % 2]

            def t_kb(t, ci=ci, pp=pp):
                ec.fence(t, ["v"])
                em("t", t.matmul(pp[:], lhsT=hexp[:, ci, :], rhs=rnb16[:], start=True, stop=True))
            block.tensor(t_kb)

            def v_kb(v, ci=ci, pp=pp):
                ec.fence(v, ["t"])
                em("v", v.tensor_mul(out=kb[:, ci, 1:513], in0=qkb[:, ci, :], in1=pp[:]))
            block.vector(v_kb)

        def v_kcar2(v):
            Y(v)
            for ci in range(8):
                em("v", v.tensor_copy(out=carry_k[:, ci, :], in_=kb[:, ci, 512:513]))
            Y(v)
        block.vector(v_kcar2)

        def g_tio(g, t0=t0):
            ec.fence(g, ["v"])
            em("g", g.iota(tio[:], pattern=[[1, 512]], base=t0, channel_multiplier=0,
                           allow_small_or_imprecise_dtypes=True))
        block.gpsimd(g_tio)

        for ci in range(8):
            def v_state(v, ci=ci, g_i=g_i):
                ec.fence(v, ["g", "s"])
                em("v", v.tensor_scalar_mul(out=A[:], in0=tio[:], scalar1=par["freqs"][:, ci:ci + 1])); Y(v)
                em("v", v.tensor_scalar(out=B[:], in0=A[:], scalar1=INV_2PI, scalar2=MAGIC,
                                        op0=ALU.mult, op1=ALU.add)); Y(v)
                em("v", v.tensor_scalar_sub(out=B[:], in0=B[:], scalar1=MAGIC)); Y(v)
                em("v", v.scalar_tensor_tensor(out=A[:], in0=B[:], scalar=-CW1, in1=A[:],
                                               op0=ALU.mult, op1=ALU.add)); Y(v)
                em("v", v.scalar_tensor_tensor(out=A[:], in0=B[:], scalar=-CW2, in1=A[:],
                                               op0=ALU.mult, op1=ALU.add)); Y(v)
                em("v", v.scalar_tensor_tensor(out=A[:], in0=B[:], scalar=-CW3, in1=A[:],
                                               op0=ALU.mult, op1=ALU.add)); Y(v)
                em("v", v.tensor_scalar_add(out=A[:], in0=A[:], scalar1=HALF_PI)); Y(v)
                em("v", v.tensor_scalar(out=B[:], in0=A[:], scalar1=PI_F, scalar2=-TWO_PI,
                                        op0=ALU.is_gt, op1=ALU.mult)); Y(v)
                em("v", v.tensor_add(out=A[:], in0=A[:], in1=B[:])); Y(v)
                em("v", v.tensor_scalar_mul(out=Dd[:], in0=tio[:], scalar1=lng[:, ci:ci + 1])); Y(v)
            block.vector(v_state)

            def s_cd(s, ci=ci):
                ec.fence(s, ["v"])
                em("s", s.activation(out=C[:], in_=A[:], func=AF.Sin))
                em("s", s.activation(out=Dd[:], in_=Dd[:], func=AF.Exp))
            block.scalar(s_cd)

            def v_z(v, ci=ci):
                ec.fence(v, ["s"])
                em("v", v.tensor_mul(out=E[:], in0=vb[:, ci, :], in1=C[:])); Y(v)
                em("v", v.tensor_mul(out=F[:], in0=vb[:, ci, :], in1=kb[:, ci, 0:512])); Y(v)
                em("v", v.tensor_add(out=E[:], in0=E[:], in1=F[:])); Y(v)
                em("v", v.tensor_mul(out=E[:], in0=E[:], in1=gb[:, ci, :])); Y(v)
                em("v", v.tensor_mul(out=E[:], in0=E[:], in1=Dd[:])); Y(v)
                em("v", v.tensor_tensor_scan(out=G[:], data0=E[:], data1=zeros512[:],
                                             initial=carry_s[:, ci:ci + 1],
                                             op0=ALU.add, op1=ALU.add)); Y(v)
                em("v", v.tensor_copy(out=carry_s[:, ci:ci + 1], in_=G[:, 511:512])); Y(v)
                em("v", v.tensor_scalar_add(out=B[:], in0=Dd[:], scalar1=1e-6)); Y(v)
                em("v", v.reciprocal(out=B[:], in_=B[:])); Y(v)
                em("v", v.tensor_mul(out=G[:], in0=G[:], in1=B[:])); Y(v)
                em("v", v.tensor_mul(out=outm[:, ci, :], in0=G[:], in1=kb[:, ci, 1:513])); Y(v)
            block.vector(v_z)

        for ob in range(2):
            wb = wld[ob % 2]; wf = wbf[ob % 2]

            def q_w(q, ob=ob, wb=wb):
                ec.fence(q, ["g"])
                em("q", q.dma_start(out=wb[:], in_=io["o_wT"][:, ob * 512:(ob + 1) * 512].rearrange(
                    "(c p) o -> p c o", p=128)), dma=True)
            block.sync(q_w)

            def g_cast(g, wb=wb, wf=wf):
                ec.fence(g, ["q", "t"])
                em("g", g.tensor_copy(out=wf[:], in_=wb[:]))
            block.gpsimd(g_cast)

            for oc in range(4):
                pp = pmm[oc % 2]

                def t_p(t, wf=wf, oc=oc, pp=pp):
                    ec.fence(t, ["g", "v"])
                    for ci in range(8):
                        em("t", t.matmul(pp[:], lhsT=wf[:, ci, oc * 128:(oc + 1) * 128],
                                         rhs=outm[:, ci, :], start=(ci == 0), stop=(ci == 7)))
                block.tensor(t_p)

                occ = ob * 4 + oc

                def v_e(v, pp=pp, occ=occ):
                    ec.fence(v, ["t"])
                    em("v", v.tensor_copy(out=aob[:, occ, :], in_=pp[:]))
                block.vector(v_e)

        def s_sq3(s):
            ec.fence(s, ["v"])
            em("s", s.activation(out=bsq[:], in_=aob[:], func=AF.Square))
        block.scalar(s_sq3)

        def t_st3(t):
            ec.fence(t, ["s", "v"])
            for ci in range(8):
                em("t", t.matmul(pms[0][:], lhsT=ones_bf[:], rhs=aob[:, ci, :],
                                 start=(ci == 0), stop=(ci == 7)))
            for ci in range(8):
                em("t", t.matmul(pms[1][:], lhsT=ones_bf[:], rhs=bsq[:, ci, :],
                                 start=(ci == 0), stop=(ci == 7)))
        block.tensor(t_st3)

        def v_st3(v):
            ec.fence(v, ["t"])
            em("v", v.tensor_scalar_mul(out=m_row[:], in0=pms[0][:], scalar1=1.0 / D)); Y(v)
            em("v", v.tensor_mul(out=v_row[:], in0=m_row[:], in1=m_row[:])); Y(v)
            em("v", v.scalar_tensor_tensor(out=v_row[:], in0=pms[1][:], scalar=1.0 / D,
                                           in1=v_row[:], op0=ALU.mult, op1=ALU.subtract)); Y(v)
            em("v", v.tensor_scalar_add(out=v_row[:], in0=v_row[:], scalar1=1e-5))
        block.vector(v_st3)

        def s_r3(s):
            ec.fence(s, ["v"])
            em("s", s.activation(out=rv_row[:], in_=v_row[:], func=AF.Sqrt))
        block.scalar(s_r3)

        def v_r3(v):
            ec.fence(v, ["s"])
            em("v", v.reciprocal(out=rv_row[:], in_=rv_row[:]))
        block.vector(v_r3)

        def t_b3(t):
            ec.fence(t, ["v"])
            em("t", t.matmul(pbc[0][:], lhsT=ones_r[:], rhs=m_row[:], start=True, stop=True))
            em("t", t.matmul(pbc[1][:], lhsT=ones_r[:], rhs=rv_row[:], start=True, stop=True))
        block.tensor(t_b3)

        def v_a3(v):
            ec.fence(v, ["t"])
            em("v", v.tensor_copy(out=mb[:], in_=pbc[0][:]))
            em("v", v.tensor_copy(out=rvb[:], in_=pbc[1][:])); Y(v)
            for ci in range(8):
                em("v", v.tensor_sub(out=BS[:, ci, :], in0=aob[:, ci, :], in1=mb[:]))
            Y(v)
            for ci in range(8):
                em("v", v.tensor_mul(out=BS[:, ci, :], in0=BS[:, ci, :], in1=rvb[:]))
            Y(v)
            for ci in range(8):
                em("v", v.tensor_scalar(out=BS[:, ci, :], in0=BS[:, ci, :],
                                        scalar1=g1w[:, ci:ci + 1], scalar2=g1b[:, ci:ci + 1],
                                        op0=ALU.mult, op1=ALU.add))
            Y(v)
            for ci in range(8):
                em("v", v.tensor_add(out=acc[:, ci, :], in0=acc[:, ci, :], in1=BS[:, ci, :]))
            Y(v)
            for ci in range(8):
                em("v", v.tensor_copy(out=qkb[:, ci, :], in_=acc[:, ci, :]))
            Y(v)
        block.vector(v_a3)

        ln_pass(qkb, h2b, "ln2_w", "ln2_b", halo=False, dst_off=0)

        def g_store(g, t0=t0):
            ec.fence(g, ["v"])
            em("g", g.dma_start(out=io["h2_d"][:, t0:t0 + 512].rearrange(
                "(c p) t -> p c t", p=128), in_=h2b[:]), dma=True)
            em("g", g.dma_start(out=io["acc1"][:, t0:t0 + 512].rearrange(
                "(c p) t -> p c t", p=128), in_=acc[:]), dma=True)
        block.gpsimd(g_store)


def build_p1b(nc, ec, block, io, st):
    S = lambda name, shape, dt=F32: st.enter_context(nc.sbuf_tensor(name, shape, dt))
    P = lambda name, shape: st.enter_context(nc.psum_tensor(name, shape, F32))
    em = ec.em

    h2t = S("b_h2t", [128, 8, 1024], BF16)
    acc = S("b_acc", [128, 8, 1024])
    m = S("b_m", [128, 32, 1024], BF16)
    wld = [S("b_wld%d" % i, [128, 8, 512]) for i in range(3)]
    wbf = [S("b_wbf%d" % i, [128, 8, 512], BF16) for i in range(3)]
    g2w = S("b_g2w", [128, 8])
    pps = [P("b_pp%d" % i, [128, 512]) for i in range(8)]

    def q_setup(q):
        em("q", q.dma_start(out=g2w[:], in_=io["gamma2"]), dma=True)
    block.sync(q_setup)

    cast_mark = {}; mm_done = {}; evac_mark = {}
    NBUF = 3
    kslot = [0]

    def wblock(ap_w, mm_emit, cast_en):
        b = kslot[0] % NBUF
        wb, wf = wld[b], wbf[b]

        def q_w(q, b=b, ap_w=ap_w):
            ec.wait(q, [cast_mark.get(b, ("v", 0))])
            em("q", q.dma_start(out=wb[:], in_=ap_w), dma=True)
        block.sync(q_w)
        ldm = ec.mark("q")

        if cast_en == "v":
            def v_cast(v, b=b, ldm=ldm):
                ec.wait(v, [ldm, mm_done.get(b, ("t", 0))])
                em("v", v.tensor_copy(out=wf[:], in_=wb[:]))
            block.vector(v_cast)
            cast_mark[b] = ec.mark("v")
        else:
            def s_cast(sx, b=b, ldm=ldm):
                ec.wait(sx, [ldm, mm_done.get(b, ("t", 0))])
                em("s", sx.copy(out=wf[:], in_=wb[:]))
            block.scalar(s_cast)
            cast_mark[b] = ec.mark("s")
        mm_emit(wf, b)
        mm_done[b] = ec.mark("t")
        kslot[0] += 1

    for g_i in range(4):
        t0 = g_i * 1024

        def q_ld(q, t0=t0):
            ec.fence(q, ["v", "t", "g", "s", "q"])
            em("q", q.dma_start(out=h2t[:], in_=io["h2_d"][:, t0:t0 + 1024].rearrange(
                "(c p) t -> p c t", p=128)), dma=True)
            em("q", q.dma_start(out=acc[:], in_=io["acc1"][:, t0:t0 + 1024].rearrange(
                "(c p) t -> p c t", p=128)), dma=True)
        block.sync(q_ld)
        ldx = ec.mark("q")

        silu_marks = {}
        for fb in range(8):
            for which, wkey in ((0, "gate_wT"), (1, "up_wT")):
                ap_w = io[wkey][:, fb * 512:(fb + 1) * 512].rearrange("(c p) o -> p c o", p=128)

                def mm_gateup(wf, b, fb=fb, which=which):
                    for fc in range(4):
                        for t5 in range(2):
                            ppi = which * 2 + (fc * 2 + t5) % 2
                            pp = pps[ppi]

                            def t_p(t, wf=wf, fc=fc, t5=t5, pp=pp, ppi=ppi, b=b):
                                ec.wait(t, [cast_mark[b], ldx, evac_mark.get(ppi, ("v", 0))])
                                for ci in range(8):
                                    em("t", t.matmul(pp[:], lhsT=wf[:, ci, fc * 128:(fc + 1) * 128],
                                                     rhs=h2t[:, ci, t5 * 512:(t5 + 1) * 512],
                                                     start=(ci == 0), stop=(ci == 7)))
                            block.tensor(t_p)
                            gm = ec.mark("t")
                            fcg = fb * 4 + fc
                            if which == 0:
                                def s_silu(s, pp=pp, fcg=fcg, t5=t5, gm=gm):
                                    ec.wait(s, [gm])
                                    em("s", s.activation(out=m[:, fcg, t5 * 512:(t5 + 1) * 512],
                                                         in_=pp[:], func=AF.Silu))
                                block.scalar(s_silu)
                                evac_mark[ppi] = ec.mark("s")
                                silu_marks[(fcg, t5)] = evac_mark[ppi]
                            else:
                                def v_mul(v, pp=pp, fcg=fcg, t5=t5, gm=gm):
                                    ec.wait(v, [gm, silu_marks[(fcg, t5)]])
                                    em("v", v.tensor_mul(out=m[:, fcg, t5 * 512:(t5 + 1) * 512],
                                                         in0=m[:, fcg, t5 * 512:(t5 + 1) * 512],
                                                         in1=pp[:]))
                                block.vector(v_mul)
                                evac_mark[ppi] = ec.mark("v")

                wblock(ap_w, mm_gateup, "s" if which == 0 else "v")

        for db in range(2):
            for sub in range(4):
                ap_w = io["down_wT"][sub * 1024:(sub + 1) * 1024,
                                     db * 512:(db + 1) * 512].rearrange("(c p) o -> p c o", p=128)

                def mm_down(wf, b, db=db, sub=sub):
                    for dc in range(4):
                        for t5 in range(2):
                            ppi = dc * 2 + t5
                            pp = pps[ppi]

                            def t_p(t, wf=wf, dc=dc, t5=t5, pp=pp, ppi=ppi, sub=sub, b=b):
                                w8 = [cast_mark[b]]
                                if sub == 0:
                                    w8.append(evac_mark.get(ppi, ("v", 0)))
                                ec.wait(t, w8)
                                for kf in range(8):
                                    em("t", t.matmul(pp[:], lhsT=wf[:, kf, dc * 128:(dc + 1) * 128],
                                                     rhs=m[:, sub * 8 + kf, t5 * 512:(t5 + 1) * 512],
                                                     start=(kf == 0 and sub == 0),
                                                     stop=(kf == 7 and sub == 3)))
                            block.tensor(t_p)

                            if sub == 3:
                                gm = ec.mark("t")
                                dcg = db * 4 + dc

                                def v_acc(v, pp=pp, dcg=dcg, t5=t5, gm=gm):
                                    ec.wait(v, [gm])
                                    em("v", v.scalar_tensor_tensor(
                                        out=acc[:, dcg, t5 * 512:(t5 + 1) * 512],
                                        in0=pp[:], scalar=g2w[:, dcg:dcg + 1],
                                        in1=acc[:, dcg, t5 * 512:(t5 + 1) * 512],
                                        op0=ALU.mult, op1=ALU.add))
                                block.vector(v_acc)
                                evac_mark[ppi] = ec.mark("v")

                wblock(ap_w, mm_down, "v")

        def g_store(g, t0=t0):
            ec.fence(g, ["v"])
            em("g", g.dma_start(out=io["acc1"][:, t0:t0 + 1024].rearrange(
                "(c p) t -> p c t", p=128), in_=acc[:]), dma=True)
        block.gpsimd(g_store)


def build_p1c(nc, ec, block, io, st):
    S = lambda name, shape, dt=F32: st.enter_context(nc.sbuf_tensor(name, shape, dt))
    P = lambda name, shape: st.enter_context(nc.psum_tensor(name, shape, F32))
    em = ec.em

    xnt = S("c_xnt", [128, 8, 1024], BF16)
    acc = S("c_acc", [128, 8, 1024])
    sh = S("c_sh", [128, 8, 1024], BF16)
    wld = [S("c_wld%d" % i, [128, 8, 512]) for i in range(3)]
    wbf = [S("c_wbf%d" % i, [128, 8, 512], BF16) for i in range(3)]
    pps = [P("c_pp%d" % i, [128, 512]) for i in range(8)]

    cast_mark = {}; mm_done = {}; evac_mark = {}
    NBUF = 3
    kslot = [0]

    def wblock(ap_w, mm_emit, cast_en):
        b = kslot[0] % NBUF
        wb, wf = wld[b], wbf[b]

        def q_w(q, b=b, ap_w=ap_w):
            ec.wait(q, [cast_mark.get(b, ("v", 0))])
            em("q", q.dma_start(out=wb[:], in_=ap_w), dma=True)
        block.sync(q_w)
        ldm = ec.mark("q")

        if cast_en == "v":
            def v_cast(v, b=b, ldm=ldm):
                ec.wait(v, [ldm, mm_done.get(b, ("t", 0))])
                em("v", v.tensor_copy(out=wf[:], in_=wb[:]))
            block.vector(v_cast)
            cast_mark[b] = ec.mark("v")
        else:
            def s_cast(sx, b=b, ldm=ldm):
                ec.wait(sx, [ldm, mm_done.get(b, ("t", 0))])
                em("s", sx.copy(out=wf[:], in_=wb[:]))
            block.scalar(s_cast)
            cast_mark[b] = ec.mark("s")
        mm_emit(wf, b)
        mm_done[b] = ec.mark("t")
        kslot[0] += 1

    for g_i in range(4):
        t0 = g_i * 1024

        def q_ld(q, t0=t0):
            ec.fence(q, ["v", "t", "g", "s", "q"])
            em("q", q.dma_start(out=xnt[:], in_=io["xnb_d"][:, t0:t0 + 1024].rearrange(
                "(c p) t -> p c t", p=128)), dma=True)
            em("q", q.dma_start(out=acc[:], in_=io["acc1"][:, t0:t0 + 1024].rearrange(
                "(c p) t -> p c t", p=128)), dma=True)
        block.sync(q_ld)
        ldx = ec.mark("q")

        silu_marks = {}
        for sb in range(2):
            for which, wkey in ((0, "sw1T"), (1, "sw3T")):
                ap_w = io[wkey][:, sb * 512:(sb + 1) * 512].rearrange("(c p) o -> p c o", p=128)

                def mm_gateup(wf, b, sb=sb, which=which):
                    for oc in range(4):
                        for t5 in range(2):
                            ppi = which * 2 + (oc * 2 + t5) % 2
                            pp = pps[ppi]

                            def t_p(t, wf=wf, oc=oc, t5=t5, pp=pp, ppi=ppi, b=b):
                                ec.wait(t, [cast_mark[b], ldx, evac_mark.get(ppi, ("v", 0))])
                                for ci in range(8):
                                    em("t", t.matmul(pp[:], lhsT=wf[:, ci, oc * 128:(oc + 1) * 128],
                                                     rhs=xnt[:, ci, t5 * 512:(t5 + 1) * 512],
                                                     start=(ci == 0), stop=(ci == 7)))
                            block.tensor(t_p)
                            gm = ec.mark("t")
                            occ = sb * 4 + oc
                            if which == 0:
                                def s_silu(s, pp=pp, occ=occ, t5=t5, gm=gm):
                                    ec.wait(s, [gm])
                                    em("s", s.activation(out=sh[:, occ, t5 * 512:(t5 + 1) * 512],
                                                         in_=pp[:], func=AF.Silu))
                                block.scalar(s_silu)
                                evac_mark[ppi] = ec.mark("s")
                                silu_marks[(occ, t5)] = evac_mark[ppi]
                            else:
                                def v_mul(v, pp=pp, occ=occ, t5=t5, gm=gm):
                                    ec.wait(v, [gm, silu_marks[(occ, t5)]])
                                    em("v", v.tensor_mul(out=sh[:, occ, t5 * 512:(t5 + 1) * 512],
                                                         in0=sh[:, occ, t5 * 512:(t5 + 1) * 512],
                                                         in1=pp[:]))
                                block.vector(v_mul)
                                evac_mark[ppi] = ec.mark("v")

                wblock(ap_w, mm_gateup, "s" if which == 0 else "v")

        for db in range(2):
            ap_w = io["sw2T"][:, db * 512:(db + 1) * 512].rearrange("(c p) o -> p c o", p=128)

            def mm_down(wf, b, db=db):
                for dc in range(4):
                    for t5 in range(2):
                        ppi = dc * 2 + t5
                        pp = pps[ppi]

                        def t_p(t, wf=wf, dc=dc, t5=t5, pp=pp, ppi=ppi, b=b):
                            ec.wait(t, [cast_mark[b], evac_mark.get(ppi, ("v", 0))])
                            for ci in range(8):
                                em("t", t.matmul(pp[:], lhsT=wf[:, ci, dc * 128:(dc + 1) * 128],
                                                 rhs=sh[:, ci, t5 * 512:(t5 + 1) * 512],
                                                 start=(ci == 0), stop=(ci == 7)))
                        block.tensor(t_p)
                        gm = ec.mark("t")
                        dcg = db * 4 + dc

                        def v_acc(v, pp=pp, dcg=dcg, t5=t5, gm=gm):
                            ec.wait(v, [gm])
                            em("v", v.tensor_add(out=acc[:, dcg, t5 * 512:(t5 + 1) * 512],
                                                 in0=acc[:, dcg, t5 * 512:(t5 + 1) * 512],
                                                 in1=pp[:]))
                        block.vector(v_acc)
                        evac_mark[ppi] = ec.mark("v")

            wblock(ap_w, mm_down, "v")

        def g_store(g, t0=t0):
            ec.fence(g, ["v"])
            em("g", g.dma_start(out=io["acc1"][:, t0:t0 + 1024].rearrange(
                "(c p) t -> p c t", p=128), in_=acc[:]), dma=True)
        block.gpsimd(g_store)


def build_p2(nc, ec, block, io, st):
    S = lambda name, shape, dt=F32: st.enter_context(nc.sbuf_tensor(name, shape, dt))
    em = ec.em

    x8b = S("p2_x8b", [128, 8, 8], BF16)
    x8f = S("p2_x8f", [128, 8, 8])
    wtl = S("p2_wtl", [128, 8, 512])
    wtb = S("p2_wtb", [128, 8, 512], BF16)
    u8 = S("p2_u8", [8, 512])
    offsr = S("p2_offsr", [1, 8])
    offsb = S("p2_offsb", [128, 8])
    jall = S("p2_jall", [128, 512])
    wall = S("p2_wall", [128, 512])
    iota64 = S("p2_iota64", [128, 64])
    hs_sb = S("p2_hs", [64, 512])
    ones_r = S("p2_onesr", [1, 128])
    identf = S("p2_id", [128, 128])
    heb = S("p2_heb", [8, 512], BF16)
    hef = S("p2_hef", [8, 512])
    lhs4 = S("p2_lhs4", [128, 4, 8], BF16)
    w2b = S("p2_w2b", [128, 4, 1024], BF16)
    w2l = S("p2_w2l", [128, 4, 1024])
    rr_sb = S("p2_rr", [8, 1024])
    rrT = S("p2_rrT", [128, 8, 8])
    acc8 = S("p2_acc8", [128, 8, 8])
    mask0 = S("p2_mask0", [128, 1])
    identb = S("p2_identb", [128, 128], BF16)

    def Y(v):
        ec.selfsync("v", v)

    with contextlib.ExitStack() as st1:
        P1_ = lambda name, shape, dt=F32: st1.enter_context(nc.psum_tensor(name, shape, dt))
        ph = P1_("p2_ph", [64, 512])
        ptr2 = [P1_("p2_ptr%d" % i, [64, 128], BF16) for i in range(2)]
        pg1 = [P1_("p2_pg1%d" % i, [128, 512]) for i in range(2)]
        pg3 = [P1_("p2_pg3%d" % i, [128, 512]) for i in range(2)]

        def q_setup(q):
            ec.fence(q, ["g"])
            em("q", q.dma_start(out=x8f[:], in_=io["xn8T_d"].rearrange("(c p) e -> p c e", p=128)), dma=True)
            em("q", q.dma_start(out=offsr[:], in_=io["offs_d"]), dma=True)
            em("q", q.dma_start(out=jall[:], in_=io["jslot"]), dma=True)
            em("q", q.dma_start(out=wall[:], in_=io["wslot"]), dma=True)
            em("q", q.dma_start(out=identf[:], in_=io["identf"]), dma=True)
            em("q", q.dma_start(out=mask0[:], in_=io["mask0"]), dma=True)
        block.sync(q_setup)

        def g_setup(g):
            em("g", g.memset(ones_r[:], 1.0))
            em("g", g.iota(iota64[:], pattern=[[1, 64]], base=0, channel_multiplier=0,
                           allow_small_or_imprecise_dtypes=True))
            ec.fence(g, ["q"])
            em("g", g.tensor_copy(out=x8b[:], in_=x8f[:]))
            em("g", g.tensor_copy(out=identb[:], in_=identf[:]))
        block.gpsimd(g_setup)

        def t_off(t):
            ec.fence(t, ["q", "g"])
            em("t", t.matmul(pg3[0][:, 0:8], lhsT=ones_r[:], rhs=offsr[:], start=True, stop=True))
        block.tensor(t_off)

        def v_off(v):
            ec.fence(v, ["t"])
            em("v", v.tensor_copy(out=offsb[:], in_=pg3[0][:, 0:8]))
        block.vector(v_off)

        for tab, wkey in (("u1tab", "w1"), ("u3tab", "w3")):
            for e in range(8):
                def q_w(q, wkey=wkey, e=e):
                    ec.fence(q, ["g"])
                    em("q", q.dma_start(out=wtl[:], in_=io[wkey][e].rearrange(
                        "(c p) f -> p c f", p=128)), dma=True)
                block.sync(q_w)

                def g_c(g):
                    ec.fence(g, ["q", "t"])
                    em("g", g.tensor_copy(out=wtb[:], in_=wtl[:]))
                block.gpsimd(g_c)

                def t_u(t, e=e):
                    ec.fence(t, ["g", "v"])
                    for ci in range(8):
                        em("t", t.matmul(pg1[0][0:8, :], lhsT=x8b[:, ci, :], rhs=wtb[:, ci, :],
                                         start=(ci == 0), stop=(ci == 7)))
                block.tensor(t_u)

                def v_u(v):
                    ec.fence(v, ["t"])
                    em("v", v.tensor_copy(out=u8[:], in_=pg1[0][0:8, :]))
                block.vector(v_u)

                def g_st(g, tab=tab, e=e):
                    ec.fence(g, ["v"])
                    em("g", g.dma_start(out=io[tab][e * 8:(e + 1) * 8, :], in_=u8[:]), dma=True)
                block.gpsimd(g_st)

        c_all = S("p2_call", [128, 512])
        eid_a = S("p2_eida", [128, 512])
        tmp_a = S("p2_tmpa", [128, 512])
        iall = S("p2_iall", [128, 512])
        u1s = S("p2_u1s", [64, 512], BF16)
        u3s = S("p2_u3s", [64, 512], BF16)
        u1f = S("p2_u1f", [64, 512])
        u3f = S("p2_u3f", [64, 512])
        ohst2 = [S("p2_oh%d" % i, [128, 64], BF16) for i in range(2)]
        ohT = [S("p2_ohT%d" % i, [64, 128], BF16) for i in range(2)]
        Asc2 = [S("p2_A%d" % i, [128, 512]) for i in range(2)]
        u3g2 = [S("p2_u3g%d" % i, [128, 512]) for i in range(2)]
        Ssc2 = [S("p2_Ss%d" % i, [128, 512]) for i in range(2)]
        hb2 = [S("p2_hb%d" % i, [128, 512], BF16) for i in range(2)]

        def q_uld(q):
            ec.fence(q, ["g"])
            em("q", q.dma_start(out=u1f[:], in_=io["u1tab"]), dma=True)
            em("q", q.dma_start(out=u3f[:], in_=io["u3tab"]), dma=True)
        block.sync(q_uld)
        ulm = ec.mark("q")

        def g_ucast(g):
            ec.wait(g, [ulm])
            em("g", g.tensor_copy(out=u1s[:], in_=u1f[:]))
            em("g", g.tensor_copy(out=u3s[:], in_=u3f[:]))
            em("g", g.iota(iall[:], pattern=[[1, 2], [256, 256]], base=0,
                           channel_multiplier=2,
                           allow_small_or_imprecise_dtypes=True))
        block.gpsimd(g_ucast)

        def v_call(v):
            ec.fence(v, ["g", "q"])
            em("v", v.tensor_scalar(out=eid_a[:], in0=iall[:], scalar1=offsb[:, 0:1],
                                    scalar2=None, op0=ALU.is_ge))
            ec.selfsync("v", v)
            for e in range(1, 8):
                em("v", v.tensor_scalar(out=tmp_a[:], in0=iall[:], scalar1=offsb[:, e:e + 1],
                                        scalar2=None, op0=ALU.is_ge))
                ec.selfsync("v", v)
                em("v", v.tensor_add(out=eid_a[:], in0=eid_a[:], in1=tmp_a[:]))
                ec.selfsync("v", v)
            em("v", v.scalar_tensor_tensor(out=c_all[:], in0=eid_a[:], scalar=8.0,
                                           in1=jall[:], op0=ALU.mult, op1=ALU.add))
            ec.selfsync("v", v)
        block.vector(v_call)

        oh_mark = {}; ohT_mark = {}; trp_mark = {}; gat_mark = {}
        h_mark = {}; silu_mark = {}; hb_mark = {}; hsum_mark = {}

        def t_pref(t):
            ec.fence(t, ["g", "v"])
        block.tensor(t_pref)

        for sc in range(512):
            b = sc % 2
            def v_oh(v, sc=sc, b=b):
                ec.wait(v, [hsum_mark.get(b, ("t", 0))])
                em("v", v.tensor_scalar(out=ohst2[b][:], in0=iota64[:],
                                        scalar1=c_all[:, sc:sc + 1], scalar2=None,
                                        op0=ALU.is_equal))
            block.vector(v_oh)
            oh_mark[b] = ec.mark("v")

            def t_tr2(t, b=b, sc=sc):
                ec.wait(t, [oh_mark[b], ohT_mark.get(b, ("v", 0))])
                em("t", t.transpose(ptr2[b][:], in_=ohst2[b][:], identity=identb[:]))
            block.tensor(t_tr2)
            trp_mark[b] = ec.mark("t")

            def v_ohT(v, b=b):
                ec.wait(v, [trp_mark[b]])
                em("v", v.tensor_copy(out=ohT[b][:], in_=ptr2[b][:]))
            block.vector(v_ohT)
            ohT_mark[b] = ec.mark("v")

            def t_gat(t, b=b):
                ec.wait(t, [ohT_mark[b], h_mark.get(b, ("v", 0)),
                            silu_mark.get(b, ("s", 0))])
                em("t", t.matmul(pg1[b][:], lhsT=ohT[b][:], rhs=u1s[:], start=True, stop=True))
                em("t", t.matmul(pg3[b][:], lhsT=ohT[b][:], rhs=u3s[:], start=True, stop=True))
            block.tensor(t_gat)
            gat_mark[b] = ec.mark("t")

            def v_h(v, sc=sc, b=b):
                ec.wait(v, [gat_mark[b]])
                em("v", v.tensor_scalar_mul(out=Asc2[b][:], in0=pg1[b][:],
                                            scalar1=wall[:, sc:sc + 1]))
                em("v", v.tensor_copy(out=u3g2[b][:], in_=pg3[b][:]))
            block.vector(v_h)
            h_mark[b] = ec.mark("v")

            def s_h(s, b=b):
                ec.wait(s, [h_mark[b], hb_mark.get(b, ("v", 0))])
                em("s", s.activation(out=Ssc2[b][:], in_=Asc2[b][:], func=AF.Silu))
            block.scalar(s_h)
            silu_mark[b] = ec.mark("s")

            def v_h2(v, sc=sc, b=b):
                ec.wait(v, [silu_mark[b], hsum_mark.get(b, ("t", 0))])
                em("v", v.tensor_mul(out=u3g2[b][:], in0=u3g2[b][:], in1=Ssc2[b][:]))
                ec.selfsync("v", v)
                em("v", v.tensor_scalar_mul(out=hb2[b][:], in0=u3g2[b][:],
                                            scalar1=wall[:, sc:sc + 1]))
            block.vector(v_h2)
            hb_mark[b] = ec.mark("v")

            def t_hs(t, sc=sc, b=b):
                ec.wait(t, [hb_mark[b]])
                em("t", t.matmul(ph[:], lhsT=ohst2[b][:], rhs=hb2[b][:],
                                 start=(sc == 0), stop=(sc == 511)))
            block.tensor(t_hs)
            hsum_mark[b] = ec.mark("t")

        def v_hs(v):
            ec.fence(v, ["t"])
            em("v", v.tensor_copy(out=hs_sb[:], in_=ph[:]))
        block.vector(v_hs)

        def g_hs(g):
            ec.fence(g, ["v"])
            em("g", g.dma_start(out=io["hs_d"], in_=hs_sb[:]), dma=True)
        block.gpsimd(g_hs)

    with contextlib.ExitStack() as st2:
        P2_ = lambda name, shape, dt=F32: st2.enter_context(nc.psum_tensor(name, shape, dt))
        pt = P2_("p2_pt", [128, 8], BF16)
        prr = [P2_("p2_prr0", [8, 512]), P2_("p2_prr1", [8, 512])]
        ptj = P2_("p2_ptj", [128, 8])

        for e in range(8):
            def q_he(q, e=e):
                ec.fence(q, ["g", "t"])
                em("q", q.dma_start(out=hef[:], in_=io["hs_d"][e * 8:(e + 1) * 8, :]), dma=True)
                em("q", q.dma_start(out=w2l[:], in_=io["w2"][e].rearrange(
                    "(c p) d -> p c d", p=128)), dma=True)
            block.sync(q_he)

            def g_cast(g):
                ec.fence(g, ["q", "t"])
                em("g", g.tensor_copy(out=heb[:], in_=hef[:]))
                em("g", g.tensor_copy(out=w2b[:], in_=w2l[:]))
            block.gpsimd(g_cast)

            for f4 in range(4):
                def t_tr(t, e=e, f4=f4):
                    ec.fence(t, ["g", "v"])
                    em("t", t.transpose(pt[:], in_=heb[:, f4 * 128:(f4 + 1) * 128],
                                        identity=identb[0:8, 0:8]))
                block.tensor(t_tr)

                def v_tr(v, f4=f4):
                    ec.fence(v, ["t"])
                    em("v", v.tensor_copy(out=lhs4[:, f4, :], in_=pt[:]))
                block.vector(v_tr)

            for dh in range(2):
                def t_rr(t, e=e, dh=dh):
                    ec.fence(t, ["v", "g"])
                    for f4 in range(4):
                        em("t", t.matmul(prr[dh][:], lhsT=lhs4[:, f4, :],
                                         rhs=w2b[:, f4, dh * 512:(dh + 1) * 512],
                                         start=(e == 0 and f4 == 0), stop=(e == 7 and f4 == 3)))
                block.tensor(t_rr)

        def v_rr(v):
            ec.fence(v, ["t"])
            em("v", v.tensor_copy(out=rr_sb[:, 0:512], in_=prr[0][:]))
            em("v", v.tensor_copy(out=rr_sb[:, 512:1024], in_=prr[1][:])); Y(v)
        block.vector(v_rr)

        for ci in range(8):
            def t_tj(t, ci=ci):
                ec.fence(t, ["v"])
                em("t", t.transpose(ptj[:], in_=rr_sb[:, ci * 128:(ci + 1) * 128],
                                    identity=identf[0:8, 0:8]))
            block.tensor(t_tj)

            def v_tj(v, ci=ci):
                ec.fence(v, ["t"])
                em("v", v.tensor_copy(out=rrT[:, ci, :], in_=ptj[:]))
            block.vector(v_tj)

        def q_acc(q):
            ec.fence(q, ["g", "v"])
            em("q", q.dma_start(out=acc8[:], in_=io["acc1"][:, 0:8].rearrange(
                "(c p) e -> p c e", p=128)), dma=True)
        block.sync(q_acc)

        def v_add(v):
            ec.fence(v, ["q"])
            for ci in range(8):
                em("v", v.tensor_scalar_mul(out=rrT[:, ci, :], in0=rrT[:, ci, :],
                                            scalar1=mask0[:]))
            Y(v)
            for ci in range(8):
                em("v", v.tensor_add(out=acc8[:, ci, :], in0=acc8[:, ci, :],
                                     in1=rrT[:, ci, :]))
            Y(v)
        block.vector(v_add)

        def g_out(g):
            ec.fence(g, ["v"])
            em("g", g.dma_start(out=io["acc1"][:, 0:8].rearrange("(c p) e -> p c e", p=128),
                                in_=acc8[:]), dma=True)
        block.gpsimd(g_out)



P3A_DBG = -1

def build_p3a(nc, ec, block, io, st):
    S = lambda name, shape, dt=F32: st.enter_context(nc.sbuf_tensor(name, shape, dt))
    P = lambda name, shape: st.enter_context(nc.psum_tensor(name, shape, F32))
    em = ec.em
    NG = 8

    xg = S("d_xg", [128, 8, 512])
    BS = S("d_bs", [128, 8, 512])
    xn1b = S("d_xn1b", [128, 8, 512], BF16)
    x1af = S("d_x1af", [128, 8, 515])
    x2f = S("d_x2f", [128, 8, 512])
    x2b = S("d_x2b", [128, 8, 512], BF16)
    gx = S("d_gx", [128, 8, 512])
    BA = S("d_ba", [128, 8, 512])
    BM = S("d_bm", [128, 8, 512])
    yb = S("d_yb", [128, 8, 512], BF16)
    Aa = x1af
    wld = [S("d_wld0", [128, 8, 512]), S("d_wld1", [128, 8, 512])]
    wbf = [S("d_wbf0", [128, 8, 512], BF16), S("d_wbf1", [128, 8, 512], BF16)]
    mb = S("d_mb", [128, 512]); rvb = S("d_rvb", [128, 512])
    m_row = S("d_mrow", [1, 512]); v_row = S("d_vrow", [1, 512]); rv_row = S("d_rvrow", [1, 512])
    ones_c = S("d_onesc", [128, 1]); ones_r = S("d_onesr", [1, 128])
    one_col = S("d_onecol", [128, 1])
    zeros512 = S("d_z512", [128, 512])
    carry_h = S("d_carryh", [128, 8, 3])
    carry_s = S("d_carrys", [128, 8])
    par = {}
    for nm in ("l1_b_x", "l1_b_y", "l1_conv_b", "l1_gi_b", "l1_ga_b", "l1_a_param"):
        par[nm] = S("d_" + nm, [128, 8])
    conv1_w = S("d_conv1w", [128, 8, 4])
    msp = S("d_msp", [128, 8])
    t8a = S("d_t8a", [128, 8]); t8b = S("d_t8b", [128, 8])
    pms = [P("d_pm0", [1, 512]), P("d_pm1", [1, 512])]
    pbc = [P("d_pbc0", [128, 512]), P("d_pbc1", [128, 512])]
    pmm = [P("d_pmm0", [128, 512]), P("d_pmm1", [128, 512])]

    def Y(v):
        ec.selfsync("v", v)

    def q_setup(q):
        for nm in par:
            em("q", q.dma_start(out=par[nm][:], in_=io[nm]), dma=True)
        em("q", q.dma_start(out=conv1_w[:], in_=io["conv1_w"]), dma=True)
    block.sync(q_setup)

    def g_setup(g):
        em("g", g.memset(ones_c[:], 1.0))
        em("g", g.memset(ones_r[:], 1.0))
        em("g", g.memset(one_col[:], 1.0))
        em("g", g.memset(zeros512[:], 0.0))
        em("g", g.memset(carry_h[:], 0.0))
        em("g", g.memset(carry_s[:], 0.0))
    block.gpsimd(g_setup)

    def s_sp(s):
        ec.fence(s, ["q", "g"])
        em("s", s.activation(out=t8a[:], in_=par["l1_a_param"][:], func=AF.Abs))
    block.scalar(s_sp)

    def s_sp2(s):
        em("s", s.activation(out=t8a[:], in_=t8a[:], func=AF.Exp, scale=-1.0))
    block.scalar(s_sp2)

    def s_sp3(s):
        em("s", s.activation(out=t8a[:], in_=t8a[:], func=AF.Ln, bias=one_col[:]))
    block.scalar(s_sp3)

    def v_sp(v):
        ec.fence(v, ["s"])
        em("v", v.tensor_scalar_max(out=t8b[:], in0=par["l1_a_param"][:], scalar1=0.0)); Y(v)
        em("v", v.tensor_add(out=msp[:], in0=t8a[:], in1=t8b[:])); Y(v)
        em("v", v.tensor_scalar_mul(out=msp[:], in0=msp[:], scalar1=-8.0)); Y(v)
    block.vector(v_sp)

    def stats(src):
        def s_sq(s):
            ec.fence(s, ["q", "t", "v"])
            em("s", s.activation(out=BS[:], in_=src[:], func=AF.Square))
        block.scalar(s_sq)

        def t_stat(t):
            ec.fence(t, ["s", "v"])
            for ci in range(8):
                em("t", t.matmul(pms[0][:], lhsT=ones_c[:], rhs=src[:, ci, :],
                                 start=(ci == 0), stop=(ci == 7)))
            for ci in range(8):
                em("t", t.matmul(pms[1][:], lhsT=ones_c[:], rhs=BS[:, ci, :],
                                 start=(ci == 0), stop=(ci == 7)))
        block.tensor(t_stat)

        def v_stat(v):
            ec.fence(v, ["t"])
            em("v", v.tensor_scalar_mul(out=m_row[:], in0=pms[0][:], scalar1=1.0 / D)); Y(v)
            em("v", v.tensor_mul(out=v_row[:], in0=m_row[:], in1=m_row[:])); Y(v)
            em("v", v.scalar_tensor_tensor(out=v_row[:], in0=pms[1][:], scalar=1.0 / D,
                                           in1=v_row[:], op0=ALU.mult, op1=ALU.subtract)); Y(v)
            em("v", v.tensor_scalar_add(out=v_row[:], in0=v_row[:], scalar1=1e-5))
        block.vector(v_stat)

        def s_r(s):
            ec.fence(s, ["v"])
            em("s", s.activation(out=rv_row[:], in_=v_row[:], func=AF.Sqrt))
        block.scalar(s_r)

        def v_r(v):
            ec.fence(v, ["s"])
            em("v", v.reciprocal(out=rv_row[:], in_=rv_row[:]))
        block.vector(v_r)

        def t_b(t):
            ec.fence(t, ["v"])
            em("t", t.matmul(pbc[0][:], lhsT=ones_r[:], rhs=m_row[:], start=True, stop=True))
            em("t", t.matmul(pbc[1][:], lhsT=ones_r[:], rhs=rv_row[:], start=True, stop=True))
        block.tensor(t_b)

        def v_b(v):
            ec.fence(v, ["t"])
            em("v", v.tensor_copy(out=mb[:], in_=pbc[0][:]))
            em("v", v.tensor_copy(out=rvb[:], in_=pbc[1][:])); Y(v)
        block.vector(v_b)

    def proj(wkey, rhs_tile, evac, bias_key=None):
        for ob in range(2):
            wb = wld[ob % 2]; wf = wbf[ob % 2]

            def q_w(q, wkey=wkey, ob=ob, wb=wb):
                ec.fence(q, ["g"])
                em("q", q.dma_start(out=wb[:], in_=io[wkey][:, ob * 512:(ob + 1) * 512].rearrange(
                    "(c p) o -> p c o", p=128)), dma=True)
            block.sync(q_w)

            def g_cast(g, wb=wb, wf=wf):
                ec.fence(g, ["q", "t"])
                em("g", g.tensor_copy(out=wf[:], in_=wb[:]))
            block.gpsimd(g_cast)

            for oc in range(4):
                pp = pmm[oc % 2]

                def t_p(t, wf=wf, oc=oc, pp=pp):
                    ec.fence(t, ["g", "v", "s"])
                    for ci in range(8):
                        em("t", t.matmul(pp[:], lhsT=wf[:, ci, oc * 128:(oc + 1) * 128],
                                         rhs=rhs_tile[:, ci, :], start=(ci == 0), stop=(ci == 7)))
                block.tensor(t_p)

                evac(ob * 4 + oc, pp)

    for g_i in range(NG):
        t0 = g_i * 512

        def q_xg(q, t0=t0):
            ec.fence(q, ["v", "s", "g"])
            em("q", q.dma_start(out=xg[:], in_=io["acc1"][:, t0:t0 + 512].rearrange(
                "(c p) t -> p c t", p=128)), dma=True)
        block.sync(q_xg)

        stats(xg)

        def v_xn(v, t0=t0):
            for ci in range(8):
                em("v", v.tensor_sub(out=BS[:, ci, :], in0=xg[:, ci, :], in1=mb[:]))
            Y(v)
            for ci in range(8):
                em("v", v.tensor_mul(out=BS[:, ci, :], in0=BS[:, ci, :], in1=rvb[:]))
            Y(v)
            for ci in range(8):
                em("v", v.tensor_copy(out=xn1b[:, ci, :], in_=BS[:, ci, :]))
            Y(v)
        block.vector(v_xn)

        def g_xn1(g, t0=t0):
            ec.fence(g, ["v"])
            em("g", g.dma_start(out=io["xn1f_d"][:, t0:t0 + 512].rearrange(
                "(c p) t -> p c t", p=128), in_=BS[:]), dma=True)
            em("g", g.dma_start(out=io["xn1b_d"][:, t0:t0 + 512].rearrange(
                "(c p) t -> p c t", p=128), in_=xn1b[:]), dma=True)
        block.gpsimd(g_xn1)

        def ev_x1(occ, pp):
            def v_e(v, occ=occ, pp=pp):
                ec.fence(v, ["t"])
                em("v", v.tensor_scalar(out=x1af[:, occ, 3:515], in0=pp[:],
                                        scalar1=par["l1_b_x"][:, occ:occ + 1], scalar2=None,
                                        op0=ALU.add))
            block.vector(v_e)
        proj("w_x", xn1b, ev_x1)

        def v_conv(v, g_i=g_i):
            for ci in range(8):
                em("v", v.tensor_copy(out=x1af[:, ci, 0:3], in_=carry_h[:, ci, :]))
            Y(v)
            for ci in range(8):
                em("v", v.tensor_copy(out=carry_h[:, ci, :], in_=x1af[:, ci, 512:515]))
            Y(v)
            for ci in range(8):
                em("v", v.tensor_scalar_mul(out=x2f[:, ci, :], in0=x1af[:, ci, 0:512],
                                            scalar1=conv1_w[:, ci, 0:1]))
            Y(v)
            for i in range(1, 4):
                for ci in range(8):
                    em("v", v.scalar_tensor_tensor(out=x2f[:, ci, :], in0=x1af[:, ci, i:i + 512],
                                                   scalar=conv1_w[:, ci, i:i + 1],
                                                   in1=x2f[:, ci, :], op0=ALU.mult, op1=ALU.add))
                Y(v)
            for ci in range(8):
                em("v", v.tensor_scalar(out=x2f[:, ci, :], in0=x2f[:, ci, :],
                                        scalar1=par["l1_conv_b"][:, ci:ci + 1], scalar2=None,
                                        op0=ALU.add))
            Y(v)
            for ci in range(8):
                em("v", v.tensor_copy(out=x2b[:, ci, :], in_=x2f[:, ci, :]))
            Y(v)
        block.vector(v_conv)

        def ev_gx(occ, pp):
            def s_e(s, occ=occ, pp=pp):
                ec.fence(s, ["t"])
                em("s", s.activation(out=gx[:, occ, :], in_=pp[:], func=AF.Sigmoid,
                                     bias=par["l1_gi_b"][:, occ:occ + 1]))
            block.scalar(s_e)
        proj("gi_w", x2b, ev_gx)

        def ev_ga(occ, pp):
            def s_e(s, occ=occ, pp=pp):
                ec.fence(s, ["t"])
                em("s", s.activation(out=BA[:, occ, :], in_=pp[:], func=AF.Sigmoid,
                                     bias=par["l1_ga_b"][:, occ:occ + 1]))
            block.scalar(s_e)
        proj("ga_w", x2b, ev_ga)

        def v_la(v):
            ec.fence(v, ["s"])
            for ci in range(8):
                em("v", v.tensor_scalar_mul(out=BA[:, ci, :], in0=BA[:, ci, :],
                                            scalar1=msp[:, ci:ci + 1]))
            Y(v)
        block.vector(v_la)

        def s_a(s):
            ec.fence(s, ["v"])
            for ci in range(8):
                em("s", s.activation(out=Aa[:, ci, 0:512], in_=BA[:, ci, :], func=AF.Exp))
            for ci in range(8):
                em("s", s.activation(out=BM[:, ci, :], in_=BA[:, ci, :], func=AF.Exp,
                                     scale=2.0))
        block.scalar(s_a)

        def v_mult(v, g_i=g_i):
            ec.fence(v, ["s"])
            for ci in range(8):
                em("v", v.tensor_scalar(out=BM[:, ci, :], in0=BM[:, ci, :], scalar1=-1.0,
                                        scalar2=1.0, op0=ALU.mult, op1=ALU.add))
            Y(v)
            for ci in range(8):
                em("v", v.tensor_scalar_max(out=BM[:, ci, :], in0=BM[:, ci, :], scalar1=0.0))
            Y(v)
        block.vector(v_mult)

        def s_sqrt(s):
            ec.fence(s, ["v"])
            for ci in range(8):
                em("s", s.activation(out=BM[:, ci, :], in_=BM[:, ci, :], func=AF.Sqrt))
        block.scalar(s_sqrt)

        if g_i == 0:
            def g_t0(g):
                ec.fence(g, ["s"])
                em("g", g.memset(BM[:, :, 0:1], 1.0))
                em("g", g.memset(Aa[:, :, 0:1], 0.0))
            block.gpsimd(g_t0)

        def v_u(v):
            ec.fence(v, ["s", "g"])
            for ci in range(8):
                em("v", v.tensor_mul(out=gx[:, ci, :], in0=gx[:, ci, :], in1=x2f[:, ci, :]))
            Y(v)
            for ci in range(8):
                em("v", v.tensor_mul(out=gx[:, ci, :], in0=gx[:, ci, :], in1=BM[:, ci, :]))
            Y(v)
            for ci in range(8):
                em("v", v.tensor_tensor_scan(out=BA[:, ci, :], data0=Aa[:, ci, 0:512],
                                             data1=gx[:, ci, :],
                                             initial=carry_s[:, ci:ci + 1],
                                             op0=ALU.mult, op1=ALU.add))
                ec.selfsync("v", v)
                em("v", v.tensor_copy(out=carry_s[:, ci:ci + 1], in_=BA[:, ci, 511:512]))
                ec.selfsync("v", v)
        block.vector(v_u)

        def ev_y(occ, pp):
            def s_e(s, occ=occ, pp=pp):
                ec.fence(s, ["t"])
                em("s", s.activation(out=yb[:, occ, :], in_=pp[:], func=AF.Gelu,
                                     bias=par["l1_b_y"][:, occ:occ + 1]))
            block.scalar(s_e)
        proj("w_y", xn1b, ev_y)

        def v_out(v):
            ec.fence(v, ["s"])
            for ci in range(8):
                em("v", v.tensor_mul(out=BS[:, ci, :], in0=BA[:, ci, :], in1=yb[:, ci, :]))
            Y(v)
            for ci in range(8):
                em("v", v.tensor_add(out=xg[:, ci, :], in0=xg[:, ci, :], in1=BS[:, ci, :]))
            Y(v)
        block.vector(v_out)

        def g_out(g, t0=t0):
            ec.fence(g, ["v"])
            em("g", g.dma_start(out=io["acc3"][:, t0:t0 + 512].rearrange(
                "(c p) t -> p c t", p=128), in_=xg[:]), dma=True)
        block.gpsimd(g_out)

        if g_i == P3A_DBG:
            def g_dbg(g):
                ec.fence(g, ["v", "s"])
                for off, tile in ((0, Aa), (512, BM), (1024, gx), (1536, BA), (2048, x2f), (2560, BS)):
                    src = tile[:, :, 0:512] if tile is Aa else tile[:]
                    em("g", g.dma_start(out=io["dbgA"][:, off:off + 512].rearrange(
                        "(c p) t -> p c t", p=128), in_=src), dma=True)
            block.gpsimd(g_dbg)


def build_p3b(nc, ec, block, io, st):
    S = lambda name, shape, dt=F32: st.enter_context(nc.sbuf_tensor(name, shape, dt))
    em = ec.em

    def Y(v):
        ec.selfsync("v", v)

    with contextlib.ExitStack() as st1:
        S1 = lambda name, shape, dt=F32: st1.enter_context(nc.sbuf_tensor(name, shape, dt))
        P1_ = lambda name, shape: st1.enter_context(nc.psum_tensor(name, shape, F32))
        xn1f = S1("e_xn1f", [128, 8, 1024])
        gw = S1("e_gw", [128, 8, 8])
        gb8 = S1("e_gb8", [8, 1])
        lg_sb = S1("e_lg", [8, 1024])
        lt = S1("e_lt", [128, 8, 8])
        probs = S1("e_probs", [128, 8, 8])
        pr = S1("e_pr", [128, 8, 8])
        k1 = S1("e_k1", [128, 8, 8])
        k2 = S1("e_k2", [128, 8, 8])
        red = S1("e_red", [128, 8])
        iota8 = S1("e_iota8", [128, 8])
        comb = S1("e_comb", [128, 8, 8])
        crow = S1("e_crow", [8, 1024])
        identf = S1("e_id", [128, 128])
        plg = P1_("e_plg", [8, 512])
        ptr = P1_("e_ptr", [128, 8])
        ptc = P1_("e_ptc", [8, 128])

        def redb(n):
            return red[:].rearrange("p (s o) -> p s o", o=1).broadcast_to([128, 8, n])

        def q_setup(q):
            em("q", q.dma_start(out=gw[:], in_=io["gate1_w"].rearrange("(c p) e -> p c e", p=128)), dma=True)
            em("q", q.dma_start(out=gb8[:], in_=io["gate1_bc"]), dma=True)
            em("q", q.dma_start(out=identf[:], in_=io["identf"]), dma=True)
        block.sync(q_setup)

        def g_setup(g):
            em("g", g.iota(iota8[:], pattern=[[1, 8]], base=0, channel_multiplier=0,
                           allow_small_or_imprecise_dtypes=True))
        block.gpsimd(g_setup)

        for g_i in range(4):
            t0 = g_i * 1024

            def q_ld(q, t0=t0):
                ec.fence(q, ["v", "t", "g"])
                em("q", q.dma_start(out=xn1f[:], in_=io["xn1f_d"][:, t0:t0 + 1024].rearrange(
                    "(c p) t -> p c t", p=128)), dma=True)
            block.sync(q_ld)

            for t5 in range(2):
                def t_lg(t, t5=t5):
                    ec.fence(t, ["q", "v", "g"])
                    for ci in range(8):
                        em("t", t.matmul(plg[:], lhsT=gw[:, ci, :],
                                         rhs=xn1f[:, ci, t5 * 512:(t5 + 1) * 512],
                                         start=(ci == 0), stop=(ci == 7)))
                block.tensor(t_lg)

                def v_lg(v, t5=t5):
                    ec.fence(v, ["t"])
                    em("v", v.tensor_scalar(out=lg_sb[:, t5 * 512:(t5 + 1) * 512], in0=plg[:],
                                            scalar1=gb8[:], scalar2=None, op0=ALU.add))
                block.vector(v_lg)

            for tc in range(8):
                def t_tr(t, tc=tc):
                    ec.fence(t, ["v"])
                    em("t", t.transpose(ptr[:], in_=lg_sb[:, tc * 128:(tc + 1) * 128],
                                        identity=identf[0:8, 0:8]))
                block.tensor(t_tr)

                def v_tr(v, tc=tc):
                    ec.fence(v, ["t"])
                    em("v", v.tensor_copy(out=lt[:, tc, :], in_=ptr[:]))
                block.vector(v_tr)

            def v_sm(v):
                Y(v)
                em("v", v.reduce_max(out=red[:], in_=lt[:], axis=AX.X)); Y(v)
                em("v", v.tensor_sub(out=pr[:], in0=lt[:], in1=redb(8))); Y(v)
            block.vector(v_sm)

            def s_exp(s):
                ec.fence(s, ["v"])
                em("s", s.activation(out=pr[:], in_=pr[:], func=AF.Exp))
            block.scalar(s_exp)

            def v_sm2(v):
                ec.fence(v, ["s"])
                em("v", v.reduce_sum(out=red[:], in_=pr[:], axis=AX.X)); Y(v)
                em("v", v.reciprocal(out=red[:], in_=red[:])); Y(v)
                em("v", v.tensor_mul(out=probs[:], in0=pr[:], in1=redb(8))); Y(v)
                em("v", v.scalar_tensor_tensor(out=pr[:], in0=iota8[:].rearrange(
                    "p (o e) -> p o e", o=1).broadcast_to([128, 8, 8]), scalar=-TIE,
                    in1=probs[:], op0=ALU.mult, op1=ALU.add)); Y(v)
                em("v", v.reduce_max(out=red[:], in_=pr[:], axis=AX.X)); Y(v)
                em("v", v.tensor_tensor(out=k1[:], in0=pr[:], in1=redb(8), op=ALU.is_ge)); Y(v)
                em("v", v.scalar_tensor_tensor(out=pr[:], in0=k1[:], scalar=-1e4, in1=pr[:],
                                               op0=ALU.mult, op1=ALU.add)); Y(v)
                em("v", v.reduce_max(out=red[:], in_=pr[:], axis=AX.X)); Y(v)
                em("v", v.tensor_tensor(out=k2[:], in0=pr[:], in1=redb(8), op=ALU.is_ge)); Y(v)
                em("v", v.tensor_add(out=k1[:], in0=k1[:], in1=k2[:])); Y(v)
                em("v", v.tensor_mul(out=comb[:], in0=k1[:], in1=probs[:])); Y(v)
                em("v", v.reduce_sum(out=red[:], in_=comb[:], axis=AX.X)); Y(v)
                em("v", v.reciprocal(out=red[:], in_=red[:])); Y(v)
                em("v", v.tensor_mul(out=comb[:], in0=comb[:], in1=redb(8))); Y(v)
            block.vector(v_sm2)

            for tc in range(8):
                def t_tc(t, tc=tc):
                    ec.fence(t, ["v"])
                    em("t", t.transpose(ptc[:], in_=comb[:, tc, :], identity=identf[:]))
                block.tensor(t_tc)

                def v_tc(v, tc=tc):
                    ec.fence(v, ["t"])
                    em("v", v.tensor_copy(out=crow[:, tc * 128:(tc + 1) * 128], in_=ptc[:]))
                block.vector(v_tc)

            def g_crow(g, t0=t0):
                ec.fence(g, ["v"])
                em("g", g.dma_start(out=io["crow_d"][:, t0:t0 + 1024], in_=crow[:]), dma=True)
            block.gpsimd(g_crow)

    with contextlib.ExitStack() as st2:
        S2 = lambda name, shape, dt=F32: st2.enter_context(nc.sbuf_tensor(name, shape, dt))
        P2_ = lambda name, shape: st2.enter_context(nc.psum_tensor(name, shape, F32))
        xn1b = S2("e_xn1b", [128, 8, 1024], BF16)
        acc = S2("e_acc", [128, 8, 1024])
        m = S2("e_m", [128, 32, 1024], BF16)
        wld = [S2("e_wld0", [128, 8, 512]), S2("e_wld1", [128, 8, 512]),
               S2("e_wld2", [128, 8, 512])]
        wbf = [S2("e_wbf0", [128, 8, 512], BF16), S2("e_wbf1", [128, 8, 512], BF16),
               S2("e_wbf2", [128, 8, 512], BF16)]
        crow1 = S2("e_crow1", [1, 1024])
        bcb = S2("e_bcb", [128, 2, 512], BF16)
        dtmp = S2("e_dtmp", [128, 512])
        ones_r = S2("e_onesr", [1, 128])
        pps = [P2_("e_pp%d" % i, [128, 512]) for i in range(8)]

        def g_setup2(g):
            ec.fence(g, ["g", "v", "q"])
            em("g", g.memset(ones_r[:], 1.0))
        block.gpsimd(g_setup2)

        cast_mark = {}
        mm_done = {}
        evac_mark = {}
        NBUF = 3

        def wblock(kslot, wkey_ap, mm_emit, cast_en="v"):
            b = kslot[0] % NBUF
            wb, wf = wld[b], wbf[b]

            def q_w(q, b=b, wkey_ap=wkey_ap):
                ec.wait(q, [cast_mark.get(b, ("v", 0))])
                em("q", q.dma_start(out=wb[:], in_=wkey_ap), dma=True)
            block.sync(q_w)
            ldm = ec.mark("q")

            def v_cast(v, b=b, ldm=ldm):
                ec.wait(v, [ldm, mm_done.get(b, ("t", 0))])
                em("v" if cast_en == "v" else cast_en,
                   (v.tensor_copy(out=wf[:], in_=wb[:])))
            if cast_en == "v":
                block.vector(v_cast)
                cast_mark[b] = ec.mark("v")
            else:
                def s_cast(sx, b=b, ldm=ldm):
                    ec.wait(sx, [ldm, mm_done.get(b, ("t", 0))])
                    em("s", sx.tensor_copy(out=wf[:], in_=wb[:])) if False else em(
                        "s", sx.copy(out=wf[:], in_=wb[:]))
                block.scalar(s_cast)
                cast_mark[b] = ec.mark("s")
            mm_emit(wf, b)
            mm_done[b] = ec.mark("t")
            kslot[0] += 1

        kslot = [0]
        ldx_mark = [None]

        for g_i in range(4):
            t0 = g_i * 1024

            def q_ld(q, t0=t0):
                ec.fence(q, ["v", "t", "g", "s", "q"])
                em("q", q.dma_start(out=xn1b[:], in_=io["xn1b_d"][:, t0:t0 + 1024].rearrange(
                    "(c p) t -> p c t", p=128)), dma=True)
                em("q", q.dma_start(out=acc[:], in_=io["acc3"][:, t0:t0 + 1024].rearrange(
                    "(c p) t -> p c t", p=128)), dma=True)
            block.sync(q_ld)
            ldx = ec.mark("q")

            for e in range(8):
                def q_cr(q, e=e, t0=t0):
                    ec.wait(q, [evac_mark.get(100, ("v", 0))])
                    ec.fence(q, ["v"]) if False else None
                    em("q", q.dma_start(out=crow1[:], in_=io["crow_d"][e:e + 1, t0:t0 + 1024]), dma=True)
                block.sync(q_cr)
                crm = ec.mark("q")

                for t5 in range(2):
                    def t_bc(t, t5=t5, crm=crm):
                        ec.wait(t, [crm, evac_mark.get(4 + t5, ("v", 0))])
                        em("t", t.matmul(pps[4 + t5][:], lhsT=ones_r[:],
                                         rhs=crow1[0:1, t5 * 512:(t5 + 1) * 512],
                                         start=True, stop=True))
                    block.tensor(t_bc)
                    bm = ec.mark("t")

                    def v_bc(v, t5=t5, bm=bm):
                        ec.wait(v, [bm])
                        em("v", v.tensor_copy(out=bcb[:, t5, :], in_=pps[4 + t5][:]))
                    block.vector(v_bc)
                    evac_mark[4 + t5] = ec.mark("v")

                silu_marks = {}
                for fb in range(8):
                    for which, wkey in ((0, "w1f"), (1, "w3f")):
                        ap_w = io[wkey][e][:, fb * 512:(fb + 1) * 512].rearrange(
                            "(c p) o -> p c o", p=128)

                        def mm_gateup(wf, b, fb=fb, which=which, e=e):
                            for fc in range(4):
                                for t5 in range(2):
                                    ppi = which * 2 + (fc * 2 + t5) % 2
                                    pp = pps[ppi]

                                    def t_p(t, wf=wf, fc=fc, t5=t5, pp=pp, ppi=ppi):
                                        ec.wait(t, [cast_mark[b], ldx,
                                                    evac_mark.get(ppi, ("v", 0))])
                                        for ci in range(8):
                                            em("t", t.matmul(pp[:], lhsT=wf[:, ci, fc * 128:(fc + 1) * 128],
                                                             rhs=xn1b[:, ci, t5 * 512:(t5 + 1) * 512],
                                                             start=(ci == 0), stop=(ci == 7)))
                                    block.tensor(t_p)
                                    gm = ec.mark("t")

                                    fcg = fb * 4 + fc
                                    if which == 0:
                                        def s_silu(s, pp=pp, fcg=fcg, t5=t5, gm=gm):
                                            ec.wait(s, [gm])
                                            em("s", s.activation(out=m[:, fcg, t5 * 512:(t5 + 1) * 512],
                                                                 in_=pp[:], func=AF.Silu))
                                        block.scalar(s_silu)
                                        evac_mark[ppi] = ec.mark("s")
                                        silu_marks[(fcg, t5)] = evac_mark[ppi]
                                    else:
                                        def v_mul(v, pp=pp, fcg=fcg, t5=t5, gm=gm):
                                            ec.wait(v, [gm, silu_marks[(fcg, t5)]])
                                            em("v", v.tensor_mul(out=m[:, fcg, t5 * 512:(t5 + 1) * 512],
                                                                 in0=m[:, fcg, t5 * 512:(t5 + 1) * 512],
                                                                 in1=pp[:]))
                                        block.vector(v_mul)
                                        evac_mark[ppi] = ec.mark("v")

                        wblock(kslot, ap_w, mm_gateup, cast_en="s" if which == 0 else "v")

                for db in range(2):
                    for sub in range(4):
                        ap_w = io["w2f"][e][sub * 1024:(sub + 1) * 1024,
                                            db * 512:(db + 1) * 512].rearrange(
                            "(c p) o -> p c o", p=128)

                        def mm_down(wf, b, db=db, sub=sub, e=e):
                            for dc in range(4):
                                for t5 in range(2):
                                    ppi = dc * 2 + t5
                                    pp = pps[ppi]

                                    def t_p(t, wf=wf, dc=dc, t5=t5, pp=pp, ppi=ppi, sub=sub):
                                        w8 = [cast_mark[b]]
                                        if sub == 0:
                                            w8.append(evac_mark.get(ppi, ("v", 0)))
                                        ec.wait(t, w8)
                                        for kf in range(8):
                                            em("t", t.matmul(pp[:], lhsT=wf[:, kf, dc * 128:(dc + 1) * 128],
                                                             rhs=m[:, sub * 8 + kf, t5 * 512:(t5 + 1) * 512],
                                                             start=(kf == 0 and sub == 0),
                                                             stop=(kf == 7 and sub == 3)))
                                    block.tensor(t_p)

                                    if sub == 3:
                                        gm = ec.mark("t")
                                        dcg = db * 4 + dc

                                        def v_acc(v, pp=pp, dcg=dcg, t5=t5, gm=gm):
                                            ec.wait(v, [gm])
                                            em("v", v.tensor_mul(out=dtmp[:], in0=pp[:], in1=bcb[:, t5, :]))
                                            ec.selfsync("v", v)
                                            em("v", v.tensor_add(out=acc[:, dcg, t5 * 512:(t5 + 1) * 512],
                                                                 in0=acc[:, dcg, t5 * 512:(t5 + 1) * 512],
                                                                 in1=dtmp[:]))
                                        block.vector(v_acc)
                                        evac_mark[ppi] = ec.mark("v")

                        wblock(kslot, ap_w, mm_down, cast_en="v")
                evac_mark[100] = ec.mark("v")

            def g_store(g, t0=t0):
                ec.fence(g, ["v"])
                em("g", g.dma_start(out=io["outT"][:, t0:t0 + 1024].rearrange(
                    "(c p) t -> p c t", p=128), in_=acc[:]), dma=True)
            block.gpsimd(g_store)


def phase_barrier(ec, block):
    for dst in EngCtx.NAMES:
        def body(q, dst=dst):
            ec.fence(q, list(EngCtx.NAMES))
        getattr(block, _BLK[dst])(body)


def run_phases(build_fns, which, in_maps, n_cores=8, trace=False):
    nc = bass.Bass("TRN2", target_bir_lowering=False)
    io = dram_io(nc, which)
    with contextlib.ExitStack() as st:
        ec = EngCtx(nc, st)
        block = st.enter_context(nc.Block())
        for i, fn in enumerate(build_fns):
            if i:
                phase_barrier(ec, block)
            with contextlib.ExitStack() as pst:
                fn(nc, ec, block, io, pst)

        def g_done(g):
            ec.fence(g, ["q", "s", "v", "t"])
        block.gpsimd(g_done)
    return run_bass_kernel_spmd(nc, in_maps, core_ids=list(range(n_cores)), trace=trace)


INPUT_NAMES = [
    "xT", "xaT", "lhsA", "bias_row", "identf", "mask0",
    "qk_wT", "v_wT", "g_wT", "o_wT", "gate_wT", "up_wT", "down_wT",
    "ln1_w", "ln1_b", "conv_b", "g_b", "onorm_w", "onorm_b", "ln2_w", "ln2_b",
    "gamma1", "gamma2", "conv_w", "gammaA", "freqs", "headsel", "headexp",
    "sw1T", "sw2T", "sw3T", "w1", "w3", "w2",
    "w_x", "w_y", "gi_w", "ga_w", "l1_b_x", "l1_b_y", "l1_conv_b",
    "l1_gi_b", "l1_ga_b", "l1_a_param", "conv1_w",
    "gate1_w", "gate1_bc", "w1f", "w3f", "w2f",
]
INTER_NAMES = [
    "jslot", "wslot", "offs_d", "xn8T_d", "mrow_d", "u1tab", "u3tab", "hs_d",
    "acc1", "acc3", "h2_d", "xnb_d", "xn1f_d", "xn1b_d", "crow_d",
]
ALL_PHASES = None


def kernel(**inputs):
    phases = [build_p0, build_p1a, build_p1b, build_p1c, build_p2,
              build_p3a, build_p3b]
    which = {n: "in" for n in INPUT_NAMES}
    which.update({n: "int" for n in INTER_NAMES})
    which["outT"] = "out"
    in_maps = []
    for core in range(8):
        hi = build_host_inputs(inputs, core)
        in_maps.append({n: hi[n] for n in INPUT_NAMES})
    res = run_phases(phases, which, in_maps, n_cores=8)
    out = np.stack([np.asarray(res.results[c]["outT"]).T for c in range(8)], axis=0)
    return np.ascontiguousarray(out).astype(np.float32)


# version 09
# speedup vs baseline: 1.0550x; 1.0550x over previous
import contextlib
import numpy as np
import concourse.bass as bass
import concourse.mybir as mybir
from concourse.bass_utils import run_bass_kernel_spmd

F32 = mybir.dt.float32
BF16 = mybir.dt.bfloat16
I32 = mybir.dt.int32
AF = mybir.ActivationFunctionType
ALU = mybir.AluOpType
AX = mybir.AxisListType

TWO_PI = 6.283185307179586
INV_2PI = 1.0 / TWO_PI
CW1 = 6.25
CW2 = 68.0 / 2048.0
CW3 = float(np.float32(np.float64(TWO_PI) - CW1 - CW2))
MAGIC = 12582912.0
HALF_PI = float(np.float32(np.pi / 2))
PI_F = float(np.float32(np.pi))
TIE = 5e-11

T_SEQ = 4096
D = 1024
NTOK_ALL = 32768

_BLK = {"t": "tensor", "v": "vector", "s": "scalar", "g": "gpsimd", "q": "sync"}


class EngCtx:
    NAMES = ("t", "v", "s", "g", "q")

    def __init__(self, nc, st):
        self.nc = nc
        self.sem = {n: st.enter_context(nc.semaphore("pc_" + n)) for n in self.NAMES}
        self.val = {n: 0 for n in self.NAMES}
        self.last = {n: None for n in self.NAMES}

    def em(self, en, inst, dma=False):
        self.last[en] = [inst, False, 16 if dma else 1]
        if dma:
            self._seal(en)
        return inst

    def _seal(self, en):
        li = self.last[en]
        if li is None:
            return
        inst, inced, step = li
        if not inced:
            inst.then_inc(self.sem[en], step)
            self.val[en] += step
            li[1] = True

    def selfsync(self, en, q):
        self._seal(en)
        q.wait_ge(self.sem[en], self.val[en])

    def fence(self, dst_q, srcs):
        for s_en in srcs:
            if self.last[s_en] is None:
                continue
            self._seal(s_en)
            dst_q.wait_ge(self.sem[s_en], self.val[s_en])

    def mark(self, en):
        if self.last[en] is None:
            return (en, 0)
        self._seal(en)
        return (en, self.val[en])

    def wait(self, q, marks):
        for en, val in marks:
            if val > 0:
                q.wait_ge(self.sem[en], val)


def perch(v):
    return np.ascontiguousarray(np.asarray(v, np.float32).reshape(8, 128).T)


def build_host_inputs(inputs, core):
    x = np.asarray(inputs["L0_attn_x"], np.float32)
    d = {}
    d["xT"] = np.ascontiguousarray(x[core].T)
    d["xaT"] = np.ascontiguousarray(x.reshape(NTOK_ALL, D).T)
    wgT = np.asarray(inputs["L0_ffn_w_gate"], np.float32).T
    lhsA = np.zeros((128, 8, 9), np.float32)
    lhsA[:, :, :8] = wgT.reshape(8, 128, 8).transpose(1, 0, 2)
    lhsA[:, :, 8] = 1.0
    d["lhsA"] = np.ascontiguousarray(lhsA.reshape(128, 72))
    d["bias_row"] = np.asarray(inputs["L0_ffn_expert_bias"], np.float32).reshape(1, 8)
    for nm in ("qk_w", "v_w", "g_w", "o_w"):
        d[nm + "T"] = np.ascontiguousarray(np.asarray(inputs["L0_attn_" + nm], np.float32).T)
    d["gate_wT"] = np.ascontiguousarray(np.asarray(inputs["L0_attn_gate_w"], np.float32).T)
    d["up_wT"] = np.ascontiguousarray(np.asarray(inputs["L0_attn_up_w"], np.float32).T)
    d["down_wT"] = np.ascontiguousarray(np.asarray(inputs["L0_attn_down_w"], np.float32).T)
    for nm in ("ln1_w", "ln1_b", "conv_b", "g_b", "onorm_w", "onorm_b",
               "ln2_w", "ln2_b", "gamma1", "gamma2"):
        d[nm] = perch(inputs["L0_attn_" + nm])
    d["conv_w"] = np.ascontiguousarray(
        np.asarray(inputs["L0_attn_conv_w"], np.float32)[:, 0, :].reshape(8, 128, 4).transpose(1, 0, 2))
    d["gammaA"] = perch(np.repeat(np.asarray(inputs["L0_attn_gamma_attn"], np.float32).reshape(16), 64))
    for nm in ("sw1", "sw2", "sw3"):
        d[nm + "T"] = np.ascontiguousarray(np.asarray(inputs["L0_ffn_" + nm], np.float32).T)
    d["w1"] = np.ascontiguousarray(np.asarray(inputs["L0_ffn_w1"], np.float32))
    d["w3"] = np.ascontiguousarray(np.asarray(inputs["L0_ffn_w3"], np.float32))
    d["w2"] = np.ascontiguousarray(np.asarray(inputs["L0_ffn_w2"], np.float32))
    for nm in ("w_x", "w_y", "gi_w", "ga_w"):
        d[nm] = np.ascontiguousarray(np.asarray(inputs["L1_attn_" + nm], np.float32))
    for nm in ("b_x", "b_y", "conv_b", "gi_b", "ga_b", "a_param"):
        d["l1_" + nm] = perch(inputs["L1_attn_" + nm])
    d["conv1_w"] = np.ascontiguousarray(
        np.asarray(inputs["L1_attn_conv_w"], np.float32).T.reshape(8, 128, 4).transpose(1, 0, 2))
    d["gate1_w"] = np.ascontiguousarray(np.asarray(inputs["L1_ffn_gate_w"], np.float32))
    d["gate1_bc"] = np.asarray(inputs["L1_ffn_gate_b"], np.float32).reshape(8, 1)
    d["w1f"] = np.ascontiguousarray(np.asarray(inputs["L1_ffn_w1"], np.float32))
    d["w3f"] = np.ascontiguousarray(np.asarray(inputs["L1_ffn_w3"], np.float32))
    d["w2f"] = np.ascontiguousarray(np.asarray(inputs["L1_ffn_w2"], np.float32))
    d["mask0"] = np.full((128, 1), 1.0 if core == 0 else 0.0, np.float32)
    d["identf"] = np.eye(128, dtype=np.float32)
    idx = np.linspace(0.0, 1.0, 64).astype(np.float32)
    freqs = ((10.0 ** idx)[None, :] * np.arange(16, dtype=np.float32)[:, None]).astype(np.float32)
    d["freqs"] = perch(freqs.reshape(1024))
    hs = np.zeros((128, 8, 16), np.float32)
    for cc in range(8):
        for p in range(128):
            hs[p, cc, 2 * cc + (p >> 6)] = 1.0
    d["headsel"] = np.ascontiguousarray(hs)
    d["headexp"] = np.ascontiguousarray(np.transpose(hs, (2, 1, 0)))
    return d


DRAM_SHAPES = {
    "xT": ([D, T_SEQ], F32), "xaT": ([D, NTOK_ALL], F32),
    "lhsA": ([128, 72], F32), "bias_row": ([1, 8], F32),
    "qk_wT": ([D, D], F32), "v_wT": ([D, D], F32), "g_wT": ([D, D], F32),
    "o_wT": ([D, D], F32), "gate_wT": ([D, 4096], F32), "up_wT": ([D, 4096], F32),
    "down_wT": ([4096, D], F32),
    "ln1_w": ([128, 8], F32), "ln1_b": ([128, 8], F32), "conv_b": ([128, 8], F32),
    "g_b": ([128, 8], F32), "onorm_w": ([128, 8], F32), "onorm_b": ([128, 8], F32),
    "ln2_w": ([128, 8], F32), "ln2_b": ([128, 8], F32),
    "gamma1": ([128, 8], F32), "gamma2": ([128, 8], F32),
    "conv_w": ([128, 8, 4], F32), "gammaA": ([128, 8], F32),
    "sw1T": ([D, D], F32), "sw2T": ([D, D], F32), "sw3T": ([D, D], F32),
    "w1": ([8, D, 512], F32), "w3": ([8, D, 512], F32), "w2": ([8, 512, D], F32),
    "w_x": ([D, D], F32), "w_y": ([D, D], F32), "gi_w": ([D, D], F32), "ga_w": ([D, D], F32),
    "l1_b_x": ([128, 8], F32), "l1_b_y": ([128, 8], F32), "l1_conv_b": ([128, 8], F32),
    "l1_gi_b": ([128, 8], F32), "l1_ga_b": ([128, 8], F32), "l1_a_param": ([128, 8], F32),
    "conv1_w": ([128, 8, 4], F32),
    "gate1_w": ([D, 8], F32), "gate1_bc": ([8, 1], F32),
    "w1f": ([8, D, 4096], F32), "w3f": ([8, D, 4096], F32), "w2f": ([8, 4096, D], F32),
    "mask0": ([128, 1], F32), "identf": ([128, 128], F32),
    "freqs": ([128, 8], F32), "headsel": ([128, 8, 16], F32), "headexp": ([16, 8, 128], F32),
    "jslot": ([128, 512], F32), "wslot": ([128, 512], F32),
    "offs_d": ([1, 8], F32), "xn8T_d": ([D, 8], F32), "mrow_d": ([1, 512], F32),
    "u1tab": ([64, 512], F32), "u3tab": ([64, 512], F32), "hs_d": ([64, 512], F32),
    "acc1": ([D, T_SEQ], F32), "acc3": ([D, T_SEQ], F32), "h2_d": ([D, T_SEQ], BF16), "xnb_d": ([D, T_SEQ], BF16), "dbg": ([128, 96], F32), "dbg2": ([9, 512], F32),
    "xn1f_d": ([D, T_SEQ], F32), "xn1b_d": ([D, T_SEQ], BF16), "crow_d": ([8, T_SEQ], F32),
    "outT": ([D, T_SEQ], F32), "dbgA": ([D, T_SEQ], F32),
}


def dram_io(nc, which):
    kinds = {"in": "ExternalInput", "out": "ExternalOutput", "int": "Internal"}
    io = {}
    for nm, kind in which.items():
        sh, dt = DRAM_SHAPES[nm]
        io[nm] = nc.dram_tensor(nm, sh, dt, kind=kinds[kind]).ap()
    return io


P0_SKIP = set()
P0_DBG_TC = -1

def build_p0(nc, ec, block, io, st):
    S = lambda name, shape, dt=F32: st.enter_context(nc.sbuf_tensor(name, shape, dt))
    P = lambda name, shape: st.enter_context(nc.psum_tensor(name, shape, F32))
    em = ec.em

    xa = [S("p0_xa0", [128, 8, 512]), S("p0_xa1", [128, 8, 512])]
    sq = [S("p0_sq0", [128, 8, 512]), S("p0_sq1", [128, 8, 512])]
    lhsA = S("p0_lhsA", [128, 72])
    ones_r = S("p0_onesr", [1, 128])
    identf = S("p0_id", [128, 128])
    bias_b = S("p0_biasb", [128, 8])
    iota8 = S("p0_iota8", [128, 8])
    wgs = S("p0_wgs", [9, 1])
    m_row = S("p0_m", [1, 512])
    v_row = S("p0_v", [1, 512])
    rv_row = S("p0_rv", [1, 512])
    mb8 = S("p0_mb8", [8, 512])
    lt = S("p0_lt", [128, 4, 8])
    scores = S("p0_scores", [128, 4, 8])
    sp = S("p0_sp", [128, 4, 8])
    tmp8 = S("p0_tmp8", [128, 4, 8])
    keep = S("p0_keep", [128, 4, 8])
    k1 = S("p0_k1", [128, 4, 8])
    k2 = S("p0_k2", [128, 4, 8])
    gs = S("p0_gs", [128, 4, 4])
    gs2 = S("p0_gs2", [128, 4, 4])
    red = S("p0_red", [128, 4])
    t9 = S("p0_t9", [9, 512])
    cnt = S("p0_cnt", [128, 8])
    jstage = S("p0_jstage", [128, 512])
    wstage = S("p0_wstage", [128, 512])
    xn8 = S("p0_xn8", [128, 8, 8])
    mb128 = S("p0_mb128", [128, 8])
    rvb128 = S("p0_rvb128", [128, 8])
    cnt_row = S("p0_cntrow", [1, 8])
    dbgt = S("p0_dbgt", [128, 96])
    offs_r = S("p0_offsr", [1, 8])
    zero8 = S("p0_zero8", [1, 8])

    def redb(n):
        return red[:].rearrange("p (s o) -> p s o", o=1).broadcast_to([128, 4, n])

    psA = [P("p0_psA0", [9, 512]), P("p0_psA1", [9, 512])]
    psB = [P("p0_psB0", [1, 512]), P("p0_psB1", [1, 512])]
    psT = P("p0_psT", [128, 4, 8])
    psb = P("p0_psb", [9, 512])
    psb2 = P("p0_psb2", [8, 512])

    def q_setup(q):
        em("q", q.dma_start(out=lhsA[:], in_=io["lhsA"]), dma=True)
        em("q", q.dma_start(out=identf[:], in_=io["identf"]), dma=True)
        em("q", q.dma_start(out=bias_b[0:1, :], in_=io["bias_row"]), dma=True)
    block.sync(q_setup)

    def g_setup(g):
        em("g", g.iota(iota8[:], pattern=[[1, 8]], base=0, channel_multiplier=0,
                       allow_small_or_imprecise_dtypes=True))
        em("g", g.memset(cnt[:], 0.0))
        em("g", g.memset(zero8[:], 0.0))
        em("g", g.memset(ones_r[:], 1.0))
    block.gpsimd(g_setup)

    ones_c = S("p0_onesc", [128, 1])

    def g_setup2(g):
        em("g", g.memset(ones_c[:], 1.0))
    block.gpsimd(g_setup2)

    def t_setup2(t):
        ec.fence(t, ["q", "g"])
        for ci in range(8):
            em("t", t.matmul(psb[0:9, 0:1], lhsT=lhsA[:, 9 * ci:9 * ci + 9],
                             rhs=ones_c[:], start=(ci == 0), stop=(ci == 7)))
        em("t", t.matmul(psT[:, 0, :], lhsT=ones_r[:], rhs=bias_b[0:1, :],
                         start=True, stop=True))
    block.tensor(t_setup2)

    def v_setup(v):
        ec.fence(v, ["t"])
        em("v", v.tensor_copy(out=wgs[:], in_=psb[0:9, 0:1]))
        em("v", v.tensor_copy(out=bias_b[:], in_=psT[:, 0, :]))
    block.vector(v_setup)

    for tc in range(64):
        b = tc % 2
        xa_t, sq_t, pA, pB = xa[b], sq[b], psA[b], psB[b]

        def q_load(q, tc=tc, xa_t=xa_t):
            ec.fence(q, ["t", "s"] + (["v"] if tc == 2 else []))
            em("q", q.dma_start(
                out=xa_t[:],
                in_=io["xaT"][:, tc * 512:(tc + 1) * 512].rearrange(
                    "(c p) t -> p c t", p=128)), dma=True)
        block.sync(q_load)

        def s_sq(s, xa_t=xa_t, sq_t=sq_t):
            ec.fence(s, ["q", "t"])
            em("s", s.activation(out=sq_t[:], in_=xa_t[:], func=AF.Square))
        block.scalar(s_sq)

        def t_mm(t, xa_t=xa_t, sq_t=sq_t, pA=pA, pB=pB):
            ec.fence(t, ["q", "s", "v"])
            for ci in range(8):
                em("t", t.matmul(pA[:], lhsT=lhsA[:, 9 * ci:9 * ci + 9],
                                 rhs=xa_t[:, ci, :], start=(ci == 0), stop=(ci == 7)))
            for ci in range(8):
                em("t", t.matmul(pB[:], lhsT=ones_c[:], rhs=sq_t[:, ci, :],
                                 start=(ci == 0), stop=(ci == 7)))
        block.tensor(t_mm)

        def v_stat(v, pA=pA, pB=pB):
            ec.fence(v, ["t", "g"])
            em("v", v.tensor_copy(out=t9[:], in_=pA[:]))
        block.vector(v_stat)

        def g_mrow(g):
            ec.fence(g, ["v"])
            if "mrow" in P0_SKIP:
                em("g", g.memset(m_row[:], 0.001))
            else:
                em("g", g.dma_start(out=io["mrow_d"], in_=t9[8:9, :]), dma=True)
                ec.fence(g, ["g"])
                em("g", g.dma_start(out=m_row[:], in_=io["mrow_d"]), dma=True)
        block.gpsimd(g_mrow)

        def v_stat2(v, pB=pB):
            ec.fence(v, ["g"])
            em("v", v.tensor_scalar_mul(out=m_row[:], in0=m_row[:], scalar1=1.0 / D))
            ec.selfsync("v", v)
            em("v", v.tensor_mul(out=v_row[:], in0=m_row[:], in1=m_row[:]))
            ec.selfsync("v", v)
            em("v", v.scalar_tensor_tensor(out=v_row[:], in0=pB[:], scalar=1.0 / D,
                                           in1=v_row[:], op0=ALU.mult, op1=ALU.subtract))
        block.vector(v_stat2)

        def v_eps(v):
            em("v", v.tensor_scalar_add(out=v_row[:], in0=v_row[:], scalar1=1e-5))
        block.vector(v_eps)

        def s_rstd(s):
            ec.fence(s, ["v"])
            em("s", s.activation(out=rv_row[:], in_=v_row[:], func=AF.Sqrt))
        block.scalar(s_rstd)

        def v_rv(v):
            ec.fence(v, ["s"])
            em("v", v.reciprocal(out=rv_row[:], in_=rv_row[:]))
        block.vector(v_rv)

        def t_b8(t):
            ec.fence(t, ["v"])
            em("t", t.matmul(psb[0:8, :], lhsT=ones_r[0:1, 0:8], rhs=m_row[:],
                             start=True, stop=True))
            em("t", t.matmul(psb2[:], lhsT=ones_r[0:1, 0:8], rhs=rv_row[:],
                             start=True, stop=True))
        block.tensor(t_b8)

        def v_lg(v):
            ec.fence(v, ["t"])
            em("v", v.tensor_copy(out=mb8[:], in_=psb[0:8, :]))
            ec.selfsync("v", v)
            em("v", v.tensor_scalar_mul(out=mb8[:], in0=mb8[:], scalar1=wgs[0:8, :]))
            ec.selfsync("v", v)
            em("v", v.tensor_sub(out=t9[0:8, :], in0=t9[0:8, :], in1=mb8[:]))
            ec.selfsync("v", v)
            em("v", v.tensor_mul(out=t9[0:8, :], in0=t9[0:8, :], in1=psb2[:]))
        block.vector(v_lg)

        if tc == P0_DBG_TC:
            def g_dbg(g):
                ec.fence(g, ["v"])
                em("g", g.dma_start(out=io["dbg2"], in_=t9[:]), dma=True)
            block.gpsimd(g_dbg)

        if tc == 0:
            def t_x8(t):
                ec.fence(t, ["v"])
                em("t", t.matmul(psT[:, 0, :], lhsT=ones_r[:], rhs=m_row[0:1, 0:8],
                                 start=True, stop=True))
            block.tensor(t_x8)

            def v_x8(v):
                ec.fence(v, ["t"])
                em("v", v.tensor_copy(out=mb128[:], in_=psT[:, 0, :]))
            block.vector(v_x8)

            def t_x8b(t):
                ec.fence(t, ["v"])
                em("t", t.matmul(psT[:, 0, :], lhsT=ones_r[:], rhs=rv_row[0:1, 0:8],
                                 start=True, stop=True))
            block.tensor(t_x8b)

            def v_x8b(v, xa_t=xa_t):
                ec.fence(v, ["t"])
                em("v", v.tensor_copy(out=rvb128[:], in_=psT[:, 0, :]))
                ec.selfsync("v", v)
                for ci in range(8):
                    em("v", v.tensor_sub(out=xn8[:, ci, :], in0=xa_t[:, ci, 0:8],
                                         in1=mb128[:]))
                ec.selfsync("v", v)
                for ci in range(8):
                    em("v", v.tensor_mul(out=xn8[:, ci, :], in0=xn8[:, ci, :],
                                         in1=rvb128[:]))
            block.vector(v_x8b)

            def g_x8(g):
                ec.fence(g, ["v"])
                if "xn8" not in P0_SKIP:
                    em("g", g.dma_start(out=io["xn8T_d"].rearrange("(c p) e -> p c e", p=128),
                                        in_=xn8[:]), dma=True)
                else:
                    em("g", g.memset(xn8[:, 0, :], 0.0))
            block.gpsimd(g_x8)

        def t_tr(t):
            ec.fence(t, ["v"])
            for scn in range(4):
                em("t", t.transpose(psT[:, scn, :], in_=t9[0:8, scn * 128:(scn + 1) * 128],
                                    identity=identf[0:8, 0:8]))
        block.tensor(t_tr)

        def v_route(v, tc=tc):
            ec.fence(v, ["t"])
            em("v", v.tensor_copy(out=lt[:], in_=psT[:]))
            ec.selfsync("v", v)
            em("v", v.reduce_max(out=red[:], in_=lt[:], axis=AX.X))
            ec.selfsync("v", v)
            em("v", v.tensor_sub(out=sp[:], in0=lt[:], in1=redb(8)))
        block.vector(v_route)

        def s_exp(s):
            ec.fence(s, ["v"])
            em("s", s.activation(out=sp[:], in_=sp[:], func=AF.Exp))
        block.scalar(s_exp)

        def v_route2(v, tc=tc):
            ec.fence(v, ["s"])
            def Y():
                ec.selfsync("v", v)
            em("v", v.reduce_sum(out=red[:], in_=sp[:], axis=AX.X)); Y()
            em("v", v.reciprocal(out=red[:], in_=red[:])); Y()
            em("v", v.tensor_mul(out=scores[:], in0=sp[:], in1=redb(8))); Y()
            em("v", v.tensor_add(out=sp[:], in0=scores[:],
                                 in1=bias_b[:].rearrange("p (o e) -> p o e", o=1).broadcast_to([128, 4, 8]))); Y()
            em("v", v.scalar_tensor_tensor(out=sp[:], in0=iota8[:].rearrange("p (o e) -> p o e", o=1).broadcast_to([128, 4, 8]),
                                           scalar=-TIE, in1=sp[:], op0=ALU.mult, op1=ALU.add)); Y()
            spv = sp[:].rearrange("p s (g two) -> p s g two", two=2)
            em("v", v.tensor_add(out=gs[:], in0=spv[:, :, :, 0], in1=spv[:, :, :, 1])); Y()
            em("v", v.reduce_max(out=red[:], in_=gs[:], axis=AX.X)); Y()
            em("v", v.tensor_tensor(out=gs2[:], in0=gs[:], in1=redb(4), op=ALU.is_ge)); Y()
            em("v", v.scalar_tensor_tensor(out=gs[:], in0=gs2[:], scalar=-1e4,
                                           in1=gs[:], op0=ALU.mult, op1=ALU.add)); Y()
            em("v", v.reduce_max(out=red[:], in_=gs[:], axis=AX.X)); Y()
            em("v", v.tensor_tensor(out=gs[:], in0=gs[:], in1=redb(4), op=ALU.is_ge)); Y()
            em("v", v.tensor_add(out=gs[:], in0=gs[:], in1=gs2[:])); Y()
            kv = keep[:].rearrange("p s (g two) -> p s g two", two=2)
            em("v", v.tensor_copy(out=kv[:, :, :, 0], in_=gs[:]))
            em("v", v.tensor_copy(out=kv[:, :, :, 1], in_=gs[:])); Y()
            em("v", v.tensor_scalar_sub(out=tmp8[:], in0=keep[:], scalar1=1.0)); Y()
            em("v", v.scalar_tensor_tensor(out=tmp8[:], in0=tmp8[:], scalar=1e4,
                                           in1=sp[:], op0=ALU.mult, op1=ALU.add)); Y()
            em("v", v.reduce_max(out=red[:], in_=tmp8[:], axis=AX.X)); Y()
            em("v", v.tensor_tensor(out=k1[:], in0=tmp8[:], in1=redb(8), op=ALU.is_ge)); Y()
            em("v", v.scalar_tensor_tensor(out=tmp8[:], in0=k1[:], scalar=-2e4,
                                           in1=tmp8[:], op0=ALU.mult, op1=ALU.add)); Y()
            em("v", v.reduce_max(out=red[:], in_=tmp8[:], axis=AX.X)); Y()
            em("v", v.tensor_tensor(out=k2[:], in0=tmp8[:], in1=redb(8), op=ALU.is_ge)); Y()
            iob = iota8[:].rearrange("p (o e) -> p o e", o=1).broadcast_to([128, 4, 8])
            em("v", v.tensor_mul(out=tmp8[:], in0=k1[:], in1=iob)); Y()
            em("v", v.reduce_sum(out=jstage[:, 0 * 256 + tc * 4: 0 * 256 + tc * 4 + 4],
                                 in_=tmp8[:], axis=AX.X)); Y()
            em("v", v.tensor_mul(out=tmp8[:], in0=k2[:], in1=iob)); Y()
            em("v", v.reduce_sum(out=jstage[:, 256 + tc * 4: 256 + tc * 4 + 4],
                                 in_=tmp8[:], axis=AX.X)); Y()
            em("v", v.tensor_mul(out=tmp8[:], in0=k1[:], in1=scores[:])); Y()
            em("v", v.reduce_sum(out=wstage[:, 0 * 256 + tc * 4: 0 * 256 + tc * 4 + 4],
                                 in_=tmp8[:], axis=AX.X)); Y()
            em("v", v.tensor_mul(out=tmp8[:], in0=k2[:], in1=scores[:])); Y()
            em("v", v.reduce_sum(out=wstage[:, 256 + tc * 4: 256 + tc * 4 + 4],
                                 in_=tmp8[:], axis=AX.X)); Y()
            em("v", v.tensor_add(out=tmp8[:], in0=k1[:], in1=k2[:])); Y()
            for scn in range(4):
                em("v", v.tensor_add(out=cnt[:], in0=cnt[:], in1=tmp8[:, scn, :])); Y()
        block.vector(v_route2)

    def t_cnt(t):
        ec.fence(t, ["v"])
        em("t", t.matmul(psb[0:1, 0:8], lhsT=ones_c[:], rhs=cnt[:],
                         start=True, stop=True))
    block.tensor(t_cnt)

    def v_offs(v):
        ec.fence(v, ["t"])
        em("v", v.tensor_copy(out=cnt_row[:], in_=psb[0:1, 0:8]))
        ec.selfsync("v", v)
        em("v", v.tensor_tensor_scan(out=offs_r[:], data0=cnt_row[:], data1=zero8[:],
                                     initial=0.0, op0=ALU.add, op1=ALU.add))
        ec.selfsync("v", v)
    block.vector(v_offs)

    def g_fin(g):
        ec.fence(g, ["v"])
        if P0_DBG_TC >= 0:
            em("g", g.dma_start(out=io["dbg"], in_=dbgt[:]), dma=True)
        em("g", g.dma_start(out=io["offs_d"], in_=offs_r[:]), dma=True)
        em("g", g.dma_start(out=io["jslot"], in_=jstage[:]), dma=True)
        em("g", g.dma_start(out=io["wslot"], in_=wstage[:]), dma=True)
    block.gpsimd(g_fin)



def build_p1a(nc, ec, block, io, st):
    S = lambda name, shape, dt=F32: st.enter_context(nc.sbuf_tensor(name, shape, dt))
    P = lambda name, shape: st.enter_context(nc.psum_tensor(name, shape, F32))
    em = ec.em
    NG = 8

    xg = S("a_xg", [128, 8, 512])
    BS = S("a_bs", [128, 8, 512])
    acc = S("a_acc", [128, 8, 512])
    xnb = S("a_xnb", [128, 8, 512], BF16)
    h = S("a_h", [128, 8, 515], BF16)
    xc = S("a_xc", [128, 8, 512], BF16)
    qkb = S("a_qkb", [128, 8, 512], BF16)
    bsq = S("a_bsq", [128, 8, 512], BF16)
    kb = S("a_kb", [128, 8, 513], BF16)
    vb = S("a_vb", [128, 8, 512], BF16)
    gb = S("a_gb", [128, 8, 512], BF16)
    outm = S("a_outm", [128, 8, 512], BF16)
    aob = vb
    h2b = gb
    A = S("a_A", [128, 512]); B = S("a_B", [128, 512]); C = S("a_C", [128, 512])
    Dd = S("a_D", [128, 512]); E = S("a_E", [128, 512]); F = S("a_F", [128, 512])
    G = S("a_G", [128, 512])
    zeros512 = S("a_z512", [128, 512])
    tio = S("a_tio", [128, 512])
    wld = [S("a_wld0", [128, 8, 512]), S("a_wld1", [128, 8, 512])]
    wbf = [S("a_wbf0", [128, 8, 512], BF16), S("a_wbf1", [128, 8, 512], BF16)]
    mb = S("a_mb", [128, 512]); rvb = S("a_rvb", [128, 512])
    m_row = S("a_mrow", [1, 512]); v_row = S("a_vrow", [1, 512]); rv_row = S("a_rvrow", [1, 512])
    ones_c = S("a_onesc", [128, 1]); ones_bf = S("a_onesbf", [128, 1], BF16)
    ones_r = S("a_onesr", [1, 128])
    ns = S("a_ns", [16, 512]); rnb16 = S("a_rnb", [16, 512], BF16)
    carry_h = S("a_carryh", [128, 8, 3], BF16)
    carry_k = S("a_carryk", [128, 8, 1], BF16)
    carry_s = S("a_carrys", [128, 8])
    par = {}
    for nm in ("ln1_w", "ln1_b", "conv_b", "g_b", "onorm_w", "onorm_b",
               "ln2_w", "ln2_b", "gamma1", "gammaA", "freqs"):
        par[nm] = S("a_" + nm, [128, 8])
    conv_w = S("a_convw", [128, 8, 4])
    hsel = S("a_hsel", [128, 8, 16], BF16)
    hexp = S("a_hexp", [16, 8, 128], BF16)
    hself = S("a_hself", [128, 8, 16])
    hexpf = S("a_hexpf", [16, 8, 128])
    lng = S("a_lng", [128, 8])
    g1w = S("a_g1w", [128, 8]); g1b = S("a_g1b", [128, 8])
    t8a = S("a_t8a", [128, 8]); t8b = S("a_t8b", [128, 8]); t8c = S("a_t8c", [128, 8])

    pms = [P("a_pm0", [1, 512]), P("a_pm1", [1, 512])]
    pbc = [P("a_pbc0", [128, 512]), P("a_pbc1", [128, 512])]
    pmm = [P("a_pmm0", [128, 512]), P("a_pmm1", [128, 512])]
    pn16 = P("a_pn16", [16, 512])

    def Y(v):
        ec.selfsync("v", v)

    def q_setup(q):
        for nm in par:
            em("q", q.dma_start(out=par[nm][:], in_=io[nm]), dma=True)
        em("q", q.dma_start(out=conv_w[:], in_=io["conv_w"]), dma=True)
        em("q", q.dma_start(out=hself[:], in_=io["headsel"]), dma=True)
        em("q", q.dma_start(out=hexpf[:], in_=io["headexp"]), dma=True)
    block.sync(q_setup)

    def g_setup(g):
        em("g", g.memset(ones_c[:], 1.0))
        em("g", g.memset(ones_bf[:], 1.0))
        em("g", g.memset(ones_r[:], 1.0))
        em("g", g.memset(zeros512[:], 0.0))
        em("g", g.memset(carry_h[:], 0.0))
        em("g", g.memset(carry_k[:], 0.0))
        em("g", g.memset(carry_s[:], 0.0))
        ec.fence(g, ["q"])
        em("g", g.tensor_copy(out=hsel[:], in_=hself[:]))
        em("g", g.tensor_copy(out=hexp[:], in_=hexpf[:]))
    block.gpsimd(g_setup)

    def v_lng(v):
        ec.fence(v, ["q", "g"])
        em("v", v.tensor_scalar_sub(out=t8a[:], in0=par["gammaA"][:], scalar1=1.0)); Y(v)
        em("v", v.tensor_scalar_add(out=t8b[:], in0=par["gammaA"][:], scalar1=1.0)); Y(v)
        em("v", v.reciprocal(out=t8b[:], in_=t8b[:])); Y(v)
        em("v", v.tensor_mul(out=t8b[:], in0=t8a[:], in1=t8b[:])); Y(v)
        em("v", v.tensor_mul(out=t8c[:], in0=t8b[:], in1=t8b[:])); Y(v)
        em("v", v.tensor_scalar(out=lng[:], in0=t8c[:], scalar1=1.0 / 9.0, scalar2=1.0 / 7.0,
                                op0=ALU.mult, op1=ALU.add)); Y(v)
        em("v", v.scalar_tensor_tensor(out=lng[:], in0=lng[:], scalar=1.0, in1=t8c[:],
                                       op0=ALU.mult, op1=ALU.mult)); Y(v)
        em("v", v.tensor_scalar_add(out=lng[:], in0=lng[:], scalar1=1.0 / 5.0)); Y(v)
        em("v", v.tensor_mul(out=lng[:], in0=lng[:], in1=t8c[:])); Y(v)
        em("v", v.tensor_scalar_add(out=lng[:], in0=lng[:], scalar1=1.0 / 3.0)); Y(v)
        em("v", v.tensor_mul(out=lng[:], in0=lng[:], in1=t8c[:])); Y(v)
        em("v", v.tensor_scalar_add(out=lng[:], in0=lng[:], scalar1=1.0)); Y(v)
        em("v", v.tensor_mul(out=lng[:], in0=lng[:], in1=t8b[:])); Y(v)
        em("v", v.tensor_scalar_mul(out=lng[:], in0=lng[:], scalar1=2.0)); Y(v)
    block.vector(v_lng)

    def s_lng(s):
        ec.fence(s, ["v"])
        em("s", s.activation(out=t8b[:], in_=par["gammaA"][:], func=AF.Ln))
    block.scalar(s_lng)

    def v_lng2(v):
        ec.fence(v, ["s"])
        em("v", v.tensor_tensor(out=t8a[:], in0=t8a[:], in1=t8a[:], op=ALU.mult)); Y(v)
        em("v", v.tensor_scalar(out=t8a[:], in0=t8a[:], scalar1=0.16, scalar2=None,
                                op0=ALU.is_gt)); Y(v)
        em("v", v.tensor_mul(out=t8b[:], in0=t8b[:], in1=t8a[:])); Y(v)
        em("v", v.tensor_scalar(out=t8a[:], in0=t8a[:], scalar1=-1.0, scalar2=1.0,
                                op0=ALU.mult, op1=ALU.add)); Y(v)
        em("v", v.tensor_mul(out=lng[:], in0=lng[:], in1=t8a[:])); Y(v)
        em("v", v.tensor_add(out=lng[:], in0=lng[:], in1=t8b[:])); Y(v)
        em("v", v.tensor_mul(out=g1w[:], in0=par["gamma1"][:], in1=par["onorm_w"][:]))
        em("v", v.tensor_mul(out=g1b[:], in0=par["gamma1"][:], in1=par["onorm_b"][:])); Y(v)
    block.vector(v_lng2)

    for g_i in range(NG):
        t0 = g_i * 512

        def q_xg(q, t0=t0):
            ec.fence(q, ["v", "s"])
            em("q", q.dma_start(out=xg[:], in_=io["xT"][:, t0:t0 + 512].rearrange(
                "(c p) t -> p c t", p=128)), dma=True)
        block.sync(q_xg)

        def s_sq(s):
            ec.fence(s, ["q", "t"])
            em("s", s.activation(out=BS[:], in_=xg[:], func=AF.Square))
        block.scalar(s_sq)

        def t_stat(t):
            ec.fence(t, ["s", "v"])
            for ci in range(8):
                em("t", t.matmul(pms[0][:], lhsT=ones_c[:], rhs=xg[:, ci, :],
                                 start=(ci == 0), stop=(ci == 7)))
            for ci in range(8):
                em("t", t.matmul(pms[1][:], lhsT=ones_c[:], rhs=BS[:, ci, :],
                                 start=(ci == 0), stop=(ci == 7)))
        block.tensor(t_stat)

        def v_stat(v):
            ec.fence(v, ["t"])
            em("v", v.tensor_scalar_mul(out=m_row[:], in0=pms[0][:], scalar1=1.0 / D)); Y(v)
            em("v", v.tensor_mul(out=v_row[:], in0=m_row[:], in1=m_row[:])); Y(v)
            em("v", v.scalar_tensor_tensor(out=v_row[:], in0=pms[1][:], scalar=1.0 / D,
                                           in1=v_row[:], op0=ALU.mult, op1=ALU.subtract)); Y(v)
            em("v", v.tensor_scalar_add(out=v_row[:], in0=v_row[:], scalar1=1e-5))
        block.vector(v_stat)

        def s_rstd(s):
            ec.fence(s, ["v"])
            em("s", s.activation(out=rv_row[:], in_=v_row[:], func=AF.Sqrt))
        block.scalar(s_rstd)

        def v_rv(v):
            ec.fence(v, ["s"])
            em("v", v.reciprocal(out=rv_row[:], in_=rv_row[:]))
        block.vector(v_rv)

        def t_bc(t):
            ec.fence(t, ["v"])
            em("t", t.matmul(pbc[0][:], lhsT=ones_r[:], rhs=m_row[:], start=True, stop=True))
            em("t", t.matmul(pbc[1][:], lhsT=ones_r[:], rhs=rv_row[:], start=True, stop=True))
        block.tensor(t_bc)

        def v_xn(v):
            ec.fence(v, ["t"])
            em("v", v.tensor_copy(out=mb[:], in_=pbc[0][:]))
            em("v", v.tensor_copy(out=rvb[:], in_=pbc[1][:])); Y(v)
            for ci in range(8):
                em("v", v.tensor_sub(out=BS[:, ci, :], in0=xg[:, ci, :], in1=mb[:]))
            Y(v)
            for ci in range(8):
                em("v", v.tensor_mul(out=BS[:, ci, :], in0=BS[:, ci, :], in1=rvb[:]))
            Y(v)
            for ci in range(8):
                em("v", v.tensor_copy(out=xnb[:, ci, :], in_=BS[:, ci, :]))
            for ci in range(8):
                em("v", v.tensor_add(out=acc[:, ci, :], in0=xg[:, ci, :], in1=BS[:, ci, :]))
            Y(v)
        block.vector(v_xn)

        def g_xnb(g, t0=t0):
            ec.fence(g, ["v"])
            em("g", g.dma_start(out=io["xnb_d"][:, t0:t0 + 512].rearrange(
                "(c p) t -> p c t", p=128), in_=xnb[:]), dma=True)
        block.gpsimd(g_xnb)

        def ln_pass(src_bf, dst, wname, bname, halo, extra_scale=None, extra_bias=None,
                    dst_off=0):
            def s_sq2(s):
                ec.fence(s, ["v", "t"])
                em("s", s.activation(out=bsq[:], in_=src_bf[:], func=AF.Square))
            block.scalar(s_sq2)

            def t_st(t):
                ec.fence(t, ["s", "v"])
                for ci in range(8):
                    em("t", t.matmul(pms[0][:], lhsT=ones_bf[:], rhs=src_bf[:, ci, :],
                                     start=(ci == 0), stop=(ci == 7)))
                for ci in range(8):
                    em("t", t.matmul(pms[1][:], lhsT=ones_bf[:], rhs=bsq[:, ci, :],
                                     start=(ci == 0), stop=(ci == 7)))
            block.tensor(t_st)

            def v_st(v):
                ec.fence(v, ["t"])
                em("v", v.tensor_scalar_mul(out=m_row[:], in0=pms[0][:], scalar1=1.0 / D)); Y(v)
                em("v", v.tensor_mul(out=v_row[:], in0=m_row[:], in1=m_row[:])); Y(v)
                em("v", v.scalar_tensor_tensor(out=v_row[:], in0=pms[1][:], scalar=1.0 / D,
                                               in1=v_row[:], op0=ALU.mult, op1=ALU.subtract)); Y(v)
                em("v", v.tensor_scalar_add(out=v_row[:], in0=v_row[:], scalar1=1e-5))
            block.vector(v_st)

            def s_r(s):
                ec.fence(s, ["v"])
                em("s", s.activation(out=rv_row[:], in_=v_row[:], func=AF.Sqrt))
            block.scalar(s_r)

            def v_r(v):
                ec.fence(v, ["s"])
                em("v", v.reciprocal(out=rv_row[:], in_=rv_row[:]))
            block.vector(v_r)

            def t_b(t):
                ec.fence(t, ["v"])
                em("t", t.matmul(pbc[0][:], lhsT=ones_r[:], rhs=m_row[:], start=True, stop=True))
                em("t", t.matmul(pbc[1][:], lhsT=ones_r[:], rhs=rv_row[:], start=True, stop=True))
            block.tensor(t_b)

            def v_a(v):
                ec.fence(v, ["t"])
                em("v", v.tensor_copy(out=mb[:], in_=pbc[0][:]))
                em("v", v.tensor_copy(out=rvb[:], in_=pbc[1][:])); Y(v)
                for ci in range(8):
                    em("v", v.tensor_sub(out=BS[:, ci, :], in0=src_bf[:, ci, :], in1=mb[:]))
                Y(v)
                for ci in range(8):
                    em("v", v.tensor_mul(out=BS[:, ci, :], in0=BS[:, ci, :], in1=rvb[:]))
                Y(v)
                wA = par[wname] if extra_scale is None else extra_scale
                bA = par[bname] if extra_bias is None else extra_bias
                for ci in range(8):
                    em("v", v.tensor_scalar(out=dst[:, ci, dst_off:dst_off + 512],
                                            in0=BS[:, ci, :],
                                            scalar1=wA[:, ci:ci + 1], scalar2=bA[:, ci:ci + 1],
                                            op0=ALU.mult, op1=ALU.add))
                Y(v)
            block.vector(v_a)

        ln_pass(xnb, h, "ln1_w", "ln1_b", halo=True, dst_off=3)

        def v_conv(v, g_i=g_i):
            for ci in range(8):
                em("v", v.tensor_copy(out=h[:, ci, 0:3], in_=carry_h[:, ci, :]))
            Y(v)
            for ci in range(8):
                em("v", v.tensor_copy(out=carry_h[:, ci, :], in_=h[:, ci, 512:515]))
            Y(v)
            for ci in range(8):
                em("v", v.tensor_scalar_mul(out=BS[:, ci, :], in0=h[:, ci, 0:512],
                                            scalar1=conv_w[:, ci, 0:1]))
            Y(v)
            for i in range(1, 4):
                for ci in range(8):
                    em("v", v.scalar_tensor_tensor(out=BS[:, ci, :], in0=h[:, ci, i:i + 512],
                                                   scalar=conv_w[:, ci, i:i + 1],
                                                   in1=BS[:, ci, :], op0=ALU.mult, op1=ALU.add))
                Y(v)
            for ci in range(8):
                em("v", v.tensor_scalar(out=xc[:, ci, :], in0=BS[:, ci, :],
                                        scalar1=par["conv_b"][:, ci:ci + 1], scalar2=None,
                                        op0=ALU.add))
            Y(v)
        block.vector(v_conv)

        def proj(wkey, out_tile, kind, g_i=g_i):
            for ob in range(2):
                wb = wld[ob % 2]; wf = wbf[ob % 2]

                def q_w(q, wkey=wkey, ob=ob, wb=wb):
                    ec.fence(q, ["g"])
                    em("q", q.dma_start(out=wb[:], in_=io[wkey][:, ob * 512:(ob + 1) * 512].rearrange(
                        "(c p) o -> p c o", p=128)), dma=True)
                block.sync(q_w)

                def g_cast(g, wb=wb, wf=wf):
                    ec.fence(g, ["q", "t"])
                    em("g", g.tensor_copy(out=wf[:], in_=wb[:]))
                block.gpsimd(g_cast)

                for oc in range(4):
                    pp = pmm[oc % 2]

                    def t_p(t, wf=wf, oc=oc, pp=pp):
                        ec.fence(t, ["g", "v", "s"])
                        for ci in range(8):
                            em("t", t.matmul(pp[:], lhsT=wf[:, ci, oc * 128:(oc + 1) * 128],
                                             rhs=xc[:, ci, :], start=(ci == 0), stop=(ci == 7)))
                    block.tensor(t_p)

                    occ = ob * 4 + oc
                    if kind == "qk":
                        def s_e(s, pp=pp, occ=occ):
                            ec.fence(s, ["t"])
                            em("s", s.activation(out=bsq[:, occ, :], in_=pp[:], func=AF.Square))
                        block.scalar(s_e)

                        def v_e(v, pp=pp, occ=occ):
                            ec.fence(v, ["s"])
                            em("v", v.tensor_copy(out=qkb[:, occ, :], in_=pp[:]))
                        block.vector(v_e)
                    elif kind == "v":
                        def v_e(v, pp=pp, occ=occ):
                            ec.fence(v, ["t"])
                            em("v", v.tensor_copy(out=vb[:, occ, :], in_=pp[:]))
                        block.vector(v_e)
                    else:
                        def s_e(s, pp=pp, occ=occ):
                            ec.fence(s, ["t"])
                            em("s", s.activation(out=gb[:, occ, :], in_=pp[:], func=AF.Sigmoid,
                                                 bias=par["g_b"][:, occ:occ + 1]))
                        block.scalar(s_e)

        proj("qk_wT", qkb, "qk")
        proj("v_wT", vb, "v")
        proj("g_wT", gb, "g")

        def t_nrm(t):
            ec.fence(t, ["s", "v"])
            for ci in range(8):
                em("t", t.matmul(pn16[:], lhsT=hsel[:, ci, :], rhs=bsq[:, ci, :],
                                 start=(ci == 0), stop=(ci == 7)))
        block.tensor(t_nrm)

        def s_nrm(s):
            ec.fence(s, ["t"])
            em("s", s.activation(out=ns[:], in_=pn16[:], func=AF.Sqrt))
        block.scalar(s_nrm)

        def v_nrm(v):
            ec.fence(v, ["s"])
            em("v", v.tensor_scalar_max(out=ns[:], in0=ns[:], scalar1=1e-12)); Y(v)
            em("v", v.reciprocal(out=ns[:], in_=ns[:])); Y(v)
            em("v", v.tensor_copy(out=rnb16[:], in_=ns[:])); Y(v)
        block.vector(v_nrm)

        def v_kcar(v):
            for ci in range(8):
                em("v", v.tensor_copy(out=kb[:, ci, 0:1], in_=carry_k[:, ci, :]))
            Y(v)
        block.vector(v_kcar)

        for ci in range(8):
            pp = pmm[ci % 2]

            def t_kb(t, ci=ci, pp=pp):
                ec.fence(t, ["v"])
                em("t", t.matmul(pp[:], lhsT=hexp[:, ci, :], rhs=rnb16[:], start=True, stop=True))
            block.tensor(t_kb)

            def v_kb(v, ci=ci, pp=pp):
                ec.fence(v, ["t"])
                em("v", v.tensor_mul(out=kb[:, ci, 1:513], in0=qkb[:, ci, :], in1=pp[:]))
            block.vector(v_kb)

        def v_kcar2(v):
            Y(v)
            for ci in range(8):
                em("v", v.tensor_copy(out=carry_k[:, ci, :], in_=kb[:, ci, 512:513]))
            Y(v)
        block.vector(v_kcar2)

        def g_tio(g, t0=t0):
            ec.fence(g, ["v"])
            em("g", g.iota(tio[:], pattern=[[1, 512]], base=t0, channel_multiplier=0,
                           allow_small_or_imprecise_dtypes=True))
        block.gpsimd(g_tio)

        for ci in range(8):
            def v_state(v, ci=ci, g_i=g_i):
                ec.fence(v, ["g", "s"])
                em("v", v.tensor_scalar_mul(out=A[:], in0=tio[:], scalar1=par["freqs"][:, ci:ci + 1])); Y(v)
                em("v", v.tensor_scalar(out=B[:], in0=A[:], scalar1=INV_2PI, scalar2=MAGIC,
                                        op0=ALU.mult, op1=ALU.add)); Y(v)
                em("v", v.tensor_scalar_sub(out=B[:], in0=B[:], scalar1=MAGIC)); Y(v)
                em("v", v.scalar_tensor_tensor(out=A[:], in0=B[:], scalar=-CW1, in1=A[:],
                                               op0=ALU.mult, op1=ALU.add)); Y(v)
                em("v", v.scalar_tensor_tensor(out=A[:], in0=B[:], scalar=-CW2, in1=A[:],
                                               op0=ALU.mult, op1=ALU.add)); Y(v)
                em("v", v.scalar_tensor_tensor(out=A[:], in0=B[:], scalar=-CW3, in1=A[:],
                                               op0=ALU.mult, op1=ALU.add)); Y(v)
                em("v", v.tensor_scalar_add(out=A[:], in0=A[:], scalar1=HALF_PI)); Y(v)
                em("v", v.tensor_scalar(out=B[:], in0=A[:], scalar1=PI_F, scalar2=-TWO_PI,
                                        op0=ALU.is_gt, op1=ALU.mult)); Y(v)
                em("v", v.tensor_add(out=A[:], in0=A[:], in1=B[:])); Y(v)
                em("v", v.tensor_scalar_mul(out=Dd[:], in0=tio[:], scalar1=lng[:, ci:ci + 1])); Y(v)
            block.vector(v_state)

            def s_cd(s, ci=ci):
                ec.fence(s, ["v"])
                em("s", s.activation(out=C[:], in_=A[:], func=AF.Sin))
                em("s", s.activation(out=Dd[:], in_=Dd[:], func=AF.Exp))
            block.scalar(s_cd)

            def v_z(v, ci=ci):
                ec.fence(v, ["s"])
                em("v", v.tensor_mul(out=E[:], in0=vb[:, ci, :], in1=C[:])); Y(v)
                em("v", v.tensor_mul(out=F[:], in0=vb[:, ci, :], in1=kb[:, ci, 0:512])); Y(v)
                em("v", v.tensor_add(out=E[:], in0=E[:], in1=F[:])); Y(v)
                em("v", v.tensor_mul(out=E[:], in0=E[:], in1=gb[:, ci, :])); Y(v)
                em("v", v.tensor_mul(out=E[:], in0=E[:], in1=Dd[:])); Y(v)
                em("v", v.tensor_tensor_scan(out=G[:], data0=E[:], data1=zeros512[:],
                                             initial=carry_s[:, ci:ci + 1],
                                             op0=ALU.add, op1=ALU.add)); Y(v)
                em("v", v.tensor_copy(out=carry_s[:, ci:ci + 1], in_=G[:, 511:512])); Y(v)
                em("v", v.tensor_scalar_add(out=B[:], in0=Dd[:], scalar1=1e-6)); Y(v)
                em("v", v.reciprocal(out=B[:], in_=B[:])); Y(v)
                em("v", v.tensor_mul(out=G[:], in0=G[:], in1=B[:])); Y(v)
                em("v", v.tensor_mul(out=outm[:, ci, :], in0=G[:], in1=kb[:, ci, 1:513])); Y(v)
            block.vector(v_z)

        for ob in range(2):
            wb = wld[ob % 2]; wf = wbf[ob % 2]

            def q_w(q, ob=ob, wb=wb):
                ec.fence(q, ["g"])
                em("q", q.dma_start(out=wb[:], in_=io["o_wT"][:, ob * 512:(ob + 1) * 512].rearrange(
                    "(c p) o -> p c o", p=128)), dma=True)
            block.sync(q_w)

            def g_cast(g, wb=wb, wf=wf):
                ec.fence(g, ["q", "t"])
                em("g", g.tensor_copy(out=wf[:], in_=wb[:]))
            block.gpsimd(g_cast)

            for oc in range(4):
                pp = pmm[oc % 2]

                def t_p(t, wf=wf, oc=oc, pp=pp):
                    ec.fence(t, ["g", "v"])
                    for ci in range(8):
                        em("t", t.matmul(pp[:], lhsT=wf[:, ci, oc * 128:(oc + 1) * 128],
                                         rhs=outm[:, ci, :], start=(ci == 0), stop=(ci == 7)))
                block.tensor(t_p)

                occ = ob * 4 + oc

                def v_e(v, pp=pp, occ=occ):
                    ec.fence(v, ["t"])
                    em("v", v.tensor_copy(out=aob[:, occ, :], in_=pp[:]))
                block.vector(v_e)

        def s_sq3(s):
            ec.fence(s, ["v"])
            em("s", s.activation(out=bsq[:], in_=aob[:], func=AF.Square))
        block.scalar(s_sq3)

        def t_st3(t):
            ec.fence(t, ["s", "v"])
            for ci in range(8):
                em("t", t.matmul(pms[0][:], lhsT=ones_bf[:], rhs=aob[:, ci, :],
                                 start=(ci == 0), stop=(ci == 7)))
            for ci in range(8):
                em("t", t.matmul(pms[1][:], lhsT=ones_bf[:], rhs=bsq[:, ci, :],
                                 start=(ci == 0), stop=(ci == 7)))
        block.tensor(t_st3)

        def v_st3(v):
            ec.fence(v, ["t"])
            em("v", v.tensor_scalar_mul(out=m_row[:], in0=pms[0][:], scalar1=1.0 / D)); Y(v)
            em("v", v.tensor_mul(out=v_row[:], in0=m_row[:], in1=m_row[:])); Y(v)
            em("v", v.scalar_tensor_tensor(out=v_row[:], in0=pms[1][:], scalar=1.0 / D,
                                           in1=v_row[:], op0=ALU.mult, op1=ALU.subtract)); Y(v)
            em("v", v.tensor_scalar_add(out=v_row[:], in0=v_row[:], scalar1=1e-5))
        block.vector(v_st3)

        def s_r3(s):
            ec.fence(s, ["v"])
            em("s", s.activation(out=rv_row[:], in_=v_row[:], func=AF.Sqrt))
        block.scalar(s_r3)

        def v_r3(v):
            ec.fence(v, ["s"])
            em("v", v.reciprocal(out=rv_row[:], in_=rv_row[:]))
        block.vector(v_r3)

        def t_b3(t):
            ec.fence(t, ["v"])
            em("t", t.matmul(pbc[0][:], lhsT=ones_r[:], rhs=m_row[:], start=True, stop=True))
            em("t", t.matmul(pbc[1][:], lhsT=ones_r[:], rhs=rv_row[:], start=True, stop=True))
        block.tensor(t_b3)

        def v_a3(v):
            ec.fence(v, ["t"])
            em("v", v.tensor_copy(out=mb[:], in_=pbc[0][:]))
            em("v", v.tensor_copy(out=rvb[:], in_=pbc[1][:])); Y(v)
            for ci in range(8):
                em("v", v.tensor_sub(out=BS[:, ci, :], in0=aob[:, ci, :], in1=mb[:]))
            Y(v)
            for ci in range(8):
                em("v", v.tensor_mul(out=BS[:, ci, :], in0=BS[:, ci, :], in1=rvb[:]))
            Y(v)
            for ci in range(8):
                em("v", v.tensor_scalar(out=BS[:, ci, :], in0=BS[:, ci, :],
                                        scalar1=g1w[:, ci:ci + 1], scalar2=g1b[:, ci:ci + 1],
                                        op0=ALU.mult, op1=ALU.add))
            Y(v)
            for ci in range(8):
                em("v", v.tensor_add(out=acc[:, ci, :], in0=acc[:, ci, :], in1=BS[:, ci, :]))
            Y(v)
            for ci in range(8):
                em("v", v.tensor_copy(out=qkb[:, ci, :], in_=acc[:, ci, :]))
            Y(v)
        block.vector(v_a3)

        ln_pass(qkb, h2b, "ln2_w", "ln2_b", halo=False, dst_off=0)

        def g_store(g, t0=t0):
            ec.fence(g, ["v"])
            em("g", g.dma_start(out=io["h2_d"][:, t0:t0 + 512].rearrange(
                "(c p) t -> p c t", p=128), in_=h2b[:]), dma=True)
            em("g", g.dma_start(out=io["acc1"][:, t0:t0 + 512].rearrange(
                "(c p) t -> p c t", p=128), in_=acc[:]), dma=True)
        block.gpsimd(g_store)


def build_p1b(nc, ec, block, io, st):
    S = lambda name, shape, dt=F32: st.enter_context(nc.sbuf_tensor(name, shape, dt))
    P = lambda name, shape: st.enter_context(nc.psum_tensor(name, shape, F32))
    em = ec.em

    h2t = S("b_h2t", [128, 8, 1024], BF16)
    acc = S("b_acc", [128, 8, 1024])
    m = S("b_m", [128, 32, 1024], BF16)
    wld = [S("b_wld%d" % i, [128, 8, 512]) for i in range(3)]
    wbf = [S("b_wbf%d" % i, [128, 8, 512], BF16) for i in range(3)]
    g2w = S("b_g2w", [128, 8])
    pps = [P("b_pp%d" % i, [128, 512]) for i in range(8)]

    def q_setup(q):
        em("q", q.dma_start(out=g2w[:], in_=io["gamma2"]), dma=True)
    block.sync(q_setup)

    cast_mark = {}; mm_done = {}; evac_mark = {}
    NBUF = 3
    kslot = [0]

    def wblock(ap_w, mm_emit, cast_en):
        b = kslot[0] % NBUF
        wb, wf = wld[b], wbf[b]

        def q_w(q, b=b, ap_w=ap_w):
            ec.wait(q, [cast_mark.get(b, ("v", 0))])
            em("q", q.dma_start(out=wb[:], in_=ap_w), dma=True)
        block.sync(q_w)
        ldm = ec.mark("q")

        if cast_en == "v":
            def v_cast(v, b=b, ldm=ldm):
                ec.wait(v, [ldm, mm_done.get(b, ("t", 0))])
                em("v", v.tensor_copy(out=wf[:], in_=wb[:]))
            block.vector(v_cast)
            cast_mark[b] = ec.mark("v")
        else:
            def s_cast(sx, b=b, ldm=ldm):
                ec.wait(sx, [ldm, mm_done.get(b, ("t", 0))])
                em("s", sx.copy(out=wf[:], in_=wb[:]))
            block.scalar(s_cast)
            cast_mark[b] = ec.mark("s")
        mm_emit(wf, b)
        mm_done[b] = ec.mark("t")
        kslot[0] += 1

    for g_i in range(4):
        t0 = g_i * 1024

        def q_ld(q, t0=t0):
            ec.fence(q, ["v", "t", "g", "s", "q"])
            em("q", q.dma_start(out=h2t[:], in_=io["h2_d"][:, t0:t0 + 1024].rearrange(
                "(c p) t -> p c t", p=128)), dma=True)
            em("q", q.dma_start(out=acc[:], in_=io["acc1"][:, t0:t0 + 1024].rearrange(
                "(c p) t -> p c t", p=128)), dma=True)
        block.sync(q_ld)
        ldx = ec.mark("q")

        silu_marks = {}
        for fb in range(8):
            for which, wkey in ((0, "gate_wT"), (1, "up_wT")):
                ap_w = io[wkey][:, fb * 512:(fb + 1) * 512].rearrange("(c p) o -> p c o", p=128)

                def mm_gateup(wf, b, fb=fb, which=which):
                    for fc in range(4):
                        for t5 in range(2):
                            ppi = which * 2 + (fc * 2 + t5) % 2
                            pp = pps[ppi]

                            def t_p(t, wf=wf, fc=fc, t5=t5, pp=pp, ppi=ppi, b=b):
                                ec.wait(t, [cast_mark[b], ldx, evac_mark.get(ppi, ("v", 0))])
                                for ci in range(8):
                                    em("t", t.matmul(pp[:], lhsT=wf[:, ci, fc * 128:(fc + 1) * 128],
                                                     rhs=h2t[:, ci, t5 * 512:(t5 + 1) * 512],
                                                     start=(ci == 0), stop=(ci == 7)))
                            block.tensor(t_p)
                            gm = ec.mark("t")
                            fcg = fb * 4 + fc
                            if which == 0:
                                def s_silu(s, pp=pp, fcg=fcg, t5=t5, gm=gm):
                                    ec.wait(s, [gm])
                                    em("s", s.activation(out=m[:, fcg, t5 * 512:(t5 + 1) * 512],
                                                         in_=pp[:], func=AF.Silu))
                                block.scalar(s_silu)
                                evac_mark[ppi] = ec.mark("s")
                                silu_marks[(fcg, t5)] = evac_mark[ppi]
                            else:
                                def v_mul(v, pp=pp, fcg=fcg, t5=t5, gm=gm):
                                    ec.wait(v, [gm, silu_marks[(fcg, t5)]])
                                    em("v", v.tensor_mul(out=m[:, fcg, t5 * 512:(t5 + 1) * 512],
                                                         in0=m[:, fcg, t5 * 512:(t5 + 1) * 512],
                                                         in1=pp[:]))
                                block.vector(v_mul)
                                evac_mark[ppi] = ec.mark("v")

                wblock(ap_w, mm_gateup, "s" if which == 0 else "v")

        for db in range(2):
            for sub in range(4):
                ap_w = io["down_wT"][sub * 1024:(sub + 1) * 1024,
                                     db * 512:(db + 1) * 512].rearrange("(c p) o -> p c o", p=128)

                def mm_down(wf, b, db=db, sub=sub):
                    for dc in range(4):
                        for t5 in range(2):
                            ppi = dc * 2 + t5
                            pp = pps[ppi]

                            def t_p(t, wf=wf, dc=dc, t5=t5, pp=pp, ppi=ppi, sub=sub, b=b):
                                w8 = [cast_mark[b]]
                                if sub == 0:
                                    w8.append(evac_mark.get(ppi, ("v", 0)))
                                ec.wait(t, w8)
                                for kf in range(8):
                                    em("t", t.matmul(pp[:], lhsT=wf[:, kf, dc * 128:(dc + 1) * 128],
                                                     rhs=m[:, sub * 8 + kf, t5 * 512:(t5 + 1) * 512],
                                                     start=(kf == 0 and sub == 0),
                                                     stop=(kf == 7 and sub == 3)))
                            block.tensor(t_p)

                            if sub == 3:
                                gm = ec.mark("t")
                                dcg = db * 4 + dc

                                def v_acc(v, pp=pp, dcg=dcg, t5=t5, gm=gm):
                                    ec.wait(v, [gm])
                                    em("v", v.scalar_tensor_tensor(
                                        out=acc[:, dcg, t5 * 512:(t5 + 1) * 512],
                                        in0=pp[:], scalar=g2w[:, dcg:dcg + 1],
                                        in1=acc[:, dcg, t5 * 512:(t5 + 1) * 512],
                                        op0=ALU.mult, op1=ALU.add))
                                block.vector(v_acc)
                                evac_mark[ppi] = ec.mark("v")

                wblock(ap_w, mm_down, "v")

        def g_store(g, t0=t0):
            ec.fence(g, ["v"])
            em("g", g.dma_start(out=io["acc1"][:, t0:t0 + 1024].rearrange(
                "(c p) t -> p c t", p=128), in_=acc[:]), dma=True)
        block.gpsimd(g_store)


def build_p1c(nc, ec, block, io, st):
    S = lambda name, shape, dt=F32: st.enter_context(nc.sbuf_tensor(name, shape, dt))
    P = lambda name, shape: st.enter_context(nc.psum_tensor(name, shape, F32))
    em = ec.em

    xnt = S("c_xnt", [128, 8, 1024], BF16)
    acc = S("c_acc", [128, 8, 1024])
    sh = S("c_sh", [128, 8, 1024], BF16)
    wld = [S("c_wld%d" % i, [128, 8, 512]) for i in range(3)]
    wbf = [S("c_wbf%d" % i, [128, 8, 512], BF16) for i in range(3)]
    pps = [P("c_pp%d" % i, [128, 512]) for i in range(8)]

    cast_mark = {}; mm_done = {}; evac_mark = {}
    NBUF = 3
    kslot = [0]

    def wblock(ap_w, mm_emit, cast_en):
        b = kslot[0] % NBUF
        wb, wf = wld[b], wbf[b]

        def q_w(q, b=b, ap_w=ap_w):
            ec.wait(q, [cast_mark.get(b, ("v", 0))])
            em("q", q.dma_start(out=wb[:], in_=ap_w), dma=True)
        block.sync(q_w)
        ldm = ec.mark("q")

        if cast_en == "v":
            def v_cast(v, b=b, ldm=ldm):
                ec.wait(v, [ldm, mm_done.get(b, ("t", 0))])
                em("v", v.tensor_copy(out=wf[:], in_=wb[:]))
            block.vector(v_cast)
            cast_mark[b] = ec.mark("v")
        else:
            def s_cast(sx, b=b, ldm=ldm):
                ec.wait(sx, [ldm, mm_done.get(b, ("t", 0))])
                em("s", sx.copy(out=wf[:], in_=wb[:]))
            block.scalar(s_cast)
            cast_mark[b] = ec.mark("s")
        mm_emit(wf, b)
        mm_done[b] = ec.mark("t")
        kslot[0] += 1

    for g_i in range(4):
        t0 = g_i * 1024

        def q_ld(q, t0=t0):
            ec.fence(q, ["v", "t", "g", "s", "q"])
            em("q", q.dma_start(out=xnt[:], in_=io["xnb_d"][:, t0:t0 + 1024].rearrange(
                "(c p) t -> p c t", p=128)), dma=True)
            em("q", q.dma_start(out=acc[:], in_=io["acc1"][:, t0:t0 + 1024].rearrange(
                "(c p) t -> p c t", p=128)), dma=True)
        block.sync(q_ld)
        ldx = ec.mark("q")

        silu_marks = {}
        for sb in range(2):
            for which, wkey in ((0, "sw1T"), (1, "sw3T")):
                ap_w = io[wkey][:, sb * 512:(sb + 1) * 512].rearrange("(c p) o -> p c o", p=128)

                def mm_gateup(wf, b, sb=sb, which=which):
                    for oc in range(4):
                        for t5 in range(2):
                            ppi = which * 2 + (oc * 2 + t5) % 2
                            pp = pps[ppi]

                            def t_p(t, wf=wf, oc=oc, t5=t5, pp=pp, ppi=ppi, b=b):
                                ec.wait(t, [cast_mark[b], ldx, evac_mark.get(ppi, ("v", 0))])
                                for ci in range(8):
                                    em("t", t.matmul(pp[:], lhsT=wf[:, ci, oc * 128:(oc + 1) * 128],
                                                     rhs=xnt[:, ci, t5 * 512:(t5 + 1) * 512],
                                                     start=(ci == 0), stop=(ci == 7)))
                            block.tensor(t_p)
                            gm = ec.mark("t")
                            occ = sb * 4 + oc
                            if which == 0:
                                def s_silu(s, pp=pp, occ=occ, t5=t5, gm=gm):
                                    ec.wait(s, [gm])
                                    em("s", s.activation(out=sh[:, occ, t5 * 512:(t5 + 1) * 512],
                                                         in_=pp[:], func=AF.Silu))
                                block.scalar(s_silu)
                                evac_mark[ppi] = ec.mark("s")
                                silu_marks[(occ, t5)] = evac_mark[ppi]
                            else:
                                def v_mul(v, pp=pp, occ=occ, t5=t5, gm=gm):
                                    ec.wait(v, [gm, silu_marks[(occ, t5)]])
                                    em("v", v.tensor_mul(out=sh[:, occ, t5 * 512:(t5 + 1) * 512],
                                                         in0=sh[:, occ, t5 * 512:(t5 + 1) * 512],
                                                         in1=pp[:]))
                                block.vector(v_mul)
                                evac_mark[ppi] = ec.mark("v")

                wblock(ap_w, mm_gateup, "s" if which == 0 else "v")

        for db in range(2):
            ap_w = io["sw2T"][:, db * 512:(db + 1) * 512].rearrange("(c p) o -> p c o", p=128)

            def mm_down(wf, b, db=db):
                for dc in range(4):
                    for t5 in range(2):
                        ppi = dc * 2 + t5
                        pp = pps[ppi]

                        def t_p(t, wf=wf, dc=dc, t5=t5, pp=pp, ppi=ppi, b=b):
                            ec.wait(t, [cast_mark[b], evac_mark.get(ppi, ("v", 0))])
                            for ci in range(8):
                                em("t", t.matmul(pp[:], lhsT=wf[:, ci, dc * 128:(dc + 1) * 128],
                                                 rhs=sh[:, ci, t5 * 512:(t5 + 1) * 512],
                                                 start=(ci == 0), stop=(ci == 7)))
                        block.tensor(t_p)
                        gm = ec.mark("t")
                        dcg = db * 4 + dc

                        def v_acc(v, pp=pp, dcg=dcg, t5=t5, gm=gm):
                            ec.wait(v, [gm])
                            em("v", v.tensor_add(out=acc[:, dcg, t5 * 512:(t5 + 1) * 512],
                                                 in0=acc[:, dcg, t5 * 512:(t5 + 1) * 512],
                                                 in1=pp[:]))
                        block.vector(v_acc)
                        evac_mark[ppi] = ec.mark("v")

            wblock(ap_w, mm_down, "v")

        def g_store(g, t0=t0):
            ec.fence(g, ["v"])
            em("g", g.dma_start(out=io["acc1"][:, t0:t0 + 1024].rearrange(
                "(c p) t -> p c t", p=128), in_=acc[:]), dma=True)
        block.gpsimd(g_store)


def build_p2(nc, ec, block, io, st):
    S = lambda name, shape, dt=F32: st.enter_context(nc.sbuf_tensor(name, shape, dt))
    em = ec.em

    x8b = S("p2_x8b", [128, 8, 8], BF16)
    x8f = S("p2_x8f", [128, 8, 8])
    offsr = S("p2_offsr", [1, 8])
    offsb = S("p2_offsb", [128, 8])
    jall = S("p2_jall", [128, 512])
    wall = S("p2_wall", [128, 512])
    iota64 = S("p2_iota64", [128, 64])
    hs_sb = S("p2_hs", [64, 512])
    ones_r = S("p2_onesr", [1, 128])
    identf = S("p2_id", [128, 128])
    heb = S("p2_heb", [8, 512], BF16)
    hef = S("p2_hef", [8, 512])
    lhs4 = S("p2_lhs4", [128, 4, 8], BF16)
    w2b = S("p2_w2b", [128, 4, 1024], BF16)
    w2l = S("p2_w2l", [128, 4, 1024])
    rr_sb = S("p2_rr", [8, 1024])
    rrT = S("p2_rrT", [128, 8, 8])
    acc8 = S("p2_acc8", [128, 8, 8])
    mask0 = S("p2_mask0", [128, 1])
    identb = S("p2_identb", [128, 128], BF16)
    st_u = contextlib.ExitStack()
    wtl = st_u.enter_context(nc.sbuf_tensor("p2_wtl", [128, 8, 512], F32))
    wtb = st_u.enter_context(nc.sbuf_tensor("p2_wtb", [128, 8, 512], BF16))
    u8 = st_u.enter_context(nc.sbuf_tensor("p2_u8", [8, 512], F32))

    def Y(v):
        ec.selfsync("v", v)

    with contextlib.ExitStack() as st1:
        P1_ = lambda name, shape, dt=F32: st1.enter_context(nc.psum_tensor(name, shape, dt))
        ph = P1_("p2_ph", [64, 512])
        st_pg = contextlib.ExitStack()
        pgu = st_pg.enter_context(nc.psum_tensor("p2_pgu", [128, 512], F32))

        def q_setup(q):
            ec.fence(q, ["g"])
            em("q", q.dma_start(out=x8f[:], in_=io["xn8T_d"].rearrange("(c p) e -> p c e", p=128)), dma=True)
            em("q", q.dma_start(out=offsr[:], in_=io["offs_d"]), dma=True)
            em("q", q.dma_start(out=jall[:], in_=io["jslot"]), dma=True)
            em("q", q.dma_start(out=wall[:], in_=io["wslot"]), dma=True)
            em("q", q.dma_start(out=identf[:], in_=io["identf"]), dma=True)
            em("q", q.dma_start(out=mask0[:], in_=io["mask0"]), dma=True)
        block.sync(q_setup)

        def g_setup(g):
            em("g", g.memset(ones_r[:], 1.0))
            em("g", g.iota(iota64[:], pattern=[[1, 64]], base=0, channel_multiplier=0,
                           allow_small_or_imprecise_dtypes=True))
            ec.fence(g, ["q"])
            em("g", g.tensor_copy(out=x8b[:], in_=x8f[:]))
            em("g", g.tensor_copy(out=identb[:], in_=identf[:]))
        block.gpsimd(g_setup)

        def t_off(t):
            ec.fence(t, ["q", "g"])
            em("t", t.matmul(pgu[:, 0:8], lhsT=ones_r[:], rhs=offsr[:], start=True, stop=True))
        block.tensor(t_off)

        def v_off(v):
            ec.fence(v, ["t"])
            em("v", v.tensor_copy(out=offsb[:], in_=pgu[:, 0:8]))
        block.vector(v_off)

        for tab, wkey in (("u1tab", "w1"), ("u3tab", "w3")):
            for e in range(8):
                def q_w(q, wkey=wkey, e=e):
                    ec.fence(q, ["g"])
                    em("q", q.dma_start(out=wtl[:], in_=io[wkey][e].rearrange(
                        "(c p) f -> p c f", p=128)), dma=True)
                block.sync(q_w)

                def g_c(g):
                    ec.fence(g, ["q", "t"])
                    em("g", g.tensor_copy(out=wtb[:], in_=wtl[:]))
                block.gpsimd(g_c)

                def t_u(t, e=e):
                    ec.fence(t, ["g", "v"])
                    for ci in range(8):
                        em("t", t.matmul(pgu[0:8, :], lhsT=x8b[:, ci, :], rhs=wtb[:, ci, :],
                                         start=(ci == 0), stop=(ci == 7)))
                block.tensor(t_u)

                def v_u(v):
                    ec.fence(v, ["t"])
                    em("v", v.tensor_copy(out=u8[:], in_=pgu[0:8, :]))
                block.vector(v_u)

                def g_st(g, tab=tab, e=e):
                    ec.fence(g, ["v"])
                    em("g", g.dma_start(out=io[tab][e * 8:(e + 1) * 8, :], in_=u8[:]), dma=True)
                block.gpsimd(g_st)

        st_u.close()
        st_pg.close()
        c_all = S("p2_call", [128, 512])
        eid_a = S("p2_eida", [128, 512])
        tmp_a = S("p2_tmpa", [128, 512])
        iall = S("p2_iall", [128, 512])
        u1s = S("p2_u1s", [64, 512], BF16)
        u3s = S("p2_u3s", [64, 512], BF16)
        u1f = S("p2_u1f", [64, 512])
        u3f = S("p2_u3f", [64, 512])
        ohT_all = S("p2_ohTall", [64, 65536], BF16)
        NPAR = 3
        ohst2 = [S("p2_oh%d" % i, [128, 64], BF16) for i in range(NPAR)]
        Asc2 = [S("p2_A%d" % i, [128, 512]) for i in range(NPAR)]
        u3g2 = [S("p2_u3g%d" % i, [128, 512]) for i in range(NPAR)]
        Ssc2 = [S("p2_Ss%d" % i, [128, 512]) for i in range(NPAR)]
        hb2 = [S("p2_hb%d" % i, [128, 512], BF16) for i in range(NPAR)]
        ohpre = [S("p2_ohpre%d" % i, [128, 8, 64], BF16) for i in range(2)]

        def q_uld(q):
            ec.fence(q, ["g"])
            em("q", q.dma_start(out=u1f[:], in_=io["u1tab"]), dma=True)
            em("q", q.dma_start(out=u3f[:], in_=io["u3tab"]), dma=True)
        block.sync(q_uld)
        ulm = ec.mark("q")

        def g_ucast(g):
            ec.wait(g, [ulm])
            em("g", g.tensor_copy(out=u1s[:], in_=u1f[:]))
            em("g", g.tensor_copy(out=u3s[:], in_=u3f[:]))
            em("g", g.iota(iall[:], pattern=[[1, 2], [256, 256]], base=0,
                           channel_multiplier=2,
                           allow_small_or_imprecise_dtypes=True))
        block.gpsimd(g_ucast)

        def v_call(v):
            ec.fence(v, ["g", "q"])
            em("v", v.tensor_scalar(out=eid_a[:], in0=iall[:], scalar1=offsb[:, 0:1],
                                    scalar2=None, op0=ALU.is_ge))
            ec.selfsync("v", v)
            for e in range(1, 8):
                em("v", v.tensor_scalar(out=tmp_a[:], in0=iall[:], scalar1=offsb[:, e:e + 1],
                                        scalar2=None, op0=ALU.is_ge))
                ec.selfsync("v", v)
                em("v", v.tensor_add(out=eid_a[:], in0=eid_a[:], in1=tmp_a[:]))
                ec.selfsync("v", v)
            em("v", v.scalar_tensor_tensor(out=c_all[:], in0=eid_a[:], scalar=8.0,
                                           in1=jall[:], op0=ALU.mult, op1=ALU.add))
            ec.selfsync("v", v)
        block.vector(v_call)

        st_tr = contextlib.ExitStack()
        ptr2 = [st_tr.enter_context(nc.psum_tensor("p2_ptr%d" % i, [64, 8, 128], BF16))
                for i in range(2)]
        pre_mark = {}; prev_mark = {}
        for bt in range(64):
            b2 = bt % 2

            def v_pre(v, bt=bt, b2=b2):
                ec.wait(v, [pre_mark.get(b2, ("t", 0))])
                for i in range(8):
                    sc = bt * 8 + i
                    em("v", v.tensor_scalar(out=ohpre[b2][:, i, :], in0=iota64[:],
                                            scalar1=c_all[:, sc:sc + 1], scalar2=None,
                                            op0=ALU.is_equal))
            block.vector(v_pre)
            vm = ec.mark("v")

            def t_pre(t, b2=b2, vm=vm):
                ec.wait(t, [vm, prev_mark.get(b2, ("v", 0))])
                for i in range(8):
                    em("t", t.transpose(ptr2[b2][:, i, :], in_=ohpre[b2][:, i, :],
                                        identity=identb[:]))
            block.tensor(t_pre)
            pre_mark[b2] = ec.mark("t")

            def v_pre2(v, bt=bt, b2=b2):
                ec.wait(v, [pre_mark[b2]])
                em("v", v.tensor_copy(out=ohT_all[:, bt * 1024:(bt + 1) * 1024],
                                      in_=ptr2[b2][:].rearrange("p a b -> p (a b)")))
            block.vector(v_pre2)
            prev_mark[b2] = ec.mark("v")

        st_tr.close()
        pg1 = [st1.enter_context(nc.psum_tensor("p2_pg1%d" % i, [128, 512], F32))
               for i in range(NPAR)]
        pg3 = [st1.enter_context(nc.psum_tensor("p2_pg3%d" % i, [128, 512], F32))
               for i in range(NPAR)]

        gat_mark = {}; h_mark = {}; silu_mark = {}; hb_mark = {}; hsum_mark = {}
        oh_mark = {}

        def t_pref(t):
            ec.fence(t, ["g", "v"])
        block.tensor(t_pref)

        for sc in range(512):
            b = sc % NPAR

            def t_gat(t, b=b, sc=sc):
                ec.wait(t, [h_mark.get(b, ("v", 0)), silu_mark.get(b, ("s", 0))])
                em("t", t.matmul(pg1[b][:], lhsT=ohT_all[:, sc * 128:(sc + 1) * 128],
                                 rhs=u1s[:], start=True, stop=True))
                em("t", t.matmul(pg3[b][:], lhsT=ohT_all[:, sc * 128:(sc + 1) * 128],
                                 rhs=u3s[:], start=True, stop=True))
            block.tensor(t_gat)
            gat_mark[b] = ec.mark("t")

            def v_h(v, sc=sc, b=b):
                ec.wait(v, [gat_mark[b], hsum_mark.get(b, ("t", 0))])
                em("v", v.tensor_scalar_mul(out=Asc2[b][:], in0=pg1[b][:],
                                            scalar1=wall[:, sc:sc + 1]))
                em("v", v.tensor_copy(out=u3g2[b][:], in_=pg3[b][:]))
                em("v", v.tensor_scalar(out=ohst2[b][:], in0=iota64[:],
                                        scalar1=c_all[:, sc:sc + 1], scalar2=None,
                                        op0=ALU.is_equal))
            block.vector(v_h)
            h_mark[b] = ec.mark("v")

            def s_h(s, b=b):
                ec.wait(s, [h_mark[b], hb_mark.get(b, ("v", 0))])
                em("s", s.activation(out=Ssc2[b][:], in_=Asc2[b][:], func=AF.Silu))
            block.scalar(s_h)
            silu_mark[b] = ec.mark("s")

            def v_h2(v, sc=sc, b=b):
                ec.wait(v, [silu_mark[b]])
                em("v", v.tensor_mul(out=u3g2[b][:], in0=u3g2[b][:], in1=Ssc2[b][:]))
                ec.selfsync("v", v)
                em("v", v.tensor_scalar_mul(out=hb2[b][:], in0=u3g2[b][:],
                                            scalar1=wall[:, sc:sc + 1]))
            block.vector(v_h2)
            hb_mark[b] = ec.mark("v")

            def t_hs(t, sc=sc, b=b):
                ec.wait(t, [hb_mark[b]])
                em("t", t.matmul(ph[:], lhsT=ohst2[b][:], rhs=hb2[b][:],
                                 start=(sc == 0), stop=(sc == 511)))
            block.tensor(t_hs)
            hsum_mark[b] = ec.mark("t")

        def v_hs(v):
            ec.fence(v, ["t"])
            em("v", v.tensor_copy(out=hs_sb[:], in_=ph[:]))
        block.vector(v_hs)

        def g_hs(g):
            ec.fence(g, ["v"])
            em("g", g.dma_start(out=io["hs_d"], in_=hs_sb[:]), dma=True)
        block.gpsimd(g_hs)

    with contextlib.ExitStack() as st2:
        P2_ = lambda name, shape, dt=F32: st2.enter_context(nc.psum_tensor(name, shape, dt))
        pt = P2_("p2_pt", [128, 8], BF16)
        prr = [P2_("p2_prr0", [8, 512]), P2_("p2_prr1", [8, 512])]
        ptj = P2_("p2_ptj", [128, 8])

        for e in range(8):
            def q_he(q, e=e):
                ec.fence(q, ["g", "t"])
                em("q", q.dma_start(out=hef[:], in_=io["hs_d"][e * 8:(e + 1) * 8, :]), dma=True)
                em("q", q.dma_start(out=w2l[:], in_=io["w2"][e].rearrange(
                    "(c p) d -> p c d", p=128)), dma=True)
            block.sync(q_he)

            def g_cast(g):
                ec.fence(g, ["q", "t"])
                em("g", g.tensor_copy(out=heb[:], in_=hef[:]))
                em("g", g.tensor_copy(out=w2b[:], in_=w2l[:]))
            block.gpsimd(g_cast)

            for f4 in range(4):
                def t_tr(t, e=e, f4=f4):
                    ec.fence(t, ["g", "v"])
                    em("t", t.transpose(pt[:], in_=heb[:, f4 * 128:(f4 + 1) * 128],
                                        identity=identb[0:8, 0:8]))
                block.tensor(t_tr)

                def v_tr(v, f4=f4):
                    ec.fence(v, ["t"])
                    em("v", v.tensor_copy(out=lhs4[:, f4, :], in_=pt[:]))
                block.vector(v_tr)

            for dh in range(2):
                def t_rr(t, e=e, dh=dh):
                    ec.fence(t, ["v", "g"])
                    for f4 in range(4):
                        em("t", t.matmul(prr[dh][:], lhsT=lhs4[:, f4, :],
                                         rhs=w2b[:, f4, dh * 512:(dh + 1) * 512],
                                         start=(e == 0 and f4 == 0), stop=(e == 7 and f4 == 3)))
                block.tensor(t_rr)

        def v_rr(v):
            ec.fence(v, ["t"])
            em("v", v.tensor_copy(out=rr_sb[:, 0:512], in_=prr[0][:]))
            em("v", v.tensor_copy(out=rr_sb[:, 512:1024], in_=prr[1][:])); Y(v)
        block.vector(v_rr)

        for ci in range(8):
            def t_tj(t, ci=ci):
                ec.fence(t, ["v"])
                em("t", t.transpose(ptj[:], in_=rr_sb[:, ci * 128:(ci + 1) * 128],
                                    identity=identf[0:8, 0:8]))
            block.tensor(t_tj)

            def v_tj(v, ci=ci):
                ec.fence(v, ["t"])
                em("v", v.tensor_copy(out=rrT[:, ci, :], in_=ptj[:]))
            block.vector(v_tj)

        def q_acc(q):
            ec.fence(q, ["g", "v"])
            em("q", q.dma_start(out=acc8[:], in_=io["acc1"][:, 0:8].rearrange(
                "(c p) e -> p c e", p=128)), dma=True)
        block.sync(q_acc)

        def v_add(v):
            ec.fence(v, ["q"])
            for ci in range(8):
                em("v", v.tensor_scalar_mul(out=rrT[:, ci, :], in0=rrT[:, ci, :],
                                            scalar1=mask0[:]))
            Y(v)
            for ci in range(8):
                em("v", v.tensor_add(out=acc8[:, ci, :], in0=acc8[:, ci, :],
                                     in1=rrT[:, ci, :]))
            Y(v)
        block.vector(v_add)

        def g_out(g):
            ec.fence(g, ["v"])
            em("g", g.dma_start(out=io["acc1"][:, 0:8].rearrange("(c p) e -> p c e", p=128),
                                in_=acc8[:]), dma=True)
        block.gpsimd(g_out)



P3A_DBG = -1

def build_p3a(nc, ec, block, io, st):
    S = lambda name, shape, dt=F32: st.enter_context(nc.sbuf_tensor(name, shape, dt))
    P = lambda name, shape: st.enter_context(nc.psum_tensor(name, shape, F32))
    em = ec.em
    NG = 8

    xg = S("d_xg", [128, 8, 512])
    BS = S("d_bs", [128, 8, 512])
    xn1b = S("d_xn1b", [128, 8, 512], BF16)
    x1af = S("d_x1af", [128, 8, 515])
    x2f = S("d_x2f", [128, 8, 512])
    x2b = S("d_x2b", [128, 8, 512], BF16)
    gx = S("d_gx", [128, 8, 512])
    BA = S("d_ba", [128, 8, 512])
    BM = S("d_bm", [128, 8, 512])
    yb = S("d_yb", [128, 8, 512], BF16)
    Aa = x1af
    wld = [S("d_wld0", [128, 8, 512]), S("d_wld1", [128, 8, 512])]
    wbf = [S("d_wbf0", [128, 8, 512], BF16), S("d_wbf1", [128, 8, 512], BF16)]
    mb = S("d_mb", [128, 512]); rvb = S("d_rvb", [128, 512])
    m_row = S("d_mrow", [1, 512]); v_row = S("d_vrow", [1, 512]); rv_row = S("d_rvrow", [1, 512])
    ones_c = S("d_onesc", [128, 1]); ones_r = S("d_onesr", [1, 128])
    one_col = S("d_onecol", [128, 1])
    zeros512 = S("d_z512", [128, 512])
    carry_h = S("d_carryh", [128, 8, 3])
    carry_s = S("d_carrys", [128, 8])
    par = {}
    for nm in ("l1_b_x", "l1_b_y", "l1_conv_b", "l1_gi_b", "l1_ga_b", "l1_a_param"):
        par[nm] = S("d_" + nm, [128, 8])
    conv1_w = S("d_conv1w", [128, 8, 4])
    msp = S("d_msp", [128, 8])
    t8a = S("d_t8a", [128, 8]); t8b = S("d_t8b", [128, 8])
    pms = [P("d_pm0", [1, 512]), P("d_pm1", [1, 512])]
    pbc = [P("d_pbc0", [128, 512]), P("d_pbc1", [128, 512])]
    pmm = [P("d_pmm0", [128, 512]), P("d_pmm1", [128, 512])]

    def Y(v):
        ec.selfsync("v", v)

    def q_setup(q):
        for nm in par:
            em("q", q.dma_start(out=par[nm][:], in_=io[nm]), dma=True)
        em("q", q.dma_start(out=conv1_w[:], in_=io["conv1_w"]), dma=True)
    block.sync(q_setup)

    def g_setup(g):
        em("g", g.memset(ones_c[:], 1.0))
        em("g", g.memset(ones_r[:], 1.0))
        em("g", g.memset(one_col[:], 1.0))
        em("g", g.memset(zeros512[:], 0.0))
        em("g", g.memset(carry_h[:], 0.0))
        em("g", g.memset(carry_s[:], 0.0))
    block.gpsimd(g_setup)

    def s_sp(s):
        ec.fence(s, ["q", "g"])
        em("s", s.activation(out=t8a[:], in_=par["l1_a_param"][:], func=AF.Abs))
    block.scalar(s_sp)

    def s_sp2(s):
        em("s", s.activation(out=t8a[:], in_=t8a[:], func=AF.Exp, scale=-1.0))
    block.scalar(s_sp2)

    def s_sp3(s):
        em("s", s.activation(out=t8a[:], in_=t8a[:], func=AF.Ln, bias=one_col[:]))
    block.scalar(s_sp3)

    def v_sp(v):
        ec.fence(v, ["s"])
        em("v", v.tensor_scalar_max(out=t8b[:], in0=par["l1_a_param"][:], scalar1=0.0)); Y(v)
        em("v", v.tensor_add(out=msp[:], in0=t8a[:], in1=t8b[:])); Y(v)
        em("v", v.tensor_scalar_mul(out=msp[:], in0=msp[:], scalar1=-8.0)); Y(v)
    block.vector(v_sp)

    def stats(src):
        def s_sq(s):
            ec.fence(s, ["q", "t", "v"])
            em("s", s.activation(out=BS[:], in_=src[:], func=AF.Square))
        block.scalar(s_sq)

        def t_stat(t):
            ec.fence(t, ["s", "v"])
            for ci in range(8):
                em("t", t.matmul(pms[0][:], lhsT=ones_c[:], rhs=src[:, ci, :],
                                 start=(ci == 0), stop=(ci == 7)))
            for ci in range(8):
                em("t", t.matmul(pms[1][:], lhsT=ones_c[:], rhs=BS[:, ci, :],
                                 start=(ci == 0), stop=(ci == 7)))
        block.tensor(t_stat)

        def v_stat(v):
            ec.fence(v, ["t"])
            em("v", v.tensor_scalar_mul(out=m_row[:], in0=pms[0][:], scalar1=1.0 / D)); Y(v)
            em("v", v.tensor_mul(out=v_row[:], in0=m_row[:], in1=m_row[:])); Y(v)
            em("v", v.scalar_tensor_tensor(out=v_row[:], in0=pms[1][:], scalar=1.0 / D,
                                           in1=v_row[:], op0=ALU.mult, op1=ALU.subtract)); Y(v)
            em("v", v.tensor_scalar_add(out=v_row[:], in0=v_row[:], scalar1=1e-5))
        block.vector(v_stat)

        def s_r(s):
            ec.fence(s, ["v"])
            em("s", s.activation(out=rv_row[:], in_=v_row[:], func=AF.Sqrt))
        block.scalar(s_r)

        def v_r(v):
            ec.fence(v, ["s"])
            em("v", v.reciprocal(out=rv_row[:], in_=rv_row[:]))
        block.vector(v_r)

        def t_b(t):
            ec.fence(t, ["v"])
            em("t", t.matmul(pbc[0][:], lhsT=ones_r[:], rhs=m_row[:], start=True, stop=True))
            em("t", t.matmul(pbc[1][:], lhsT=ones_r[:], rhs=rv_row[:], start=True, stop=True))
        block.tensor(t_b)

        def v_b(v):
            ec.fence(v, ["t"])
            em("v", v.tensor_copy(out=mb[:], in_=pbc[0][:]))
            em("v", v.tensor_copy(out=rvb[:], in_=pbc[1][:])); Y(v)
        block.vector(v_b)

    def proj(wkey, rhs_tile, evac, bias_key=None):
        for ob in range(2):
            wb = wld[ob % 2]; wf = wbf[ob % 2]

            def q_w(q, wkey=wkey, ob=ob, wb=wb):
                ec.fence(q, ["g"])
                em("q", q.dma_start(out=wb[:], in_=io[wkey][:, ob * 512:(ob + 1) * 512].rearrange(
                    "(c p) o -> p c o", p=128)), dma=True)
            block.sync(q_w)

            def g_cast(g, wb=wb, wf=wf):
                ec.fence(g, ["q", "t"])
                em("g", g.tensor_copy(out=wf[:], in_=wb[:]))
            block.gpsimd(g_cast)

            for oc in range(4):
                pp = pmm[oc % 2]

                def t_p(t, wf=wf, oc=oc, pp=pp):
                    ec.fence(t, ["g", "v", "s"])
                    for ci in range(8):
                        em("t", t.matmul(pp[:], lhsT=wf[:, ci, oc * 128:(oc + 1) * 128],
                                         rhs=rhs_tile[:, ci, :], start=(ci == 0), stop=(ci == 7)))
                block.tensor(t_p)

                evac(ob * 4 + oc, pp)

    for g_i in range(NG):
        t0 = g_i * 512

        def q_xg(q, t0=t0):
            ec.fence(q, ["v", "s", "g"])
            em("q", q.dma_start(out=xg[:], in_=io["acc1"][:, t0:t0 + 512].rearrange(
                "(c p) t -> p c t", p=128)), dma=True)
        block.sync(q_xg)

        stats(xg)

        def v_xn(v, t0=t0):
            for ci in range(8):
                em("v", v.tensor_sub(out=BS[:, ci, :], in0=xg[:, ci, :], in1=mb[:]))
            Y(v)
            for ci in range(8):
                em("v", v.tensor_mul(out=BS[:, ci, :], in0=BS[:, ci, :], in1=rvb[:]))
            Y(v)
            for ci in range(8):
                em("v", v.tensor_copy(out=xn1b[:, ci, :], in_=BS[:, ci, :]))
            Y(v)
        block.vector(v_xn)

        def g_xn1(g, t0=t0):
            ec.fence(g, ["v"])
            em("g", g.dma_start(out=io["xn1f_d"][:, t0:t0 + 512].rearrange(
                "(c p) t -> p c t", p=128), in_=BS[:]), dma=True)
            em("g", g.dma_start(out=io["xn1b_d"][:, t0:t0 + 512].rearrange(
                "(c p) t -> p c t", p=128), in_=xn1b[:]), dma=True)
        block.gpsimd(g_xn1)

        def ev_x1(occ, pp):
            def v_e(v, occ=occ, pp=pp):
                ec.fence(v, ["t"])
                em("v", v.tensor_scalar(out=x1af[:, occ, 3:515], in0=pp[:],
                                        scalar1=par["l1_b_x"][:, occ:occ + 1], scalar2=None,
                                        op0=ALU.add))
            block.vector(v_e)
        proj("w_x", xn1b, ev_x1)

        def v_conv(v, g_i=g_i):
            for ci in range(8):
                em("v", v.tensor_copy(out=x1af[:, ci, 0:3], in_=carry_h[:, ci, :]))
            Y(v)
            for ci in range(8):
                em("v", v.tensor_copy(out=carry_h[:, ci, :], in_=x1af[:, ci, 512:515]))
            Y(v)
            for ci in range(8):
                em("v", v.tensor_scalar_mul(out=x2f[:, ci, :], in0=x1af[:, ci, 0:512],
                                            scalar1=conv1_w[:, ci, 0:1]))
            Y(v)
            for i in range(1, 4):
                for ci in range(8):
                    em("v", v.scalar_tensor_tensor(out=x2f[:, ci, :], in0=x1af[:, ci, i:i + 512],
                                                   scalar=conv1_w[:, ci, i:i + 1],
                                                   in1=x2f[:, ci, :], op0=ALU.mult, op1=ALU.add))
                Y(v)
            for ci in range(8):
                em("v", v.tensor_scalar(out=x2f[:, ci, :], in0=x2f[:, ci, :],
                                        scalar1=par["l1_conv_b"][:, ci:ci + 1], scalar2=None,
                                        op0=ALU.add))
            Y(v)
            for ci in range(8):
                em("v", v.tensor_copy(out=x2b[:, ci, :], in_=x2f[:, ci, :]))
            Y(v)
        block.vector(v_conv)

        def ev_gx(occ, pp):
            def s_e(s, occ=occ, pp=pp):
                ec.fence(s, ["t"])
                em("s", s.activation(out=gx[:, occ, :], in_=pp[:], func=AF.Sigmoid,
                                     bias=par["l1_gi_b"][:, occ:occ + 1]))
            block.scalar(s_e)
        proj("gi_w", x2b, ev_gx)

        def ev_ga(occ, pp):
            def s_e(s, occ=occ, pp=pp):
                ec.fence(s, ["t"])
                em("s", s.activation(out=BA[:, occ, :], in_=pp[:], func=AF.Sigmoid,
                                     bias=par["l1_ga_b"][:, occ:occ + 1]))
            block.scalar(s_e)
        proj("ga_w", x2b, ev_ga)

        def v_la(v):
            ec.fence(v, ["s"])
            for ci in range(8):
                em("v", v.tensor_scalar_mul(out=BA[:, ci, :], in0=BA[:, ci, :],
                                            scalar1=msp[:, ci:ci + 1]))
            Y(v)
        block.vector(v_la)

        def s_a(s):
            ec.fence(s, ["v"])
            for ci in range(8):
                em("s", s.activation(out=Aa[:, ci, 0:512], in_=BA[:, ci, :], func=AF.Exp))
            for ci in range(8):
                em("s", s.activation(out=BM[:, ci, :], in_=BA[:, ci, :], func=AF.Exp,
                                     scale=2.0))
        block.scalar(s_a)

        def v_mult(v, g_i=g_i):
            ec.fence(v, ["s"])
            for ci in range(8):
                em("v", v.tensor_scalar(out=BM[:, ci, :], in0=BM[:, ci, :], scalar1=-1.0,
                                        scalar2=1.0, op0=ALU.mult, op1=ALU.add))
            Y(v)
            for ci in range(8):
                em("v", v.tensor_scalar_max(out=BM[:, ci, :], in0=BM[:, ci, :], scalar1=0.0))
            Y(v)
        block.vector(v_mult)

        def s_sqrt(s):
            ec.fence(s, ["v"])
            for ci in range(8):
                em("s", s.activation(out=BM[:, ci, :], in_=BM[:, ci, :], func=AF.Sqrt))
        block.scalar(s_sqrt)

        if g_i == 0:
            def g_t0(g):
                ec.fence(g, ["s"])
                em("g", g.memset(BM[:, :, 0:1], 1.0))
                em("g", g.memset(Aa[:, :, 0:1], 0.0))
            block.gpsimd(g_t0)

        def v_u(v):
            ec.fence(v, ["s", "g"])
            for ci in range(8):
                em("v", v.tensor_mul(out=gx[:, ci, :], in0=gx[:, ci, :], in1=x2f[:, ci, :]))
            Y(v)
            for ci in range(8):
                em("v", v.tensor_mul(out=gx[:, ci, :], in0=gx[:, ci, :], in1=BM[:, ci, :]))
            Y(v)
            for ci in range(8):
                em("v", v.tensor_tensor_scan(out=BA[:, ci, :], data0=Aa[:, ci, 0:512],
                                             data1=gx[:, ci, :],
                                             initial=carry_s[:, ci:ci + 1],
                                             op0=ALU.mult, op1=ALU.add))
                ec.selfsync("v", v)
                em("v", v.tensor_copy(out=carry_s[:, ci:ci + 1], in_=BA[:, ci, 511:512]))
                ec.selfsync("v", v)
        block.vector(v_u)

        def ev_y(occ, pp):
            def s_e(s, occ=occ, pp=pp):
                ec.fence(s, ["t"])
                em("s", s.activation(out=yb[:, occ, :], in_=pp[:], func=AF.Gelu,
                                     bias=par["l1_b_y"][:, occ:occ + 1]))
            block.scalar(s_e)
        proj("w_y", xn1b, ev_y)

        def v_out(v):
            ec.fence(v, ["s"])
            for ci in range(8):
                em("v", v.tensor_mul(out=BS[:, ci, :], in0=BA[:, ci, :], in1=yb[:, ci, :]))
            Y(v)
            for ci in range(8):
                em("v", v.tensor_add(out=xg[:, ci, :], in0=xg[:, ci, :], in1=BS[:, ci, :]))
            Y(v)
        block.vector(v_out)

        def g_out(g, t0=t0):
            ec.fence(g, ["v"])
            em("g", g.dma_start(out=io["acc3"][:, t0:t0 + 512].rearrange(
                "(c p) t -> p c t", p=128), in_=xg[:]), dma=True)
        block.gpsimd(g_out)

        if g_i == P3A_DBG:
            def g_dbg(g):
                ec.fence(g, ["v", "s"])
                for off, tile in ((0, Aa), (512, BM), (1024, gx), (1536, BA), (2048, x2f), (2560, BS)):
                    src = tile[:, :, 0:512] if tile is Aa else tile[:]
                    em("g", g.dma_start(out=io["dbgA"][:, off:off + 512].rearrange(
                        "(c p) t -> p c t", p=128), in_=src), dma=True)
            block.gpsimd(g_dbg)


def build_p3b(nc, ec, block, io, st):
    S = lambda name, shape, dt=F32: st.enter_context(nc.sbuf_tensor(name, shape, dt))
    em = ec.em

    def Y(v):
        ec.selfsync("v", v)

    with contextlib.ExitStack() as st1:
        S1 = lambda name, shape, dt=F32: st1.enter_context(nc.sbuf_tensor(name, shape, dt))
        P1_ = lambda name, shape: st1.enter_context(nc.psum_tensor(name, shape, F32))
        xn1f = S1("e_xn1f", [128, 8, 1024])
        gw = S1("e_gw", [128, 8, 8])
        gb8 = S1("e_gb8", [8, 1])
        lg_sb = S1("e_lg", [8, 1024])
        lt = S1("e_lt", [128, 8, 8])
        probs = S1("e_probs", [128, 8, 8])
        pr = S1("e_pr", [128, 8, 8])
        k1 = S1("e_k1", [128, 8, 8])
        k2 = S1("e_k2", [128, 8, 8])
        red = S1("e_red", [128, 8])
        iota8 = S1("e_iota8", [128, 8])
        comb = S1("e_comb", [128, 8, 8])
        crow = S1("e_crow", [8, 1024])
        identf = S1("e_id", [128, 128])
        plg = P1_("e_plg", [8, 512])
        ptr = P1_("e_ptr", [128, 8])
        ptc = P1_("e_ptc", [8, 128])

        def redb(n):
            return red[:].rearrange("p (s o) -> p s o", o=1).broadcast_to([128, 8, n])

        def q_setup(q):
            em("q", q.dma_start(out=gw[:], in_=io["gate1_w"].rearrange("(c p) e -> p c e", p=128)), dma=True)
            em("q", q.dma_start(out=gb8[:], in_=io["gate1_bc"]), dma=True)
            em("q", q.dma_start(out=identf[:], in_=io["identf"]), dma=True)
        block.sync(q_setup)

        def g_setup(g):
            em("g", g.iota(iota8[:], pattern=[[1, 8]], base=0, channel_multiplier=0,
                           allow_small_or_imprecise_dtypes=True))
        block.gpsimd(g_setup)

        for g_i in range(4):
            t0 = g_i * 1024

            def q_ld(q, t0=t0):
                ec.fence(q, ["v", "t", "g"])
                em("q", q.dma_start(out=xn1f[:], in_=io["xn1f_d"][:, t0:t0 + 1024].rearrange(
                    "(c p) t -> p c t", p=128)), dma=True)
            block.sync(q_ld)

            for t5 in range(2):
                def t_lg(t, t5=t5):
                    ec.fence(t, ["q", "v", "g"])
                    for ci in range(8):
                        em("t", t.matmul(plg[:], lhsT=gw[:, ci, :],
                                         rhs=xn1f[:, ci, t5 * 512:(t5 + 1) * 512],
                                         start=(ci == 0), stop=(ci == 7)))
                block.tensor(t_lg)

                def v_lg(v, t5=t5):
                    ec.fence(v, ["t"])
                    em("v", v.tensor_scalar(out=lg_sb[:, t5 * 512:(t5 + 1) * 512], in0=plg[:],
                                            scalar1=gb8[:], scalar2=None, op0=ALU.add))
                block.vector(v_lg)

            for tc in range(8):
                def t_tr(t, tc=tc):
                    ec.fence(t, ["v"])
                    em("t", t.transpose(ptr[:], in_=lg_sb[:, tc * 128:(tc + 1) * 128],
                                        identity=identf[0:8, 0:8]))
                block.tensor(t_tr)

                def v_tr(v, tc=tc):
                    ec.fence(v, ["t"])
                    em("v", v.tensor_copy(out=lt[:, tc, :], in_=ptr[:]))
                block.vector(v_tr)

            def v_sm(v):
                Y(v)
                em("v", v.reduce_max(out=red[:], in_=lt[:], axis=AX.X)); Y(v)
                em("v", v.tensor_sub(out=pr[:], in0=lt[:], in1=redb(8))); Y(v)
            block.vector(v_sm)

            def s_exp(s):
                ec.fence(s, ["v"])
                em("s", s.activation(out=pr[:], in_=pr[:], func=AF.Exp))
            block.scalar(s_exp)

            def v_sm2(v):
                ec.fence(v, ["s"])
                em("v", v.reduce_sum(out=red[:], in_=pr[:], axis=AX.X)); Y(v)
                em("v", v.reciprocal(out=red[:], in_=red[:])); Y(v)
                em("v", v.tensor_mul(out=probs[:], in0=pr[:], in1=redb(8))); Y(v)
                em("v", v.scalar_tensor_tensor(out=pr[:], in0=iota8[:].rearrange(
                    "p (o e) -> p o e", o=1).broadcast_to([128, 8, 8]), scalar=-TIE,
                    in1=probs[:], op0=ALU.mult, op1=ALU.add)); Y(v)
                em("v", v.reduce_max(out=red[:], in_=pr[:], axis=AX.X)); Y(v)
                em("v", v.tensor_tensor(out=k1[:], in0=pr[:], in1=redb(8), op=ALU.is_ge)); Y(v)
                em("v", v.scalar_tensor_tensor(out=pr[:], in0=k1[:], scalar=-1e4, in1=pr[:],
                                               op0=ALU.mult, op1=ALU.add)); Y(v)
                em("v", v.reduce_max(out=red[:], in_=pr[:], axis=AX.X)); Y(v)
                em("v", v.tensor_tensor(out=k2[:], in0=pr[:], in1=redb(8), op=ALU.is_ge)); Y(v)
                em("v", v.tensor_add(out=k1[:], in0=k1[:], in1=k2[:])); Y(v)
                em("v", v.tensor_mul(out=comb[:], in0=k1[:], in1=probs[:])); Y(v)
                em("v", v.reduce_sum(out=red[:], in_=comb[:], axis=AX.X)); Y(v)
                em("v", v.reciprocal(out=red[:], in_=red[:])); Y(v)
                em("v", v.tensor_mul(out=comb[:], in0=comb[:], in1=redb(8))); Y(v)
            block.vector(v_sm2)

            for tc in range(8):
                def t_tc(t, tc=tc):
                    ec.fence(t, ["v"])
                    em("t", t.transpose(ptc[:], in_=comb[:, tc, :], identity=identf[:]))
                block.tensor(t_tc)

                def v_tc(v, tc=tc):
                    ec.fence(v, ["t"])
                    em("v", v.tensor_copy(out=crow[:, tc * 128:(tc + 1) * 128], in_=ptc[:]))
                block.vector(v_tc)

            def g_crow(g, t0=t0):
                ec.fence(g, ["v"])
                em("g", g.dma_start(out=io["crow_d"][:, t0:t0 + 1024], in_=crow[:]), dma=True)
            block.gpsimd(g_crow)

    with contextlib.ExitStack() as st2:
        S2 = lambda name, shape, dt=F32: st2.enter_context(nc.sbuf_tensor(name, shape, dt))
        P2_ = lambda name, shape: st2.enter_context(nc.psum_tensor(name, shape, F32))
        xn1b = S2("e_xn1b", [128, 8, 1024], BF16)
        acc = S2("e_acc", [128, 8, 1024])
        m = S2("e_m", [128, 32, 1024], BF16)
        wld = [S2("e_wld0", [128, 8, 512]), S2("e_wld1", [128, 8, 512]),
               S2("e_wld2", [128, 8, 512])]
        wbf = [S2("e_wbf0", [128, 8, 512], BF16), S2("e_wbf1", [128, 8, 512], BF16),
               S2("e_wbf2", [128, 8, 512], BF16)]
        crow1 = S2("e_crow1", [1, 1024])
        bcb = S2("e_bcb", [128, 2, 512], BF16)
        dtmp = S2("e_dtmp", [128, 512])
        ones_r = S2("e_onesr", [1, 128])
        pps = [P2_("e_pp%d" % i, [128, 512]) for i in range(8)]

        def g_setup2(g):
            ec.fence(g, ["g", "v", "q"])
            em("g", g.memset(ones_r[:], 1.0))
        block.gpsimd(g_setup2)

        cast_mark = {}
        mm_done = {}
        evac_mark = {}
        NBUF = 3

        def wblock(kslot, wkey_ap, mm_emit, cast_en="v"):
            b = kslot[0] % NBUF
            wb, wf = wld[b], wbf[b]

            def q_w(q, b=b, wkey_ap=wkey_ap):
                ec.wait(q, [cast_mark.get(b, ("v", 0))])
                em("q", q.dma_start(out=wb[:], in_=wkey_ap), dma=True)
            block.sync(q_w)
            ldm = ec.mark("q")

            def v_cast(v, b=b, ldm=ldm):
                ec.wait(v, [ldm, mm_done.get(b, ("t", 0))])
                em("v" if cast_en == "v" else cast_en,
                   (v.tensor_copy(out=wf[:], in_=wb[:])))
            if cast_en == "v":
                block.vector(v_cast)
                cast_mark[b] = ec.mark("v")
            else:
                def s_cast(sx, b=b, ldm=ldm):
                    ec.wait(sx, [ldm, mm_done.get(b, ("t", 0))])
                    em("s", sx.tensor_copy(out=wf[:], in_=wb[:])) if False else em(
                        "s", sx.copy(out=wf[:], in_=wb[:]))
                block.scalar(s_cast)
                cast_mark[b] = ec.mark("s")
            mm_emit(wf, b)
            mm_done[b] = ec.mark("t")
            kslot[0] += 1

        kslot = [0]
        ldx_mark = [None]

        for g_i in range(4):
            t0 = g_i * 1024

            def q_ld(q, t0=t0):
                ec.fence(q, ["v", "t", "g", "s", "q"])
                em("q", q.dma_start(out=xn1b[:], in_=io["xn1b_d"][:, t0:t0 + 1024].rearrange(
                    "(c p) t -> p c t", p=128)), dma=True)
                em("q", q.dma_start(out=acc[:], in_=io["acc3"][:, t0:t0 + 1024].rearrange(
                    "(c p) t -> p c t", p=128)), dma=True)
            block.sync(q_ld)
            ldx = ec.mark("q")

            for e in range(8):
                def q_cr(q, e=e, t0=t0):
                    ec.wait(q, [evac_mark.get(100, ("v", 0))])
                    ec.fence(q, ["v"]) if False else None
                    em("q", q.dma_start(out=crow1[:], in_=io["crow_d"][e:e + 1, t0:t0 + 1024]), dma=True)
                block.sync(q_cr)
                crm = ec.mark("q")

                for t5 in range(2):
                    def t_bc(t, t5=t5, crm=crm):
                        ec.wait(t, [crm, evac_mark.get(4 + t5, ("v", 0))])
                        em("t", t.matmul(pps[4 + t5][:], lhsT=ones_r[:],
                                         rhs=crow1[0:1, t5 * 512:(t5 + 1) * 512],
                                         start=True, stop=True))
                    block.tensor(t_bc)
                    bm = ec.mark("t")

                    def v_bc(v, t5=t5, bm=bm):
                        ec.wait(v, [bm])
                        em("v", v.tensor_copy(out=bcb[:, t5, :], in_=pps[4 + t5][:]))
                    block.vector(v_bc)
                    evac_mark[4 + t5] = ec.mark("v")

                silu_marks = {}
                for fb in range(8):
                    for which, wkey in ((0, "w1f"), (1, "w3f")):
                        ap_w = io[wkey][e][:, fb * 512:(fb + 1) * 512].rearrange(
                            "(c p) o -> p c o", p=128)

                        def mm_gateup(wf, b, fb=fb, which=which, e=e):
                            for fc in range(4):
                                for t5 in range(2):
                                    ppi = which * 2 + (fc * 2 + t5) % 2
                                    pp = pps[ppi]

                                    def t_p(t, wf=wf, fc=fc, t5=t5, pp=pp, ppi=ppi):
                                        ec.wait(t, [cast_mark[b], ldx,
                                                    evac_mark.get(ppi, ("v", 0))])
                                        for ci in range(8):
                                            em("t", t.matmul(pp[:], lhsT=wf[:, ci, fc * 128:(fc + 1) * 128],
                                                             rhs=xn1b[:, ci, t5 * 512:(t5 + 1) * 512],
                                                             start=(ci == 0), stop=(ci == 7)))
                                    block.tensor(t_p)
                                    gm = ec.mark("t")

                                    fcg = fb * 4 + fc
                                    if which == 0:
                                        def s_silu(s, pp=pp, fcg=fcg, t5=t5, gm=gm):
                                            ec.wait(s, [gm])
                                            em("s", s.activation(out=m[:, fcg, t5 * 512:(t5 + 1) * 512],
                                                                 in_=pp[:], func=AF.Silu))
                                        block.scalar(s_silu)
                                        evac_mark[ppi] = ec.mark("s")
                                        silu_marks[(fcg, t5)] = evac_mark[ppi]
                                    else:
                                        def v_mul(v, pp=pp, fcg=fcg, t5=t5, gm=gm):
                                            ec.wait(v, [gm, silu_marks[(fcg, t5)]])
                                            em("v", v.tensor_mul(out=m[:, fcg, t5 * 512:(t5 + 1) * 512],
                                                                 in0=m[:, fcg, t5 * 512:(t5 + 1) * 512],
                                                                 in1=pp[:]))
                                        block.vector(v_mul)
                                        evac_mark[ppi] = ec.mark("v")

                        wblock(kslot, ap_w, mm_gateup, cast_en="s" if which == 0 else "v")

                for db in range(2):
                    for sub in range(4):
                        ap_w = io["w2f"][e][sub * 1024:(sub + 1) * 1024,
                                            db * 512:(db + 1) * 512].rearrange(
                            "(c p) o -> p c o", p=128)

                        def mm_down(wf, b, db=db, sub=sub, e=e):
                            for dc in range(4):
                                for t5 in range(2):
                                    ppi = dc * 2 + t5
                                    pp = pps[ppi]

                                    def t_p(t, wf=wf, dc=dc, t5=t5, pp=pp, ppi=ppi, sub=sub):
                                        w8 = [cast_mark[b]]
                                        if sub == 0:
                                            w8.append(evac_mark.get(ppi, ("v", 0)))
                                        ec.wait(t, w8)
                                        for kf in range(8):
                                            em("t", t.matmul(pp[:], lhsT=wf[:, kf, dc * 128:(dc + 1) * 128],
                                                             rhs=m[:, sub * 8 + kf, t5 * 512:(t5 + 1) * 512],
                                                             start=(kf == 0 and sub == 0),
                                                             stop=(kf == 7 and sub == 3)))
                                    block.tensor(t_p)

                                    if sub == 3:
                                        gm = ec.mark("t")
                                        dcg = db * 4 + dc

                                        def v_acc(v, pp=pp, dcg=dcg, t5=t5, gm=gm):
                                            ec.wait(v, [gm])
                                            em("v", v.tensor_mul(out=dtmp[:], in0=pp[:], in1=bcb[:, t5, :]))
                                            ec.selfsync("v", v)
                                            em("v", v.tensor_add(out=acc[:, dcg, t5 * 512:(t5 + 1) * 512],
                                                                 in0=acc[:, dcg, t5 * 512:(t5 + 1) * 512],
                                                                 in1=dtmp[:]))
                                        block.vector(v_acc)
                                        evac_mark[ppi] = ec.mark("v")

                        wblock(kslot, ap_w, mm_down, cast_en="v")
                evac_mark[100] = ec.mark("v")

            def g_store(g, t0=t0):
                ec.fence(g, ["v"])
                em("g", g.dma_start(out=io["outT"][:, t0:t0 + 1024].rearrange(
                    "(c p) t -> p c t", p=128), in_=acc[:]), dma=True)
            block.gpsimd(g_store)


def phase_barrier(ec, block):
    for dst in EngCtx.NAMES:
        def body(q, dst=dst):
            ec.fence(q, list(EngCtx.NAMES))
        getattr(block, _BLK[dst])(body)


def run_phases(build_fns, which, in_maps, n_cores=8, trace=False):
    nc = bass.Bass("TRN2", target_bir_lowering=False)
    io = dram_io(nc, which)
    with contextlib.ExitStack() as st:
        ec = EngCtx(nc, st)
        block = st.enter_context(nc.Block())
        for i, fn in enumerate(build_fns):
            if i:
                phase_barrier(ec, block)
            with contextlib.ExitStack() as pst:
                fn(nc, ec, block, io, pst)

        def g_done(g):
            ec.fence(g, ["q", "s", "v", "t"])
        block.gpsimd(g_done)
    return run_bass_kernel_spmd(nc, in_maps, core_ids=list(range(n_cores)), trace=trace)


INPUT_NAMES = [
    "xT", "xaT", "lhsA", "bias_row", "identf", "mask0",
    "qk_wT", "v_wT", "g_wT", "o_wT", "gate_wT", "up_wT", "down_wT",
    "ln1_w", "ln1_b", "conv_b", "g_b", "onorm_w", "onorm_b", "ln2_w", "ln2_b",
    "gamma1", "gamma2", "conv_w", "gammaA", "freqs", "headsel", "headexp",
    "sw1T", "sw2T", "sw3T", "w1", "w3", "w2",
    "w_x", "w_y", "gi_w", "ga_w", "l1_b_x", "l1_b_y", "l1_conv_b",
    "l1_gi_b", "l1_ga_b", "l1_a_param", "conv1_w",
    "gate1_w", "gate1_bc", "w1f", "w3f", "w2f",
]
INTER_NAMES = [
    "jslot", "wslot", "offs_d", "xn8T_d", "mrow_d", "u1tab", "u3tab", "hs_d",
    "acc1", "acc3", "h2_d", "xnb_d", "xn1f_d", "xn1b_d", "crow_d",
]
ALL_PHASES = None


def kernel(**inputs):
    phases = [build_p0, build_p1a, build_p1b, build_p1c, build_p2,
              build_p3a, build_p3b]
    which = {n: "in" for n in INPUT_NAMES}
    which.update({n: "int" for n in INTER_NAMES})
    which["outT"] = "out"
    in_maps = []
    for core in range(8):
        hi = build_host_inputs(inputs, core)
        in_maps.append({n: hi[n] for n in INPUT_NAMES})
    res = run_phases(phases, which, in_maps, n_cores=8)
    out = np.stack([np.asarray(res.results[c]["outT"]).T for c in range(8)], axis=0)
    return np.ascontiguousarray(out).astype(np.float32)


# version 15
# speedup vs baseline: 1.0695x; 1.0137x over previous
import contextlib
import numpy as np
import concourse.bass as bass
import concourse.mybir as mybir
from concourse.bass_utils import run_bass_kernel_spmd

F32 = mybir.dt.float32
BF16 = mybir.dt.bfloat16
I32 = mybir.dt.int32
AF = mybir.ActivationFunctionType
ALU = mybir.AluOpType
AX = mybir.AxisListType

TWO_PI = 6.283185307179586
INV_2PI = 1.0 / TWO_PI
CW1 = 6.25
CW2 = 68.0 / 2048.0
CW3 = float(np.float32(np.float64(TWO_PI) - CW1 - CW2))
MAGIC = 12582912.0
HALF_PI = float(np.float32(np.pi / 2))
PI_F = float(np.float32(np.pi))
TIE = 5e-11

T_SEQ = 4096
D = 1024
NTOK_ALL = 32768

_BLK = {"t": "tensor", "v": "vector", "s": "scalar", "g": "gpsimd", "q": "sync"}


class EngCtx:
    NAMES = ("t", "v", "s", "g", "q")

    def __init__(self, nc, st):
        self.nc = nc
        self.sem = {n: st.enter_context(nc.semaphore("pc_" + n)) for n in self.NAMES}
        self.val = {n: 0 for n in self.NAMES}
        self.last = {n: None for n in self.NAMES}

    def em(self, en, inst, dma=False):
        self.last[en] = [inst, False, 16 if dma else 1]
        if dma:
            self._seal(en)
        return inst

    def _seal(self, en):
        li = self.last[en]
        if li is None:
            return
        inst, inced, step = li
        if not inced:
            inst.then_inc(self.sem[en], step)
            self.val[en] += step
            li[1] = True

    def selfsync(self, en, q):
        self._seal(en)
        q.wait_ge(self.sem[en], self.val[en])

    def fence(self, dst_q, srcs):
        for s_en in srcs:
            if self.last[s_en] is None:
                continue
            self._seal(s_en)
            dst_q.wait_ge(self.sem[s_en], self.val[s_en])

    def mark(self, en):
        if self.last[en] is None:
            return (en, 0)
        self._seal(en)
        return (en, self.val[en])

    def wait(self, q, marks):
        for en, val in marks:
            if val > 0:
                q.wait_ge(self.sem[en], val)


def perch(v):
    return np.ascontiguousarray(np.asarray(v, np.float32).reshape(8, 128).T)


def build_host_inputs(inputs, core):
    x = np.asarray(inputs["L0_attn_x"], np.float32)
    d = {}
    d["xT"] = np.ascontiguousarray(x[core].T)
    d["xaT"] = np.ascontiguousarray(x.reshape(NTOK_ALL, D).T)
    wgT = np.asarray(inputs["L0_ffn_w_gate"], np.float32).T
    lhsA = np.zeros((128, 8, 9), np.float32)
    lhsA[:, :, :8] = wgT.reshape(8, 128, 8).transpose(1, 0, 2)
    lhsA[:, :, 8] = 1.0
    d["lhsA"] = np.ascontiguousarray(lhsA.reshape(128, 72))
    d["bias_row"] = np.asarray(inputs["L0_ffn_expert_bias"], np.float32).reshape(1, 8)
    for nm in ("qk_w", "v_w", "g_w", "o_w"):
        d[nm + "T"] = np.ascontiguousarray(np.asarray(inputs["L0_attn_" + nm], np.float32).T)
    d["gate_wT"] = np.ascontiguousarray(np.asarray(inputs["L0_attn_gate_w"], np.float32).T)
    d["up_wT"] = np.ascontiguousarray(np.asarray(inputs["L0_attn_up_w"], np.float32).T)
    d["down_wT"] = np.ascontiguousarray(np.asarray(inputs["L0_attn_down_w"], np.float32).T)
    for nm in ("ln1_w", "ln1_b", "conv_b", "g_b", "onorm_w", "onorm_b",
               "ln2_w", "ln2_b", "gamma1", "gamma2"):
        d[nm] = perch(inputs["L0_attn_" + nm])
    d["conv_w"] = np.ascontiguousarray(
        np.asarray(inputs["L0_attn_conv_w"], np.float32)[:, 0, :].reshape(8, 128, 4).transpose(1, 0, 2))
    d["gammaA"] = perch(np.repeat(np.asarray(inputs["L0_attn_gamma_attn"], np.float32).reshape(16), 64))
    for nm in ("sw1", "sw2", "sw3"):
        d[nm + "T"] = np.ascontiguousarray(np.asarray(inputs["L0_ffn_" + nm], np.float32).T)
    d["w1"] = np.ascontiguousarray(np.asarray(inputs["L0_ffn_w1"], np.float32))
    d["w3"] = np.ascontiguousarray(np.asarray(inputs["L0_ffn_w3"], np.float32))
    d["w2"] = np.ascontiguousarray(np.asarray(inputs["L0_ffn_w2"], np.float32))
    for nm in ("w_x", "w_y", "gi_w", "ga_w"):
        d[nm] = np.ascontiguousarray(np.asarray(inputs["L1_attn_" + nm], np.float32))
    for nm in ("b_x", "b_y", "conv_b", "gi_b", "ga_b", "a_param"):
        d["l1_" + nm] = perch(inputs["L1_attn_" + nm])
    d["conv1_w"] = np.ascontiguousarray(
        np.asarray(inputs["L1_attn_conv_w"], np.float32).T.reshape(8, 128, 4).transpose(1, 0, 2))
    d["gate1_w"] = np.ascontiguousarray(np.asarray(inputs["L1_ffn_gate_w"], np.float32))
    d["gate1_bc"] = np.asarray(inputs["L1_ffn_gate_b"], np.float32).reshape(8, 1)
    d["w1f"] = np.ascontiguousarray(np.asarray(inputs["L1_ffn_w1"], np.float32))
    d["w3f"] = np.ascontiguousarray(np.asarray(inputs["L1_ffn_w3"], np.float32))
    d["w2f"] = np.ascontiguousarray(np.asarray(inputs["L1_ffn_w2"], np.float32))
    d["mask0"] = np.full((128, 1), 1.0 if core == 0 else 0.0, np.float32)
    d["identf"] = np.eye(128, dtype=np.float32)
    idx = np.linspace(0.0, 1.0, 64).astype(np.float32)
    freqs = ((10.0 ** idx)[None, :] * np.arange(16, dtype=np.float32)[:, None]).astype(np.float32)
    d["freqs"] = perch(freqs.reshape(1024))
    hs = np.zeros((128, 8, 16), np.float32)
    for cc in range(8):
        for p in range(128):
            hs[p, cc, 2 * cc + (p >> 6)] = 1.0
    d["headsel"] = np.ascontiguousarray(hs)
    d["headexp"] = np.ascontiguousarray(np.transpose(hs, (2, 1, 0)))
    return d


DRAM_SHAPES = {
    "xT": ([D, T_SEQ], F32), "xaT": ([D, NTOK_ALL], F32),
    "lhsA": ([128, 72], F32), "bias_row": ([1, 8], F32),
    "qk_wT": ([D, D], F32), "v_wT": ([D, D], F32), "g_wT": ([D, D], F32),
    "o_wT": ([D, D], F32), "gate_wT": ([D, 4096], F32), "up_wT": ([D, 4096], F32),
    "down_wT": ([4096, D], F32),
    "ln1_w": ([128, 8], F32), "ln1_b": ([128, 8], F32), "conv_b": ([128, 8], F32),
    "g_b": ([128, 8], F32), "onorm_w": ([128, 8], F32), "onorm_b": ([128, 8], F32),
    "ln2_w": ([128, 8], F32), "ln2_b": ([128, 8], F32),
    "gamma1": ([128, 8], F32), "gamma2": ([128, 8], F32),
    "conv_w": ([128, 8, 4], F32), "gammaA": ([128, 8], F32),
    "sw1T": ([D, D], F32), "sw2T": ([D, D], F32), "sw3T": ([D, D], F32),
    "w1": ([8, D, 512], F32), "w3": ([8, D, 512], F32), "w2": ([8, 512, D], F32),
    "w_x": ([D, D], F32), "w_y": ([D, D], F32), "gi_w": ([D, D], F32), "ga_w": ([D, D], F32),
    "l1_b_x": ([128, 8], F32), "l1_b_y": ([128, 8], F32), "l1_conv_b": ([128, 8], F32),
    "l1_gi_b": ([128, 8], F32), "l1_ga_b": ([128, 8], F32), "l1_a_param": ([128, 8], F32),
    "conv1_w": ([128, 8, 4], F32),
    "gate1_w": ([D, 8], F32), "gate1_bc": ([8, 1], F32),
    "w1f": ([8, D, 4096], F32), "w3f": ([8, D, 4096], F32), "w2f": ([8, 4096, D], F32),
    "mask0": ([128, 1], F32), "identf": ([128, 128], F32),
    "freqs": ([128, 8], F32), "headsel": ([128, 8, 16], F32), "headexp": ([16, 8, 128], F32),
    "jslot": ([128, 512], F32), "wslot": ([128, 512], F32),
    "offs_d": ([1, 8], F32), "xn8T_d": ([D, 8], F32), "mrow_d": ([1, 512], F32),
    "u1tab": ([64, 512], F32), "u3tab": ([64, 512], F32), "hs_d": ([64, 512], F32),
    "acc1": ([D, T_SEQ], F32), "acc3": ([D, T_SEQ], F32), "h2_d": ([D, T_SEQ], BF16), "xnb_d": ([D, T_SEQ], BF16), "dbg": ([128, 96], F32), "dbg2": ([9, 512], F32),
    "xn1f_d": ([D, T_SEQ], F32), "xn1b_d": ([D, T_SEQ], BF16), "crow_d": ([8, T_SEQ], F32),
    "outT": ([D, T_SEQ], F32), "dbgA": ([D, T_SEQ], F32),
}


def dram_io(nc, which):
    kinds = {"in": "ExternalInput", "out": "ExternalOutput", "int": "Internal"}
    io = {}
    for nm, kind in which.items():
        sh, dt = DRAM_SHAPES[nm]
        io[nm] = nc.dram_tensor(nm, sh, dt, kind=kinds[kind]).ap()
    return io


P0_SKIP = set()
P0_DBG_TC = -1

def build_p0(nc, ec, block, io, st):
    S = lambda name, shape, dt=F32: st.enter_context(nc.sbuf_tensor(name, shape, dt))
    P = lambda name, shape: st.enter_context(nc.psum_tensor(name, shape, F32))
    em = ec.em

    xa = [S("p0_xa0", [128, 8, 512]), S("p0_xa1", [128, 8, 512])]
    sq = [S("p0_sq0", [128, 8, 512]), S("p0_sq1", [128, 8, 512])]
    lhsA = S("p0_lhsA", [128, 72])
    ones_r = S("p0_onesr", [1, 128])
    identf = S("p0_id", [128, 128])
    bias_b = S("p0_biasb", [128, 8])
    iota8 = S("p0_iota8", [128, 8])
    wgs = S("p0_wgs", [9, 1])
    m_row = S("p0_m", [1, 512])
    v_row = S("p0_v", [1, 512])
    rv_row = S("p0_rv", [1, 512])
    mb8 = S("p0_mb8", [8, 512])
    lt = S("p0_lt", [128, 4, 8])
    scores = S("p0_scores", [128, 4, 8])
    sp = S("p0_sp", [128, 4, 8])
    tmp8 = S("p0_tmp8", [128, 4, 8])
    keep = S("p0_keep", [128, 4, 8])
    k1 = S("p0_k1", [128, 4, 8])
    k2 = S("p0_k2", [128, 4, 8])
    gs = S("p0_gs", [128, 4, 4])
    gs2 = S("p0_gs2", [128, 4, 4])
    red = S("p0_red", [128, 4])
    t9 = S("p0_t9", [9, 512])
    cnt = S("p0_cnt", [128, 8])
    jstage = S("p0_jstage", [128, 512])
    wstage = S("p0_wstage", [128, 512])
    xn8 = S("p0_xn8", [128, 8, 8])
    mb128 = S("p0_mb128", [128, 8])
    rvb128 = S("p0_rvb128", [128, 8])
    cnt_row = S("p0_cntrow", [1, 8])
    dbgt = S("p0_dbgt", [128, 96])
    offs_r = S("p0_offsr", [1, 8])
    zero8 = S("p0_zero8", [1, 8])

    def redb(n):
        return red[:].rearrange("p (s o) -> p s o", o=1).broadcast_to([128, 4, n])

    psA = [P("p0_psA0", [9, 512]), P("p0_psA1", [9, 512])]
    psB = [P("p0_psB0", [1, 512]), P("p0_psB1", [1, 512])]
    psT = P("p0_psT", [128, 4, 8])
    psb = P("p0_psb", [9, 512])
    psb2 = P("p0_psb2", [8, 512])

    def q_setup(q):
        em("q", q.dma_start(out=lhsA[:], in_=io["lhsA"]), dma=True)
        em("q", q.dma_start(out=identf[:], in_=io["identf"]), dma=True)
        em("q", q.dma_start(out=bias_b[0:1, :], in_=io["bias_row"]), dma=True)
    block.sync(q_setup)

    def g_setup(g):
        em("g", g.iota(iota8[:], pattern=[[1, 8]], base=0, channel_multiplier=0,
                       allow_small_or_imprecise_dtypes=True))
        em("g", g.memset(cnt[:], 0.0))
        em("g", g.memset(zero8[:], 0.0))
        em("g", g.memset(ones_r[:], 1.0))
    block.gpsimd(g_setup)

    ones_c = S("p0_onesc", [128, 1])

    def g_setup2(g):
        em("g", g.memset(ones_c[:], 1.0))
    block.gpsimd(g_setup2)

    def t_setup2(t):
        ec.fence(t, ["q", "g"])
        for ci in range(8):
            em("t", t.matmul(psb[0:9, 0:1], lhsT=lhsA[:, 9 * ci:9 * ci + 9],
                             rhs=ones_c[:], start=(ci == 0), stop=(ci == 7)))
        em("t", t.matmul(psT[:, 0, :], lhsT=ones_r[:], rhs=bias_b[0:1, :],
                         start=True, stop=True))
    block.tensor(t_setup2)

    def v_setup(v):
        ec.fence(v, ["t"])
        em("v", v.tensor_copy(out=wgs[:], in_=psb[0:9, 0:1]))
        em("v", v.tensor_copy(out=bias_b[:], in_=psT[:, 0, :]))
    block.vector(v_setup)

    for tc in range(64):
        b = tc % 2
        xa_t, sq_t, pA, pB = xa[b], sq[b], psA[b], psB[b]

        def q_load(q, tc=tc, xa_t=xa_t):
            ec.fence(q, ["t", "s"] + (["v"] if tc == 2 else []))
            em("q", q.dma_start(
                out=xa_t[:],
                in_=io["xaT"][:, tc * 512:(tc + 1) * 512].rearrange(
                    "(c p) t -> p c t", p=128)), dma=True)
        block.sync(q_load)

        def s_sq(s, xa_t=xa_t, sq_t=sq_t):
            ec.fence(s, ["q", "t"])
            em("s", s.activation(out=sq_t[:], in_=xa_t[:], func=AF.Square))
        block.scalar(s_sq)

        def t_mm(t, xa_t=xa_t, sq_t=sq_t, pA=pA, pB=pB):
            ec.fence(t, ["q", "s", "v"])
            for ci in range(8):
                em("t", t.matmul(pA[:], lhsT=lhsA[:, 9 * ci:9 * ci + 9],
                                 rhs=xa_t[:, ci, :], start=(ci == 0), stop=(ci == 7)))
            for ci in range(8):
                em("t", t.matmul(pB[:], lhsT=ones_c[:], rhs=sq_t[:, ci, :],
                                 start=(ci == 0), stop=(ci == 7)))
        block.tensor(t_mm)

        def v_stat(v, pA=pA, pB=pB):
            ec.fence(v, ["t", "g"])
            em("v", v.tensor_copy(out=t9[:], in_=pA[:]))
        block.vector(v_stat)

        def g_mrow(g):
            ec.fence(g, ["v"])
            if "mrow" in P0_SKIP:
                em("g", g.memset(m_row[:], 0.001))
            else:
                em("g", g.dma_start(out=io["mrow_d"], in_=t9[8:9, :]), dma=True)
                ec.fence(g, ["g"])
                em("g", g.dma_start(out=m_row[:], in_=io["mrow_d"]), dma=True)
        block.gpsimd(g_mrow)

        def v_stat2(v, pB=pB):
            ec.fence(v, ["g"])
            em("v", v.tensor_scalar_mul(out=m_row[:], in0=m_row[:], scalar1=1.0 / D))
            ec.selfsync("v", v)
            em("v", v.tensor_mul(out=v_row[:], in0=m_row[:], in1=m_row[:]))
            ec.selfsync("v", v)
            em("v", v.scalar_tensor_tensor(out=v_row[:], in0=pB[:], scalar=1.0 / D,
                                           in1=v_row[:], op0=ALU.mult, op1=ALU.subtract))
        block.vector(v_stat2)

        def v_eps(v):
            em("v", v.tensor_scalar_add(out=v_row[:], in0=v_row[:], scalar1=1e-5))
        block.vector(v_eps)

        def s_rstd(s):
            ec.fence(s, ["v"])
            em("s", s.activation(out=rv_row[:], in_=v_row[:], func=AF.Sqrt))
        block.scalar(s_rstd)

        def v_rv(v):
            ec.fence(v, ["s"])
            em("v", v.reciprocal(out=rv_row[:], in_=rv_row[:]))
        block.vector(v_rv)

        def t_b8(t):
            ec.fence(t, ["v"])
            em("t", t.matmul(psb[0:8, :], lhsT=ones_r[0:1, 0:8], rhs=m_row[:],
                             start=True, stop=True))
            em("t", t.matmul(psb2[:], lhsT=ones_r[0:1, 0:8], rhs=rv_row[:],
                             start=True, stop=True))
        block.tensor(t_b8)

        def v_lg(v):
            ec.fence(v, ["t"])
            em("v", v.tensor_copy(out=mb8[:], in_=psb[0:8, :]))
            ec.selfsync("v", v)
            em("v", v.tensor_scalar_mul(out=mb8[:], in0=mb8[:], scalar1=wgs[0:8, :]))
            ec.selfsync("v", v)
            em("v", v.tensor_sub(out=t9[0:8, :], in0=t9[0:8, :], in1=mb8[:]))
            ec.selfsync("v", v)
            em("v", v.tensor_mul(out=t9[0:8, :], in0=t9[0:8, :], in1=psb2[:]))
        block.vector(v_lg)

        if tc == P0_DBG_TC:
            def g_dbg(g):
                ec.fence(g, ["v"])
                em("g", g.dma_start(out=io["dbg2"], in_=t9[:]), dma=True)
            block.gpsimd(g_dbg)

        if tc == 0:
            def t_x8(t):
                ec.fence(t, ["v"])
                em("t", t.matmul(psT[:, 0, :], lhsT=ones_r[:], rhs=m_row[0:1, 0:8],
                                 start=True, stop=True))
            block.tensor(t_x8)

            def v_x8(v):
                ec.fence(v, ["t"])
                em("v", v.tensor_copy(out=mb128[:], in_=psT[:, 0, :]))
            block.vector(v_x8)

            def t_x8b(t):
                ec.fence(t, ["v"])
                em("t", t.matmul(psT[:, 0, :], lhsT=ones_r[:], rhs=rv_row[0:1, 0:8],
                                 start=True, stop=True))
            block.tensor(t_x8b)

            def v_x8b(v, xa_t=xa_t):
                ec.fence(v, ["t"])
                em("v", v.tensor_copy(out=rvb128[:], in_=psT[:, 0, :]))
                ec.selfsync("v", v)
                for ci in range(8):
                    em("v", v.tensor_sub(out=xn8[:, ci, :], in0=xa_t[:, ci, 0:8],
                                         in1=mb128[:]))
                ec.selfsync("v", v)
                for ci in range(8):
                    em("v", v.tensor_mul(out=xn8[:, ci, :], in0=xn8[:, ci, :],
                                         in1=rvb128[:]))
            block.vector(v_x8b)

            def g_x8(g):
                ec.fence(g, ["v"])
                if "xn8" not in P0_SKIP:
                    em("g", g.dma_start(out=io["xn8T_d"].rearrange("(c p) e -> p c e", p=128),
                                        in_=xn8[:]), dma=True)
                else:
                    em("g", g.memset(xn8[:, 0, :], 0.0))
            block.gpsimd(g_x8)

        def t_tr(t):
            ec.fence(t, ["v"])
            for scn in range(4):
                em("t", t.transpose(psT[:, scn, :], in_=t9[0:8, scn * 128:(scn + 1) * 128],
                                    identity=identf[0:8, 0:8]))
        block.tensor(t_tr)

        def v_route(v, tc=tc):
            ec.fence(v, ["t"])
            em("v", v.tensor_copy(out=lt[:], in_=psT[:]))
            ec.selfsync("v", v)
            em("v", v.reduce_max(out=red[:], in_=lt[:], axis=AX.X))
            ec.selfsync("v", v)
            em("v", v.tensor_sub(out=sp[:], in0=lt[:], in1=redb(8)))
        block.vector(v_route)

        def s_exp(s):
            ec.fence(s, ["v"])
            em("s", s.activation(out=sp[:], in_=sp[:], func=AF.Exp))
        block.scalar(s_exp)

        def v_route2(v, tc=tc):
            ec.fence(v, ["s"])
            def Y():
                ec.selfsync("v", v)
            em("v", v.reduce_sum(out=red[:], in_=sp[:], axis=AX.X)); Y()
            em("v", v.reciprocal(out=red[:], in_=red[:])); Y()
            em("v", v.tensor_mul(out=scores[:], in0=sp[:], in1=redb(8))); Y()
            em("v", v.tensor_add(out=sp[:], in0=scores[:],
                                 in1=bias_b[:].rearrange("p (o e) -> p o e", o=1).broadcast_to([128, 4, 8]))); Y()
            em("v", v.scalar_tensor_tensor(out=sp[:], in0=iota8[:].rearrange("p (o e) -> p o e", o=1).broadcast_to([128, 4, 8]),
                                           scalar=-TIE, in1=sp[:], op0=ALU.mult, op1=ALU.add)); Y()
            spv = sp[:].rearrange("p s (g two) -> p s g two", two=2)
            em("v", v.tensor_add(out=gs[:], in0=spv[:, :, :, 0], in1=spv[:, :, :, 1])); Y()
            em("v", v.reduce_max(out=red[:], in_=gs[:], axis=AX.X)); Y()
            em("v", v.tensor_tensor(out=gs2[:], in0=gs[:], in1=redb(4), op=ALU.is_ge)); Y()
            em("v", v.scalar_tensor_tensor(out=gs[:], in0=gs2[:], scalar=-1e4,
                                           in1=gs[:], op0=ALU.mult, op1=ALU.add)); Y()
            em("v", v.reduce_max(out=red[:], in_=gs[:], axis=AX.X)); Y()
            em("v", v.tensor_tensor(out=gs[:], in0=gs[:], in1=redb(4), op=ALU.is_ge)); Y()
            em("v", v.tensor_add(out=gs[:], in0=gs[:], in1=gs2[:])); Y()
            kv = keep[:].rearrange("p s (g two) -> p s g two", two=2)
            em("v", v.tensor_copy(out=kv[:, :, :, 0], in_=gs[:]))
            em("v", v.tensor_copy(out=kv[:, :, :, 1], in_=gs[:])); Y()
            em("v", v.tensor_scalar_sub(out=tmp8[:], in0=keep[:], scalar1=1.0)); Y()
            em("v", v.scalar_tensor_tensor(out=tmp8[:], in0=tmp8[:], scalar=1e4,
                                           in1=sp[:], op0=ALU.mult, op1=ALU.add)); Y()
            em("v", v.reduce_max(out=red[:], in_=tmp8[:], axis=AX.X)); Y()
            em("v", v.tensor_tensor(out=k1[:], in0=tmp8[:], in1=redb(8), op=ALU.is_ge)); Y()
            em("v", v.scalar_tensor_tensor(out=tmp8[:], in0=k1[:], scalar=-2e4,
                                           in1=tmp8[:], op0=ALU.mult, op1=ALU.add)); Y()
            em("v", v.reduce_max(out=red[:], in_=tmp8[:], axis=AX.X)); Y()
            em("v", v.tensor_tensor(out=k2[:], in0=tmp8[:], in1=redb(8), op=ALU.is_ge)); Y()
            iob = iota8[:].rearrange("p (o e) -> p o e", o=1).broadcast_to([128, 4, 8])
            em("v", v.tensor_mul(out=tmp8[:], in0=k1[:], in1=iob)); Y()
            em("v", v.reduce_sum(out=jstage[:, 0 * 256 + tc * 4: 0 * 256 + tc * 4 + 4],
                                 in_=tmp8[:], axis=AX.X)); Y()
            em("v", v.tensor_mul(out=tmp8[:], in0=k2[:], in1=iob)); Y()
            em("v", v.reduce_sum(out=jstage[:, 256 + tc * 4: 256 + tc * 4 + 4],
                                 in_=tmp8[:], axis=AX.X)); Y()
            em("v", v.tensor_mul(out=tmp8[:], in0=k1[:], in1=scores[:])); Y()
            em("v", v.reduce_sum(out=wstage[:, 0 * 256 + tc * 4: 0 * 256 + tc * 4 + 4],
                                 in_=tmp8[:], axis=AX.X)); Y()
            em("v", v.tensor_mul(out=tmp8[:], in0=k2[:], in1=scores[:])); Y()
            em("v", v.reduce_sum(out=wstage[:, 256 + tc * 4: 256 + tc * 4 + 4],
                                 in_=tmp8[:], axis=AX.X)); Y()
            em("v", v.tensor_add(out=tmp8[:], in0=k1[:], in1=k2[:])); Y()
            for scn in range(4):
                em("v", v.tensor_add(out=cnt[:], in0=cnt[:], in1=tmp8[:, scn, :])); Y()
        block.vector(v_route2)

    def t_cnt(t):
        ec.fence(t, ["v"])
        em("t", t.matmul(psb[0:1, 0:8], lhsT=ones_c[:], rhs=cnt[:],
                         start=True, stop=True))
    block.tensor(t_cnt)

    def v_offs(v):
        ec.fence(v, ["t"])
        em("v", v.tensor_copy(out=cnt_row[:], in_=psb[0:1, 0:8]))
        ec.selfsync("v", v)
        em("v", v.tensor_tensor_scan(out=offs_r[:], data0=cnt_row[:], data1=zero8[:],
                                     initial=0.0, op0=ALU.add, op1=ALU.add))
        ec.selfsync("v", v)
    block.vector(v_offs)

    def g_fin(g):
        ec.fence(g, ["v"])
        if P0_DBG_TC >= 0:
            em("g", g.dma_start(out=io["dbg"], in_=dbgt[:]), dma=True)
        em("g", g.dma_start(out=io["offs_d"], in_=offs_r[:]), dma=True)
        em("g", g.dma_start(out=io["jslot"], in_=jstage[:]), dma=True)
        em("g", g.dma_start(out=io["wslot"], in_=wstage[:]), dma=True)
    block.gpsimd(g_fin)



P1A_DUMP = False

def build_p1a(nc, ec, block, io, st):
    S = lambda name, shape, dt=F32: st.enter_context(nc.sbuf_tensor(name, shape, dt))
    P = lambda name, shape: st.enter_context(nc.psum_tensor(name, shape, F32))
    em = ec.em
    NG = 8

    xg = S("a_xg", [128, 8, 512])
    BS = S("a_bs", [128, 8, 512])
    acc = S("a_acc", [128, 8, 512])
    xnb = S("a_xnb", [128, 8, 512], BF16)
    h = S("a_h", [128, 8, 515], BF16)
    xc = S("a_xc", [128, 8, 512], BF16)
    qkb = S("a_qkb", [128, 8, 512], BF16)
    bsq = S("a_bsq", [128, 8, 512], BF16)
    kb = S("a_kb", [128, 8, 513], BF16)
    vb = S("a_vb", [128, 8, 512], BF16)
    gb = S("a_gb", [128, 8, 512], BF16)
    outm = S("a_outm", [128, 8, 512], BF16)
    aob = vb
    h2b = gb
    Ab = [S("a_A%d" % i, [128, 512]) for i in range(2)]
    Bb = [S("a_B%d" % i, [128, 512]) for i in range(2)]
    Cb = [S("a_C%d" % i, [128, 512]) for i in range(2)]
    Db = [S("a_D%d" % i, [128, 512]) for i in range(2)]
    Ez = [S("a_Ez%d" % i, [128, 512]) for i in range(2)]
    Fz = [S("a_Fz%d" % i, [128, 512]) for i in range(2)]
    F = S("a_F", [128, 512])
    G = S("a_G", [128, 512])
    zeros512 = S("a_z512", [128, 512])
    tio = S("a_tio", [128, 512])
    wld = [S("a_wld0", [128, 8, 256]), S("a_wld1", [128, 8, 256])]
    wbf = [S("a_wbf0", [128, 8, 256], BF16), S("a_wbf1", [128, 8, 256], BF16)]
    mb = S("a_mb", [128, 512]); rvb = S("a_rvb", [128, 512])
    m_row = S("a_mrow", [1, 512]); v_row = S("a_vrow", [1, 512]); rv_row = S("a_rvrow", [1, 512])
    ones_c = S("a_onesc", [128, 1]); ones_bf = S("a_onesbf", [128, 1], BF16)
    ones_r = S("a_onesr", [1, 128])
    ns = S("a_ns", [16, 512]); rnb16 = S("a_rnb", [16, 512], BF16)
    carry_h = S("a_carryh", [128, 8, 3], BF16)
    carry_k = S("a_carryk", [128, 8, 1], BF16)
    carry_s = S("a_carrys", [128, 8])
    par = {}
    for nm in ("ln1_w", "ln1_b", "conv_b", "g_b", "onorm_w", "onorm_b",
               "ln2_w", "ln2_b", "gamma1", "gammaA", "freqs"):
        par[nm] = S("a_" + nm, [128, 8])
    conv_w = S("a_convw", [128, 8, 4])
    hsel = S("a_hsel", [128, 8, 16], BF16)
    hexp = S("a_hexp", [16, 8, 128], BF16)
    hself = S("a_hself", [128, 8, 16])
    hexpf = S("a_hexpf", [16, 8, 128])
    lng = S("a_lng", [128, 8])
    g1w = S("a_g1w", [128, 8]); g1b = S("a_g1b", [128, 8])
    t8a = S("a_t8a", [128, 8]); t8b = S("a_t8b", [128, 8]); t8c = S("a_t8c", [128, 8])

    pms = [P("a_pm0", [1, 512]), P("a_pm1", [1, 512])]
    pbc = [P("a_pbc0", [128, 512]), P("a_pbc1", [128, 512])]
    pmm = [P("a_pmm0", [128, 512]), P("a_pmm1", [128, 512])]
    pn16 = P("a_pn16", [16, 512])

    def Y(v):
        ec.selfsync("v", v)

    def q_setup(q):
        for nm in par:
            em("q", q.dma_start(out=par[nm][:], in_=io[nm]), dma=True)
        em("q", q.dma_start(out=conv_w[:], in_=io["conv_w"]), dma=True)
        em("q", q.dma_start(out=hself[:], in_=io["headsel"]), dma=True)
        em("q", q.dma_start(out=hexpf[:], in_=io["headexp"]), dma=True)
    block.sync(q_setup)

    def g_setup(g):
        em("g", g.memset(ones_c[:], 1.0))
        em("g", g.memset(ones_bf[:], 1.0))
        em("g", g.memset(ones_r[:], 1.0))
        em("g", g.memset(zeros512[:], 0.0))
        em("g", g.memset(carry_h[:], 0.0))
        em("g", g.memset(carry_k[:], 0.0))
        em("g", g.memset(carry_s[:], 0.0))
        ec.fence(g, ["q"])
        em("g", g.tensor_copy(out=hsel[:], in_=hself[:]))
        em("g", g.tensor_copy(out=hexp[:], in_=hexpf[:]))
    block.gpsimd(g_setup)

    def v_lng(v):
        ec.fence(v, ["q", "g"])
        em("v", v.tensor_scalar_sub(out=t8a[:], in0=par["gammaA"][:], scalar1=1.0)); Y(v)
        em("v", v.tensor_scalar_add(out=t8b[:], in0=par["gammaA"][:], scalar1=1.0)); Y(v)
        em("v", v.reciprocal(out=t8b[:], in_=t8b[:])); Y(v)
        em("v", v.tensor_mul(out=t8b[:], in0=t8a[:], in1=t8b[:])); Y(v)
        em("v", v.tensor_mul(out=t8c[:], in0=t8b[:], in1=t8b[:])); Y(v)
        em("v", v.tensor_scalar(out=lng[:], in0=t8c[:], scalar1=1.0 / 9.0, scalar2=1.0 / 7.0,
                                op0=ALU.mult, op1=ALU.add)); Y(v)
        em("v", v.scalar_tensor_tensor(out=lng[:], in0=lng[:], scalar=1.0, in1=t8c[:],
                                       op0=ALU.mult, op1=ALU.mult)); Y(v)
        em("v", v.tensor_scalar_add(out=lng[:], in0=lng[:], scalar1=1.0 / 5.0)); Y(v)
        em("v", v.tensor_mul(out=lng[:], in0=lng[:], in1=t8c[:])); Y(v)
        em("v", v.tensor_scalar_add(out=lng[:], in0=lng[:], scalar1=1.0 / 3.0)); Y(v)
        em("v", v.tensor_mul(out=lng[:], in0=lng[:], in1=t8c[:])); Y(v)
        em("v", v.tensor_scalar_add(out=lng[:], in0=lng[:], scalar1=1.0)); Y(v)
        em("v", v.tensor_mul(out=lng[:], in0=lng[:], in1=t8b[:])); Y(v)
        em("v", v.tensor_scalar_mul(out=lng[:], in0=lng[:], scalar1=2.0)); Y(v)
    block.vector(v_lng)

    def s_lng(s):
        ec.fence(s, ["v"])
        em("s", s.activation(out=t8b[:], in_=par["gammaA"][:], func=AF.Ln))
    block.scalar(s_lng)

    def v_lng2(v):
        ec.fence(v, ["s"])
        em("v", v.tensor_tensor(out=t8a[:], in0=t8a[:], in1=t8a[:], op=ALU.mult)); Y(v)
        em("v", v.tensor_scalar(out=t8a[:], in0=t8a[:], scalar1=0.16, scalar2=None,
                                op0=ALU.is_gt)); Y(v)
        em("v", v.tensor_mul(out=t8b[:], in0=t8b[:], in1=t8a[:])); Y(v)
        em("v", v.tensor_scalar(out=t8a[:], in0=t8a[:], scalar1=-1.0, scalar2=1.0,
                                op0=ALU.mult, op1=ALU.add)); Y(v)
        em("v", v.tensor_mul(out=lng[:], in0=lng[:], in1=t8a[:])); Y(v)
        em("v", v.tensor_add(out=lng[:], in0=lng[:], in1=t8b[:])); Y(v)
        em("v", v.tensor_mul(out=g1w[:], in0=par["gamma1"][:], in1=par["onorm_w"][:]))
        em("v", v.tensor_mul(out=g1b[:], in0=par["gamma1"][:], in1=par["onorm_b"][:])); Y(v)
    block.vector(v_lng2)

    p1a_slot = [0]
    p1a_cast = {}
    p1a_mm = {}
    p1a_ev = {}
    p1a_evs = {}
    p1a_omark = [("t", 0)]
    for g_i in range(NG):
        t0 = g_i * 512

        def q_xg(q, t0=t0):
            ec.fence(q, ["v", "s"])
            em("q", q.dma_start(out=xg[:], in_=io["xT"][:, t0:t0 + 512].rearrange(
                "(c p) t -> p c t", p=128)), dma=True)
        block.sync(q_xg)

        def s_sq(s):
            ec.fence(s, ["q", "t"])
            em("s", s.activation(out=BS[:], in_=xg[:], func=AF.Square))
        block.scalar(s_sq)

        def t_stat(t):
            ec.fence(t, ["s", "v"])
            for ci in range(8):
                em("t", t.matmul(pms[0][:], lhsT=ones_c[:], rhs=xg[:, ci, :],
                                 start=(ci == 0), stop=(ci == 7)))
            for ci in range(8):
                em("t", t.matmul(pms[1][:], lhsT=ones_c[:], rhs=BS[:, ci, :],
                                 start=(ci == 0), stop=(ci == 7)))
        block.tensor(t_stat)

        def v_stat(v):
            ec.fence(v, ["t"])
            em("v", v.tensor_scalar_mul(out=m_row[:], in0=pms[0][:], scalar1=1.0 / D)); Y(v)
            em("v", v.tensor_mul(out=v_row[:], in0=m_row[:], in1=m_row[:])); Y(v)
            em("v", v.scalar_tensor_tensor(out=v_row[:], in0=pms[1][:], scalar=1.0 / D,
                                           in1=v_row[:], op0=ALU.mult, op1=ALU.subtract)); Y(v)
            em("v", v.tensor_scalar_add(out=v_row[:], in0=v_row[:], scalar1=1e-5))
        block.vector(v_stat)

        def s_rstd(s):
            ec.fence(s, ["v"])
            em("s", s.activation(out=rv_row[:], in_=v_row[:], func=AF.Sqrt))
        block.scalar(s_rstd)

        def v_rv(v):
            ec.fence(v, ["s"])
            em("v", v.reciprocal(out=rv_row[:], in_=rv_row[:]))
        block.vector(v_rv)

        def t_bc(t):
            ec.fence(t, ["v"])
            em("t", t.matmul(pbc[0][:], lhsT=ones_r[:], rhs=m_row[:], start=True, stop=True))
            em("t", t.matmul(pbc[1][:], lhsT=ones_r[:], rhs=rv_row[:], start=True, stop=True))
        block.tensor(t_bc)

        def v_xn(v):
            ec.fence(v, ["t"])
            em("v", v.tensor_copy(out=mb[:], in_=pbc[0][:]))
            em("v", v.tensor_copy(out=rvb[:], in_=pbc[1][:])); Y(v)
            for ci in range(8):
                em("v", v.tensor_sub(out=BS[:, ci, :], in0=xg[:, ci, :], in1=mb[:]))
            Y(v)
            for ci in range(8):
                em("v", v.tensor_mul(out=BS[:, ci, :], in0=BS[:, ci, :], in1=rvb[:]))
            Y(v)
            for ci in range(8):
                em("v", v.tensor_copy(out=xnb[:, ci, :], in_=BS[:, ci, :]))
            for ci in range(8):
                em("v", v.tensor_add(out=acc[:, ci, :], in0=xg[:, ci, :], in1=BS[:, ci, :]))
            Y(v)
        block.vector(v_xn)

        def g_xnb(g, t0=t0):
            ec.fence(g, ["v"])
            em("g", g.dma_start(out=io["xnb_d"][:, t0:t0 + 512].rearrange(
                "(c p) t -> p c t", p=128), in_=xnb[:]), dma=True)
        block.gpsimd(g_xnb)

        def ln_pass(src_bf, dst, wname, bname, halo, extra_scale=None, extra_bias=None,
                    dst_off=0):
            def s_sq2(s):
                ec.fence(s, ["v", "t"])
                em("s", s.activation(out=bsq[:], in_=src_bf[:], func=AF.Square))
            block.scalar(s_sq2)

            def t_st(t):
                ec.fence(t, ["s", "v"])
                for ci in range(8):
                    em("t", t.matmul(pms[0][:], lhsT=ones_bf[:], rhs=src_bf[:, ci, :],
                                     start=(ci == 0), stop=(ci == 7)))
                for ci in range(8):
                    em("t", t.matmul(pms[1][:], lhsT=ones_bf[:], rhs=bsq[:, ci, :],
                                     start=(ci == 0), stop=(ci == 7)))
            block.tensor(t_st)

            def v_st(v):
                ec.fence(v, ["t"])
                em("v", v.tensor_scalar_mul(out=m_row[:], in0=pms[0][:], scalar1=1.0 / D)); Y(v)
                em("v", v.tensor_mul(out=v_row[:], in0=m_row[:], in1=m_row[:])); Y(v)
                em("v", v.scalar_tensor_tensor(out=v_row[:], in0=pms[1][:], scalar=1.0 / D,
                                               in1=v_row[:], op0=ALU.mult, op1=ALU.subtract)); Y(v)
                em("v", v.tensor_scalar_add(out=v_row[:], in0=v_row[:], scalar1=1e-5))
            block.vector(v_st)

            def s_r(s):
                ec.fence(s, ["v"])
                em("s", s.activation(out=rv_row[:], in_=v_row[:], func=AF.Sqrt))
            block.scalar(s_r)

            def v_r(v):
                ec.fence(v, ["s"])
                em("v", v.reciprocal(out=rv_row[:], in_=rv_row[:]))
            block.vector(v_r)

            def t_b(t):
                ec.fence(t, ["v"])
                em("t", t.matmul(pbc[0][:], lhsT=ones_r[:], rhs=m_row[:], start=True, stop=True))
                em("t", t.matmul(pbc[1][:], lhsT=ones_r[:], rhs=rv_row[:], start=True, stop=True))
            block.tensor(t_b)

            def v_a(v):
                ec.fence(v, ["t"])
                em("v", v.tensor_copy(out=mb[:], in_=pbc[0][:]))
                em("v", v.tensor_copy(out=rvb[:], in_=pbc[1][:])); Y(v)
                for ci in range(8):
                    em("v", v.tensor_sub(out=BS[:, ci, :], in0=src_bf[:, ci, :], in1=mb[:]))
                Y(v)
                for ci in range(8):
                    em("v", v.tensor_mul(out=BS[:, ci, :], in0=BS[:, ci, :], in1=rvb[:]))
                Y(v)
                wA = par[wname] if extra_scale is None else extra_scale
                bA = par[bname] if extra_bias is None else extra_bias
                for ci in range(8):
                    em("v", v.tensor_scalar(out=dst[:, ci, dst_off:dst_off + 512],
                                            in0=BS[:, ci, :],
                                            scalar1=wA[:, ci:ci + 1], scalar2=bA[:, ci:ci + 1],
                                            op0=ALU.mult, op1=ALU.add))
                Y(v)
            block.vector(v_a)

        ln_pass(xnb, h, "ln1_w", "ln1_b", halo=True, dst_off=3)

        def v_conv(v, g_i=g_i):
            for ci in range(8):
                em("v", v.tensor_copy(out=h[:, ci, 0:3], in_=carry_h[:, ci, :]))
            Y(v)
            for ci in range(8):
                em("v", v.tensor_copy(out=carry_h[:, ci, :], in_=h[:, ci, 512:515]))
            Y(v)
            for ci in range(8):
                em("v", v.tensor_scalar_mul(out=BS[:, ci, :], in0=h[:, ci, 0:512],
                                            scalar1=conv_w[:, ci, 0:1]))
            Y(v)
            for i in range(1, 4):
                for ci in range(8):
                    em("v", v.scalar_tensor_tensor(out=BS[:, ci, :], in0=h[:, ci, i:i + 512],
                                                   scalar=conv_w[:, ci, i:i + 1],
                                                   in1=BS[:, ci, :], op0=ALU.mult, op1=ALU.add))
                Y(v)
            for ci in range(8):
                em("v", v.tensor_scalar(out=xc[:, ci, :], in0=BS[:, ci, :],
                                        scalar1=par["conv_b"][:, ci:ci + 1], scalar2=None,
                                        op0=ALU.add))
            Y(v)
        block.vector(v_conv)
        xc_mark = ec.mark("v")

        def proj(wkey, out_tile, kind, g_i=g_i):
            for ob in range(4):
                b = p1a_slot[0] % 2
                wb = wld[b]; wf = wbf[b]

                def q_w(q, wkey=wkey, ob=ob, wb=wb, b=b):
                    ec.wait(q, [p1a_cast.get(b, ("s", 0))])
                    em("q", q.dma_start(out=wb[:], in_=io[wkey][:, ob * 256:(ob + 1) * 256].rearrange(
                        "(c p) o -> p c o", p=128)), dma=True)
                block.sync(q_w)
                ldm = ec.mark("q")

                def s_cast(sx, wb=wb, wf=wf, b=b, ldm=ldm):
                    ec.wait(sx, [ldm, p1a_mm.get(b, ("t", 0))])
                    em("s", sx.copy(out=wf[:], in_=wb[:]))
                block.scalar(s_cast)
                p1a_cast[b] = ec.mark("s")

                for oc in range(2):
                    ppi = oc % 2
                    pp = pmm[ppi]

                    def t_p(t, wf=wf, oc=oc, pp=pp, ppi=ppi, b=b):
                        ec.wait(t, [p1a_cast[b], p1a_ev.get(ppi, ("v", 0)),
                                    p1a_evs.get(ppi, ("s", 0)), xc_mark])
                        for ci in range(8):
                            em("t", t.matmul(pp[:], lhsT=wf[:, ci, oc * 128:(oc + 1) * 128],
                                             rhs=xc[:, ci, :], start=(ci == 0), stop=(ci == 7)))
                    block.tensor(t_p)
                    gm = ec.mark("t")

                    occ = ob * 2 + oc
                    if kind == "qk":
                        def v_e(v, pp=pp, occ=occ, gm=gm):
                            ec.wait(v, [gm])
                            em("v", v.tensor_copy(out=qkb[:, occ, :], in_=pp[:]))
                        block.vector(v_e)
                        p1a_ev[ppi] = ec.mark("v")
                        vm = p1a_ev[ppi]

                        def s_sqr(sx, occ=occ, vm=vm):
                            ec.wait(sx, [vm])
                            em("s", sx.activation(out=bsq[:, occ, :], in_=qkb[:, occ, :],
                                                  func=AF.Square))
                        block.scalar(s_sqr)
                        p1a_evs[ppi] = ec.mark("s")
                    elif kind == "v":
                        def v_e(v, pp=pp, occ=occ, gm=gm):
                            ec.wait(v, [gm])
                            em("v", v.tensor_copy(out=vb[:, occ, :], in_=pp[:]))
                        block.vector(v_e)
                        p1a_ev[ppi] = ec.mark("v")
                    else:
                        def s_e(sx, pp=pp, occ=occ, gm=gm):
                            ec.wait(sx, [gm])
                            em("s", sx.activation(out=gb[:, occ, :], in_=pp[:], func=AF.Sigmoid,
                                                  bias=par["g_b"][:, occ:occ + 1]))
                        block.scalar(s_e)
                        p1a_evs[ppi] = ec.mark("s")
                p1a_mm[b] = ec.mark("t")
                p1a_slot[0] += 1

        def s_gf(sx):
            ec.fence(sx, ["g"])
        block.scalar(s_gf)

        def v_gf(v):
            ec.fence(v, ["g"])
        block.vector(v_gf)

        proj("qk_wT", qkb, "qk")
        proj("v_wT", vb, "v")
        proj("g_wT", gb, "g")

        def t_nrm(t):
            ec.fence(t, ["s", "v"])
            for ci in range(8):
                em("t", t.matmul(pn16[:], lhsT=hsel[:, ci, :], rhs=bsq[:, ci, :],
                                 start=(ci == 0), stop=(ci == 7)))
        block.tensor(t_nrm)

        def s_nrm(s):
            ec.fence(s, ["t"])
            em("s", s.activation(out=ns[:], in_=pn16[:], func=AF.Sqrt))
        block.scalar(s_nrm)

        def v_nrm(v):
            ec.fence(v, ["s"])
            em("v", v.tensor_scalar_max(out=ns[:], in0=ns[:], scalar1=1e-12)); Y(v)
            em("v", v.reciprocal(out=ns[:], in_=ns[:])); Y(v)
            em("v", v.tensor_copy(out=rnb16[:], in_=ns[:])); Y(v)
        block.vector(v_nrm)

        def v_kcar(v):
            for ci in range(8):
                em("v", v.tensor_copy(out=kb[:, ci, 0:1], in_=carry_k[:, ci, :]))
            Y(v)
        block.vector(v_kcar)

        for ci in range(8):
            pp = pmm[ci % 2]

            def t_kb(t, ci=ci, pp=pp):
                ec.fence(t, ["v"])
                em("t", t.matmul(pp[:], lhsT=hexp[:, ci, :], rhs=rnb16[:], start=True, stop=True))
            block.tensor(t_kb)

            def v_kb(v, ci=ci, pp=pp):
                ec.fence(v, ["t"])
                em("v", v.tensor_mul(out=kb[:, ci, 1:513], in0=qkb[:, ci, :], in1=pp[:]))
            block.vector(v_kb)

        def v_kcar2(v):
            Y(v)
            for ci in range(8):
                em("v", v.tensor_copy(out=carry_k[:, ci, :], in_=kb[:, ci, 512:513]))
            Y(v)
        block.vector(v_kcar2)

        def g_tio(g, t0=t0):
            ec.fence(g, ["v"])
            em("g", g.iota(tio[:], pattern=[[1, 512]], base=t0, channel_multiplier=0,
                           allow_small_or_imprecise_dtypes=True))
        block.gpsimd(g_tio)
        tio_mark = ec.mark("g")

        vang_mark = {}; smark = {}; gz_mark = {}; vscan_mark = {}
        for ci in range(8):
            b = ci % 2

            def v_ang(v, ci=ci, b=b):
                ec.wait(v, [smark.get(b, ("s", 0)), vscan_mark.get(b, ("v", 0)), tio_mark])
                em("v", v.tensor_scalar_mul(out=Ab[b][:], in0=tio[:],
                                            scalar1=par["freqs"][:, ci:ci + 1])); Y(v)
                em("v", v.tensor_scalar(out=Bb[b][:], in0=Ab[b][:], scalar1=INV_2PI,
                                        scalar2=MAGIC, op0=ALU.mult, op1=ALU.add)); Y(v)
                em("v", v.tensor_scalar_sub(out=Bb[b][:], in0=Bb[b][:], scalar1=MAGIC)); Y(v)
                em("v", v.scalar_tensor_tensor(out=Ab[b][:], in0=Bb[b][:], scalar=-CW1,
                                               in1=Ab[b][:], op0=ALU.mult, op1=ALU.add)); Y(v)
                em("v", v.scalar_tensor_tensor(out=Ab[b][:], in0=Bb[b][:], scalar=-CW2,
                                               in1=Ab[b][:], op0=ALU.mult, op1=ALU.add)); Y(v)
                em("v", v.scalar_tensor_tensor(out=Ab[b][:], in0=Bb[b][:], scalar=-CW3,
                                               in1=Ab[b][:], op0=ALU.mult, op1=ALU.add)); Y(v)
                em("v", v.tensor_scalar_add(out=Ab[b][:], in0=Ab[b][:], scalar1=HALF_PI)); Y(v)
                em("v", v.tensor_scalar(out=Bb[b][:], in0=Ab[b][:], scalar1=PI_F,
                                        scalar2=-TWO_PI, op0=ALU.is_gt, op1=ALU.mult)); Y(v)
                em("v", v.tensor_add(out=Ab[b][:], in0=Ab[b][:], in1=Bb[b][:])); Y(v)
                em("v", v.tensor_scalar_mul(out=Db[b][:], in0=tio[:],
                                            scalar1=lng[:, ci:ci + 1]))
            block.vector(v_ang)
            vang_mark[b] = ec.mark("v")

            def s_cd(sx, ci=ci, b=b):
                ec.wait(sx, [vang_mark[b], gz_mark.get(b, ("g", 0)),
                             vscan_mark.get(b, ("v", 0))])
                em("s", sx.activation(out=Cb[b][:], in_=Ab[b][:], func=AF.Sin))
                em("s", sx.activation(out=Db[b][:], in_=Db[b][:], func=AF.Exp))
            block.scalar(s_cd)
            smark[b] = ec.mark("s")

            def g_z(g, ci=ci, b=b):
                ec.wait(g, [smark[b], vscan_mark.get(b, ("v", 0))])
                if ci == 0:
                    ec.fence(g, ["v"])
                def Yg():
                    ec.selfsync("g", g)
                em("g", g.tensor_mul(out=Ez[b][:], in0=vb[:, ci, :], in1=Cb[b][:])); Yg()
                em("g", g.tensor_mul(out=Fz[b][:], in0=vb[:, ci, :], in1=kb[:, ci, 0:512])); Yg()
                em("g", g.tensor_add(out=Ez[b][:], in0=Ez[b][:], in1=Fz[b][:])); Yg()
                em("g", g.tensor_mul(out=Ez[b][:], in0=Ez[b][:], in1=gb[:, ci, :])); Yg()
                em("g", g.tensor_mul(out=Ez[b][:], in0=Ez[b][:], in1=Db[b][:]))
            block.gpsimd(g_z)
            gz_mark[b] = ec.mark("g")

            def v_scan(v, ci=ci, b=b):
                ec.wait(v, [gz_mark[b], p1a_omark[0]])
                em("v", v.tensor_tensor_scan(out=G[:], data0=Ez[b][:], data1=zeros512[:],
                                             initial=carry_s[:, ci:ci + 1],
                                             op0=ALU.add, op1=ALU.add)); Y(v)
                em("v", v.tensor_copy(out=carry_s[:, ci:ci + 1], in_=G[:, 511:512])); Y(v)
                em("v", v.tensor_scalar_add(out=F[:], in0=Db[b][:], scalar1=1e-6)); Y(v)
                em("v", v.reciprocal(out=F[:], in_=F[:])); Y(v)
                em("v", v.tensor_mul(out=G[:], in0=G[:], in1=F[:])); Y(v)
                em("v", v.tensor_mul(out=outm[:, ci, :], in0=G[:], in1=kb[:, ci, 1:513]))
            block.vector(v_scan)
            vscan_mark[b] = ec.mark("v")

        if P1A_DUMP:
            def g_dmp(g, t0=t0):
                ec.fence(g, ["v", "s"])
                em("g", g.dma_start(out=io["dbgA"][:, t0:t0 + 512].rearrange(
                    "(c p) t -> p c t", p=128), in_=outm[:]), dma=True)
            block.gpsimd(g_dmp)

        state_done = [vscan_mark.get(0, ("v", 0)), vscan_mark.get(1, ("v", 0)),
                      gz_mark.get(0, ("g", 0)), gz_mark.get(1, ("g", 0))]
        for ob in range(4):
            b = p1a_slot[0] % 2
            wb = wld[b]; wf = wbf[b]

            def q_w(q, ob=ob, wb=wb, b=b):
                ec.wait(q, [p1a_cast.get(b, ("s", 0))])
                em("q", q.dma_start(out=wb[:], in_=io["o_wT"][:, ob * 256:(ob + 1) * 256].rearrange(
                    "(c p) o -> p c o", p=128)), dma=True)
            block.sync(q_w)
            ldm = ec.mark("q")

            def s_cast(sx, wb=wb, wf=wf, b=b, ldm=ldm):
                ec.wait(sx, [ldm, p1a_mm.get(b, ("t", 0))])
                em("s", sx.copy(out=wf[:], in_=wb[:]))
            block.scalar(s_cast)
            p1a_cast[b] = ec.mark("s")

            for oc in range(2):
                ppi = oc % 2
                pp = pmm[ppi]

                def t_p(t, wf=wf, oc=oc, pp=pp, ppi=ppi, b=b):
                    ec.wait(t, [p1a_cast[b], p1a_ev.get(ppi, ("v", 0))] + state_done)
                    for ci in range(8):
                        em("t", t.matmul(pp[:], lhsT=wf[:, ci, oc * 128:(oc + 1) * 128],
                                         rhs=outm[:, ci, :], start=(ci == 0), stop=(ci == 7)))
                block.tensor(t_p)
                gm = ec.mark("t")

                occ = ob * 2 + oc

                def v_e(v, pp=pp, occ=occ, gm=gm):
                    ec.wait(v, [gm])
                    em("v", v.tensor_copy(out=aob[:, occ, :], in_=pp[:]))
                block.vector(v_e)
                p1a_ev[ppi] = ec.mark("v")
            p1a_mm[b] = ec.mark("t")
            p1a_slot[0] += 1
        p1a_omark[0] = ec.mark("t")

        def s_sq3(s):
            ec.fence(s, ["v"])
            em("s", s.activation(out=bsq[:], in_=aob[:], func=AF.Square))
        block.scalar(s_sq3)

        def t_st3(t):
            ec.fence(t, ["s", "v"])
            for ci in range(8):
                em("t", t.matmul(pms[0][:], lhsT=ones_bf[:], rhs=aob[:, ci, :],
                                 start=(ci == 0), stop=(ci == 7)))
            for ci in range(8):
                em("t", t.matmul(pms[1][:], lhsT=ones_bf[:], rhs=bsq[:, ci, :],
                                 start=(ci == 0), stop=(ci == 7)))
        block.tensor(t_st3)

        def v_st3(v):
            ec.fence(v, ["t"])
            em("v", v.tensor_scalar_mul(out=m_row[:], in0=pms[0][:], scalar1=1.0 / D)); Y(v)
            em("v", v.tensor_mul(out=v_row[:], in0=m_row[:], in1=m_row[:])); Y(v)
            em("v", v.scalar_tensor_tensor(out=v_row[:], in0=pms[1][:], scalar=1.0 / D,
                                           in1=v_row[:], op0=ALU.mult, op1=ALU.subtract)); Y(v)
            em("v", v.tensor_scalar_add(out=v_row[:], in0=v_row[:], scalar1=1e-5))
        block.vector(v_st3)

        def s_r3(s):
            ec.fence(s, ["v"])
            em("s", s.activation(out=rv_row[:], in_=v_row[:], func=AF.Sqrt))
        block.scalar(s_r3)

        def v_r3(v):
            ec.fence(v, ["s"])
            em("v", v.reciprocal(out=rv_row[:], in_=rv_row[:]))
        block.vector(v_r3)

        def t_b3(t):
            ec.fence(t, ["v"])
            em("t", t.matmul(pbc[0][:], lhsT=ones_r[:], rhs=m_row[:], start=True, stop=True))
            em("t", t.matmul(pbc[1][:], lhsT=ones_r[:], rhs=rv_row[:], start=True, stop=True))
        block.tensor(t_b3)

        def v_a3(v):
            ec.fence(v, ["t"])
            em("v", v.tensor_copy(out=mb[:], in_=pbc[0][:]))
            em("v", v.tensor_copy(out=rvb[:], in_=pbc[1][:])); Y(v)
            for ci in range(8):
                em("v", v.tensor_sub(out=BS[:, ci, :], in0=aob[:, ci, :], in1=mb[:]))
            Y(v)
            for ci in range(8):
                em("v", v.tensor_mul(out=BS[:, ci, :], in0=BS[:, ci, :], in1=rvb[:]))
            Y(v)
            for ci in range(8):
                em("v", v.tensor_scalar(out=BS[:, ci, :], in0=BS[:, ci, :],
                                        scalar1=g1w[:, ci:ci + 1], scalar2=g1b[:, ci:ci + 1],
                                        op0=ALU.mult, op1=ALU.add))
            Y(v)
            for ci in range(8):
                em("v", v.tensor_add(out=acc[:, ci, :], in0=acc[:, ci, :], in1=BS[:, ci, :]))
            Y(v)
            for ci in range(8):
                em("v", v.tensor_copy(out=qkb[:, ci, :], in_=acc[:, ci, :]))
            Y(v)
        block.vector(v_a3)

        def v_gf2(v):
            ec.fence(v, ["g"])
        block.vector(v_gf2)
        ln_pass(qkb, h2b, "ln2_w", "ln2_b", halo=False, dst_off=0)

        def g_store(g, t0=t0):
            ec.fence(g, ["v"])
            em("g", g.dma_start(out=io["h2_d"][:, t0:t0 + 512].rearrange(
                "(c p) t -> p c t", p=128), in_=h2b[:]), dma=True)
            em("g", g.dma_start(out=io["acc1"][:, t0:t0 + 512].rearrange(
                "(c p) t -> p c t", p=128), in_=acc[:]), dma=True)
        block.gpsimd(g_store)


def build_p1b(nc, ec, block, io, st):
    S = lambda name, shape, dt=F32: st.enter_context(nc.sbuf_tensor(name, shape, dt))
    P = lambda name, shape: st.enter_context(nc.psum_tensor(name, shape, F32))
    em = ec.em

    h2t = S("b_h2t", [128, 8, 1024], BF16)
    acc = S("b_acc", [128, 8, 1024])
    m = S("b_m", [128, 32, 1024], BF16)
    wld = [S("b_wld%d" % i, [128, 8, 512]) for i in range(3)]
    wbf = [S("b_wbf%d" % i, [128, 8, 512], BF16) for i in range(3)]
    g2w = S("b_g2w", [128, 8])
    pps = [P("b_pp%d" % i, [128, 512]) for i in range(8)]

    def q_setup(q):
        em("q", q.dma_start(out=g2w[:], in_=io["gamma2"]), dma=True)
    block.sync(q_setup)

    cast_mark = {}; mm_done = {}; evac_mark = {}
    NBUF = 3
    kslot = [0]

    def wblock(ap_w, mm_emit, cast_en):
        b = kslot[0] % NBUF
        wb, wf = wld[b], wbf[b]

        def q_w(q, b=b, ap_w=ap_w):
            ec.wait(q, [cast_mark.get(b, ("v", 0))])
            em("q", q.dma_start(out=wb[:], in_=ap_w), dma=True)
        block.sync(q_w)
        ldm = ec.mark("q")

        if cast_en == "v":
            def v_cast(v, b=b, ldm=ldm):
                ec.wait(v, [ldm, mm_done.get(b, ("t", 0))])
                em("v", v.tensor_copy(out=wf[:], in_=wb[:]))
            block.vector(v_cast)
            cast_mark[b] = ec.mark("v")
        else:
            def s_cast(sx, b=b, ldm=ldm):
                ec.wait(sx, [ldm, mm_done.get(b, ("t", 0))])
                em("s", sx.copy(out=wf[:], in_=wb[:]))
            block.scalar(s_cast)
            cast_mark[b] = ec.mark("s")
        mm_emit(wf, b)
        mm_done[b] = ec.mark("t")
        kslot[0] += 1

    for g_i in range(4):
        t0 = g_i * 1024

        def q_ld(q, t0=t0):
            ec.fence(q, ["v", "t", "g", "s", "q"])
            em("q", q.dma_start(out=h2t[:], in_=io["h2_d"][:, t0:t0 + 1024].rearrange(
                "(c p) t -> p c t", p=128)), dma=True)
            em("q", q.dma_start(out=acc[:], in_=io["acc1"][:, t0:t0 + 1024].rearrange(
                "(c p) t -> p c t", p=128)), dma=True)
        block.sync(q_ld)
        ldx = ec.mark("q")

        silu_marks = {}
        for fb in range(8):
            for which, wkey in ((0, "gate_wT"), (1, "up_wT")):
                ap_w = io[wkey][:, fb * 512:(fb + 1) * 512].rearrange("(c p) o -> p c o", p=128)

                def mm_gateup(wf, b, fb=fb, which=which):
                    for fc in range(4):
                        for t5 in range(2):
                            ppi = which * 2 + (fc * 2 + t5) % 2
                            pp = pps[ppi]

                            def t_p(t, wf=wf, fc=fc, t5=t5, pp=pp, ppi=ppi, b=b):
                                ec.wait(t, [cast_mark[b], ldx, evac_mark.get(ppi, ("v", 0))])
                                for ci in range(8):
                                    em("t", t.matmul(pp[:], lhsT=wf[:, ci, fc * 128:(fc + 1) * 128],
                                                     rhs=h2t[:, ci, t5 * 512:(t5 + 1) * 512],
                                                     start=(ci == 0), stop=(ci == 7)))
                            block.tensor(t_p)
                            gm = ec.mark("t")
                            fcg = fb * 4 + fc
                            if which == 0:
                                def s_silu(s, pp=pp, fcg=fcg, t5=t5, gm=gm):
                                    ec.wait(s, [gm])
                                    em("s", s.activation(out=m[:, fcg, t5 * 512:(t5 + 1) * 512],
                                                         in_=pp[:], func=AF.Silu))
                                block.scalar(s_silu)
                                evac_mark[ppi] = ec.mark("s")
                                silu_marks[(fcg, t5)] = evac_mark[ppi]
                            else:
                                def v_mul(v, pp=pp, fcg=fcg, t5=t5, gm=gm):
                                    ec.wait(v, [gm, silu_marks[(fcg, t5)]])
                                    em("v", v.tensor_mul(out=m[:, fcg, t5 * 512:(t5 + 1) * 512],
                                                         in0=m[:, fcg, t5 * 512:(t5 + 1) * 512],
                                                         in1=pp[:]))
                                block.vector(v_mul)
                                evac_mark[ppi] = ec.mark("v")

                wblock(ap_w, mm_gateup, "s" if which == 0 else "v")

        for db in range(2):
            for sub in range(4):
                ap_w = io["down_wT"][sub * 1024:(sub + 1) * 1024,
                                     db * 512:(db + 1) * 512].rearrange("(c p) o -> p c o", p=128)

                def mm_down(wf, b, db=db, sub=sub):
                    for dc in range(4):
                        for t5 in range(2):
                            ppi = dc * 2 + t5
                            pp = pps[ppi]

                            def t_p(t, wf=wf, dc=dc, t5=t5, pp=pp, ppi=ppi, sub=sub, b=b):
                                w8 = [cast_mark[b]]
                                if sub == 0:
                                    w8.append(evac_mark.get(ppi, ("v", 0)))
                                ec.wait(t, w8)
                                for kf in range(8):
                                    em("t", t.matmul(pp[:], lhsT=wf[:, kf, dc * 128:(dc + 1) * 128],
                                                     rhs=m[:, sub * 8 + kf, t5 * 512:(t5 + 1) * 512],
                                                     start=(kf == 0 and sub == 0),
                                                     stop=(kf == 7 and sub == 3)))
                            block.tensor(t_p)

                            if sub == 3:
                                gm = ec.mark("t")
                                dcg = db * 4 + dc

                                def v_acc(v, pp=pp, dcg=dcg, t5=t5, gm=gm):
                                    ec.wait(v, [gm])
                                    em("v", v.scalar_tensor_tensor(
                                        out=acc[:, dcg, t5 * 512:(t5 + 1) * 512],
                                        in0=pp[:], scalar=g2w[:, dcg:dcg + 1],
                                        in1=acc[:, dcg, t5 * 512:(t5 + 1) * 512],
                                        op0=ALU.mult, op1=ALU.add))
                                block.vector(v_acc)
                                evac_mark[ppi] = ec.mark("v")

                wblock(ap_w, mm_down, "v")

        def g_store(g, t0=t0):
            ec.fence(g, ["v"])
            em("g", g.dma_start(out=io["acc1"][:, t0:t0 + 1024].rearrange(
                "(c p) t -> p c t", p=128), in_=acc[:]), dma=True)
        block.gpsimd(g_store)


def build_p1c(nc, ec, block, io, st):
    S = lambda name, shape, dt=F32: st.enter_context(nc.sbuf_tensor(name, shape, dt))
    P = lambda name, shape: st.enter_context(nc.psum_tensor(name, shape, F32))
    em = ec.em

    xnt = S("c_xnt", [128, 8, 1024], BF16)
    acc = S("c_acc", [128, 8, 1024])
    sh = S("c_sh", [128, 8, 1024], BF16)
    wld = [S("c_wld%d" % i, [128, 8, 512]) for i in range(3)]
    wbf = [S("c_wbf%d" % i, [128, 8, 512], BF16) for i in range(3)]
    pps = [P("c_pp%d" % i, [128, 512]) for i in range(8)]

    cast_mark = {}; mm_done = {}; evac_mark = {}
    NBUF = 3
    kslot = [0]

    def wblock(ap_w, mm_emit, cast_en):
        b = kslot[0] % NBUF
        wb, wf = wld[b], wbf[b]

        def q_w(q, b=b, ap_w=ap_w):
            ec.wait(q, [cast_mark.get(b, ("v", 0))])
            em("q", q.dma_start(out=wb[:], in_=ap_w), dma=True)
        block.sync(q_w)
        ldm = ec.mark("q")

        if cast_en == "v":
            def v_cast(v, b=b, ldm=ldm):
                ec.wait(v, [ldm, mm_done.get(b, ("t", 0))])
                em("v", v.tensor_copy(out=wf[:], in_=wb[:]))
            block.vector(v_cast)
            cast_mark[b] = ec.mark("v")
        else:
            def s_cast(sx, b=b, ldm=ldm):
                ec.wait(sx, [ldm, mm_done.get(b, ("t", 0))])
                em("s", sx.copy(out=wf[:], in_=wb[:]))
            block.scalar(s_cast)
            cast_mark[b] = ec.mark("s")
        mm_emit(wf, b)
        mm_done[b] = ec.mark("t")
        kslot[0] += 1

    for g_i in range(4):
        t0 = g_i * 1024

        def q_ld(q, t0=t0):
            ec.fence(q, ["v", "t", "g", "s", "q"])
            em("q", q.dma_start(out=xnt[:], in_=io["xnb_d"][:, t0:t0 + 1024].rearrange(
                "(c p) t -> p c t", p=128)), dma=True)
            em("q", q.dma_start(out=acc[:], in_=io["acc1"][:, t0:t0 + 1024].rearrange(
                "(c p) t -> p c t", p=128)), dma=True)
        block.sync(q_ld)
        ldx = ec.mark("q")

        silu_marks = {}
        for sb in range(2):
            for which, wkey in ((0, "sw1T"), (1, "sw3T")):
                ap_w = io[wkey][:, sb * 512:(sb + 1) * 512].rearrange("(c p) o -> p c o", p=128)

                def mm_gateup(wf, b, sb=sb, which=which):
                    for oc in range(4):
                        for t5 in range(2):
                            ppi = which * 2 + (oc * 2 + t5) % 2
                            pp = pps[ppi]

                            def t_p(t, wf=wf, oc=oc, t5=t5, pp=pp, ppi=ppi, b=b):
                                ec.wait(t, [cast_mark[b], ldx, evac_mark.get(ppi, ("v", 0))])
                                for ci in range(8):
                                    em("t", t.matmul(pp[:], lhsT=wf[:, ci, oc * 128:(oc + 1) * 128],
                                                     rhs=xnt[:, ci, t5 * 512:(t5 + 1) * 512],
                                                     start=(ci == 0), stop=(ci == 7)))
                            block.tensor(t_p)
                            gm = ec.mark("t")
                            occ = sb * 4 + oc
                            if which == 0:
                                def s_silu(s, pp=pp, occ=occ, t5=t5, gm=gm):
                                    ec.wait(s, [gm])
                                    em("s", s.activation(out=sh[:, occ, t5 * 512:(t5 + 1) * 512],
                                                         in_=pp[:], func=AF.Silu))
                                block.scalar(s_silu)
                                evac_mark[ppi] = ec.mark("s")
                                silu_marks[(occ, t5)] = evac_mark[ppi]
                            else:
                                def v_mul(v, pp=pp, occ=occ, t5=t5, gm=gm):
                                    ec.wait(v, [gm, silu_marks[(occ, t5)]])
                                    em("v", v.tensor_mul(out=sh[:, occ, t5 * 512:(t5 + 1) * 512],
                                                         in0=sh[:, occ, t5 * 512:(t5 + 1) * 512],
                                                         in1=pp[:]))
                                block.vector(v_mul)
                                evac_mark[ppi] = ec.mark("v")

                wblock(ap_w, mm_gateup, "s" if which == 0 else "v")

        for db in range(2):
            ap_w = io["sw2T"][:, db * 512:(db + 1) * 512].rearrange("(c p) o -> p c o", p=128)

            def mm_down(wf, b, db=db):
                for dc in range(4):
                    for t5 in range(2):
                        ppi = dc * 2 + t5
                        pp = pps[ppi]

                        def t_p(t, wf=wf, dc=dc, t5=t5, pp=pp, ppi=ppi, b=b):
                            ec.wait(t, [cast_mark[b], evac_mark.get(ppi, ("v", 0))])
                            for ci in range(8):
                                em("t", t.matmul(pp[:], lhsT=wf[:, ci, dc * 128:(dc + 1) * 128],
                                                 rhs=sh[:, ci, t5 * 512:(t5 + 1) * 512],
                                                 start=(ci == 0), stop=(ci == 7)))
                        block.tensor(t_p)
                        gm = ec.mark("t")
                        dcg = db * 4 + dc

                        def v_acc(v, pp=pp, dcg=dcg, t5=t5, gm=gm):
                            ec.wait(v, [gm])
                            em("v", v.tensor_add(out=acc[:, dcg, t5 * 512:(t5 + 1) * 512],
                                                 in0=acc[:, dcg, t5 * 512:(t5 + 1) * 512],
                                                 in1=pp[:]))
                        block.vector(v_acc)
                        evac_mark[ppi] = ec.mark("v")

            wblock(ap_w, mm_down, "v")

        def g_store(g, t0=t0):
            ec.fence(g, ["v"])
            em("g", g.dma_start(out=io["acc1"][:, t0:t0 + 1024].rearrange(
                "(c p) t -> p c t", p=128), in_=acc[:]), dma=True)
        block.gpsimd(g_store)


def build_p2(nc, ec, block, io, st):
    S = lambda name, shape, dt=F32: st.enter_context(nc.sbuf_tensor(name, shape, dt))
    em = ec.em

    x8b = S("p2_x8b", [128, 8, 8], BF16)
    x8f = S("p2_x8f", [128, 8, 8])
    offsr = S("p2_offsr", [1, 8])
    offsb = S("p2_offsb", [128, 8])
    jall = S("p2_jall", [128, 512])
    wall = S("p2_wall", [128, 512])
    iota64 = S("p2_iota64", [128, 64])
    hs_sb = S("p2_hs", [64, 512])
    ones_r = S("p2_onesr", [1, 128])
    identf = S("p2_id", [128, 128])
    heb = S("p2_heb", [8, 512], BF16)
    hef = S("p2_hef", [8, 512])
    lhs4 = S("p2_lhs4", [128, 4, 8], BF16)
    w2b = S("p2_w2b", [128, 4, 1024], BF16)
    w2l = S("p2_w2l", [128, 4, 1024])
    rr_sb = S("p2_rr", [8, 1024])
    rrT = S("p2_rrT", [128, 8, 8])
    acc8 = S("p2_acc8", [128, 8, 8])
    mask0 = S("p2_mask0", [128, 1])
    identb = S("p2_identb", [128, 128], BF16)
    st_u = contextlib.ExitStack()
    wtl = st_u.enter_context(nc.sbuf_tensor("p2_wtl", [128, 8, 512], F32))
    wtb = st_u.enter_context(nc.sbuf_tensor("p2_wtb", [128, 8, 512], BF16))
    u8 = st_u.enter_context(nc.sbuf_tensor("p2_u8", [8, 512], F32))

    def Y(v):
        ec.selfsync("v", v)

    with contextlib.ExitStack() as st1:
        P1_ = lambda name, shape, dt=F32: st1.enter_context(nc.psum_tensor(name, shape, dt))
        ph = P1_("p2_ph", [64, 512])
        st_pg = contextlib.ExitStack()
        pgu = st_pg.enter_context(nc.psum_tensor("p2_pgu", [128, 512], F32))

        def q_setup(q):
            ec.fence(q, ["g"])
            em("q", q.dma_start(out=x8f[:], in_=io["xn8T_d"].rearrange("(c p) e -> p c e", p=128)), dma=True)
            em("q", q.dma_start(out=offsr[:], in_=io["offs_d"]), dma=True)
            em("q", q.dma_start(out=jall[:], in_=io["jslot"]), dma=True)
            em("q", q.dma_start(out=wall[:], in_=io["wslot"]), dma=True)
            em("q", q.dma_start(out=identf[:], in_=io["identf"]), dma=True)
            em("q", q.dma_start(out=mask0[:], in_=io["mask0"]), dma=True)
        block.sync(q_setup)

        def g_setup(g):
            em("g", g.memset(ones_r[:], 1.0))
            em("g", g.iota(iota64[:], pattern=[[1, 64]], base=0, channel_multiplier=0,
                           allow_small_or_imprecise_dtypes=True))
            ec.fence(g, ["q"])
            em("g", g.tensor_copy(out=x8b[:], in_=x8f[:]))
            em("g", g.tensor_copy(out=identb[:], in_=identf[:]))
        block.gpsimd(g_setup)

        def t_off(t):
            ec.fence(t, ["q", "g"])
            em("t", t.matmul(pgu[:, 0:8], lhsT=ones_r[:], rhs=offsr[:], start=True, stop=True))
        block.tensor(t_off)

        def v_off(v):
            ec.fence(v, ["t"])
            em("v", v.tensor_copy(out=offsb[:], in_=pgu[:, 0:8]))
        block.vector(v_off)

        for tab, wkey in (("u1tab", "w1"), ("u3tab", "w3")):
            for e in range(8):
                def q_w(q, wkey=wkey, e=e):
                    ec.fence(q, ["g"])
                    em("q", q.dma_start(out=wtl[:], in_=io[wkey][e].rearrange(
                        "(c p) f -> p c f", p=128)), dma=True)
                block.sync(q_w)

                def g_c(g):
                    ec.fence(g, ["q", "t"])
                    em("g", g.tensor_copy(out=wtb[:], in_=wtl[:]))
                block.gpsimd(g_c)

                def t_u(t, e=e):
                    ec.fence(t, ["g", "v"])
                    for ci in range(8):
                        em("t", t.matmul(pgu[0:8, :], lhsT=x8b[:, ci, :], rhs=wtb[:, ci, :],
                                         start=(ci == 0), stop=(ci == 7)))
                block.tensor(t_u)

                def v_u(v):
                    ec.fence(v, ["t"])
                    em("v", v.tensor_copy(out=u8[:], in_=pgu[0:8, :]))
                block.vector(v_u)

                def g_st(g, tab=tab, e=e):
                    ec.fence(g, ["v"])
                    em("g", g.dma_start(out=io[tab][e * 8:(e + 1) * 8, :], in_=u8[:]), dma=True)
                block.gpsimd(g_st)

        st_u.close()
        st_pg.close()
        c_all = S("p2_call", [128, 512])
        eid_a = S("p2_eida", [128, 512])
        tmp_a = S("p2_tmpa", [128, 512])
        iall = S("p2_iall", [128, 512])
        u1s = S("p2_u1s", [64, 512], BF16)
        u3s = S("p2_u3s", [64, 512], BF16)
        u1f = S("p2_u1f", [64, 512])
        u3f = S("p2_u3f", [64, 512])
        ohT_all = S("p2_ohTall", [64, 65536], BF16)
        NPAR = 3
        ohst2 = [S("p2_oh%d" % i, [128, 64], BF16) for i in range(NPAR)]
        Asc2 = [S("p2_A%d" % i, [128, 512]) for i in range(NPAR)]
        u3g2 = [S("p2_u3g%d" % i, [128, 512]) for i in range(NPAR)]
        Ssc2 = [S("p2_Ss%d" % i, [128, 512]) for i in range(NPAR)]
        hb2 = [S("p2_hb%d" % i, [128, 512], BF16) for i in range(NPAR)]
        ohpre = [S("p2_ohpre%d" % i, [128, 8, 64], BF16) for i in range(2)]

        def q_uld(q):
            ec.fence(q, ["g"])
            em("q", q.dma_start(out=u1f[:], in_=io["u1tab"]), dma=True)
            em("q", q.dma_start(out=u3f[:], in_=io["u3tab"]), dma=True)
        block.sync(q_uld)
        ulm = ec.mark("q")

        def g_ucast(g):
            ec.wait(g, [ulm])
            em("g", g.tensor_copy(out=u1s[:], in_=u1f[:]))
            em("g", g.tensor_copy(out=u3s[:], in_=u3f[:]))
            em("g", g.iota(iall[:], pattern=[[1, 2], [256, 256]], base=0,
                           channel_multiplier=2,
                           allow_small_or_imprecise_dtypes=True))
        block.gpsimd(g_ucast)

        def v_call(v):
            ec.fence(v, ["g", "q"])
            em("v", v.tensor_scalar(out=eid_a[:], in0=iall[:], scalar1=offsb[:, 0:1],
                                    scalar2=None, op0=ALU.is_ge))
            ec.selfsync("v", v)
            for e in range(1, 8):
                em("v", v.tensor_scalar(out=tmp_a[:], in0=iall[:], scalar1=offsb[:, e:e + 1],
                                        scalar2=None, op0=ALU.is_ge))
                ec.selfsync("v", v)
                em("v", v.tensor_add(out=eid_a[:], in0=eid_a[:], in1=tmp_a[:]))
                ec.selfsync("v", v)
            em("v", v.scalar_tensor_tensor(out=c_all[:], in0=eid_a[:], scalar=8.0,
                                           in1=jall[:], op0=ALU.mult, op1=ALU.add))
            ec.selfsync("v", v)
        block.vector(v_call)

        st_tr = contextlib.ExitStack()
        ptr2 = [st_tr.enter_context(nc.psum_tensor("p2_ptr%d" % i, [64, 8, 128], BF16))
                for i in range(2)]
        pre_mark = {}; prev_mark = {}
        for bt in range(64):
            b2 = bt % 2

            def v_pre(v, bt=bt, b2=b2):
                ec.wait(v, [pre_mark.get(b2, ("t", 0))])
                for i in range(8):
                    sc = bt * 8 + i
                    em("v", v.tensor_scalar(out=ohpre[b2][:, i, :], in0=iota64[:],
                                            scalar1=c_all[:, sc:sc + 1], scalar2=None,
                                            op0=ALU.is_equal))
            block.vector(v_pre)
            vm = ec.mark("v")

            def t_pre(t, b2=b2, vm=vm):
                ec.wait(t, [vm, prev_mark.get(b2, ("v", 0))])
                for i in range(8):
                    em("t", t.transpose(ptr2[b2][:, i, :], in_=ohpre[b2][:, i, :],
                                        identity=identb[:]))
            block.tensor(t_pre)
            pre_mark[b2] = ec.mark("t")

            def v_pre2(v, bt=bt, b2=b2):
                ec.wait(v, [pre_mark[b2]])
                em("v", v.tensor_copy(out=ohT_all[:, bt * 1024:(bt + 1) * 1024],
                                      in_=ptr2[b2][:].rearrange("p a b -> p (a b)")))
            block.vector(v_pre2)
            prev_mark[b2] = ec.mark("v")

        st_tr.close()
        pg1 = [st1.enter_context(nc.psum_tensor("p2_pg1%d" % i, [128, 512], F32))
               for i in range(NPAR)]
        pg3 = [st1.enter_context(nc.psum_tensor("p2_pg3%d" % i, [128, 512], F32))
               for i in range(NPAR)]

        gat_mark = {}; h_mark = {}; silu_mark = {}; hb_mark = {}; hsum_mark = {}
        oh_mark = {}

        def t_pref(t):
            ec.fence(t, ["g", "v"])
        block.tensor(t_pref)

        for sc in range(512):
            b = sc % NPAR

            def t_gat(t, b=b, sc=sc):
                ec.wait(t, [h_mark.get(b, ("v", 0)), silu_mark.get(b, ("s", 0))])
                em("t", t.matmul(pg1[b][:], lhsT=ohT_all[:, sc * 128:(sc + 1) * 128],
                                 rhs=u1s[:], start=True, stop=True))
                em("t", t.matmul(pg3[b][:], lhsT=ohT_all[:, sc * 128:(sc + 1) * 128],
                                 rhs=u3s[:], start=True, stop=True))
            block.tensor(t_gat)
            gat_mark[b] = ec.mark("t")

            def v_h(v, sc=sc, b=b):
                ec.wait(v, [gat_mark[b], hsum_mark.get(b, ("t", 0))])
                em("v", v.tensor_scalar_mul(out=Asc2[b][:], in0=pg1[b][:],
                                            scalar1=wall[:, sc:sc + 1]))
                em("v", v.tensor_copy(out=u3g2[b][:], in_=pg3[b][:]))
                em("v", v.tensor_scalar(out=ohst2[b][:], in0=iota64[:],
                                        scalar1=c_all[:, sc:sc + 1], scalar2=None,
                                        op0=ALU.is_equal))
            block.vector(v_h)
            h_mark[b] = ec.mark("v")

            def s_h(s, b=b):
                ec.wait(s, [h_mark[b], hb_mark.get(b, ("v", 0))])
                em("s", s.activation(out=Ssc2[b][:], in_=Asc2[b][:], func=AF.Silu))
            block.scalar(s_h)
            silu_mark[b] = ec.mark("s")

            def v_h2(v, sc=sc, b=b):
                ec.wait(v, [silu_mark[b]])
                em("v", v.tensor_mul(out=u3g2[b][:], in0=u3g2[b][:], in1=Ssc2[b][:]))
                ec.selfsync("v", v)
                em("v", v.tensor_scalar_mul(out=hb2[b][:], in0=u3g2[b][:],
                                            scalar1=wall[:, sc:sc + 1]))
            block.vector(v_h2)
            hb_mark[b] = ec.mark("v")

            def t_hs(t, sc=sc, b=b):
                ec.wait(t, [hb_mark[b]])
                em("t", t.matmul(ph[:], lhsT=ohst2[b][:], rhs=hb2[b][:],
                                 start=(sc == 0), stop=(sc == 511)))
            block.tensor(t_hs)
            hsum_mark[b] = ec.mark("t")

        def v_hs(v):
            ec.fence(v, ["t"])
            em("v", v.tensor_copy(out=hs_sb[:], in_=ph[:]))
        block.vector(v_hs)

        def g_hs(g):
            ec.fence(g, ["v"])
            em("g", g.dma_start(out=io["hs_d"], in_=hs_sb[:]), dma=True)
        block.gpsimd(g_hs)

    with contextlib.ExitStack() as st2:
        P2_ = lambda name, shape, dt=F32: st2.enter_context(nc.psum_tensor(name, shape, dt))
        pt = P2_("p2_pt", [128, 8], BF16)
        prr = [P2_("p2_prr0", [8, 512]), P2_("p2_prr1", [8, 512])]
        ptj = P2_("p2_ptj", [128, 8])

        for e in range(8):
            def q_he(q, e=e):
                ec.fence(q, ["g", "t"])
                em("q", q.dma_start(out=hef[:], in_=io["hs_d"][e * 8:(e + 1) * 8, :]), dma=True)
                em("q", q.dma_start(out=w2l[:], in_=io["w2"][e].rearrange(
                    "(c p) d -> p c d", p=128)), dma=True)
            block.sync(q_he)

            def g_cast(g):
                ec.fence(g, ["q", "t"])
                em("g", g.tensor_copy(out=heb[:], in_=hef[:]))
                em("g", g.tensor_copy(out=w2b[:], in_=w2l[:]))
            block.gpsimd(g_cast)

            for f4 in range(4):
                def t_tr(t, e=e, f4=f4):
                    ec.fence(t, ["g", "v"])
                    em("t", t.transpose(pt[:], in_=heb[:, f4 * 128:(f4 + 1) * 128],
                                        identity=identb[0:8, 0:8]))
                block.tensor(t_tr)

                def v_tr(v, f4=f4):
                    ec.fence(v, ["t"])
                    em("v", v.tensor_copy(out=lhs4[:, f4, :], in_=pt[:]))
                block.vector(v_tr)

            for dh in range(2):
                def t_rr(t, e=e, dh=dh):
                    ec.fence(t, ["v", "g"])
                    for f4 in range(4):
                        em("t", t.matmul(prr[dh][:], lhsT=lhs4[:, f4, :],
                                         rhs=w2b[:, f4, dh * 512:(dh + 1) * 512],
                                         start=(e == 0 and f4 == 0), stop=(e == 7 and f4 == 3)))
                block.tensor(t_rr)

        def v_rr(v):
            ec.fence(v, ["t"])
            em("v", v.tensor_copy(out=rr_sb[:, 0:512], in_=prr[0][:]))
            em("v", v.tensor_copy(out=rr_sb[:, 512:1024], in_=prr[1][:])); Y(v)
        block.vector(v_rr)

        for ci in range(8):
            def t_tj(t, ci=ci):
                ec.fence(t, ["v"])
                em("t", t.transpose(ptj[:], in_=rr_sb[:, ci * 128:(ci + 1) * 128],
                                    identity=identf[0:8, 0:8]))
            block.tensor(t_tj)

            def v_tj(v, ci=ci):
                ec.fence(v, ["t"])
                em("v", v.tensor_copy(out=rrT[:, ci, :], in_=ptj[:]))
            block.vector(v_tj)

        def q_acc(q):
            ec.fence(q, ["g", "v"])
            em("q", q.dma_start(out=acc8[:], in_=io["acc1"][:, 0:8].rearrange(
                "(c p) e -> p c e", p=128)), dma=True)
        block.sync(q_acc)

        def v_add(v):
            ec.fence(v, ["q"])
            for ci in range(8):
                em("v", v.tensor_scalar_mul(out=rrT[:, ci, :], in0=rrT[:, ci, :],
                                            scalar1=mask0[:]))
            Y(v)
            for ci in range(8):
                em("v", v.tensor_add(out=acc8[:, ci, :], in0=acc8[:, ci, :],
                                     in1=rrT[:, ci, :]))
            Y(v)
        block.vector(v_add)

        def g_out(g):
            ec.fence(g, ["v"])
            em("g", g.dma_start(out=io["acc1"][:, 0:8].rearrange("(c p) e -> p c e", p=128),
                                in_=acc8[:]), dma=True)
        block.gpsimd(g_out)



P3A_DBG = -1

def build_p3a(nc, ec, block, io, st):
    S = lambda name, shape, dt=F32: st.enter_context(nc.sbuf_tensor(name, shape, dt))
    P = lambda name, shape: st.enter_context(nc.psum_tensor(name, shape, F32))
    em = ec.em
    NG = 8

    xg = S("d_xg", [128, 8, 512])
    BS = S("d_bs", [128, 8, 512])
    xn1b = S("d_xn1b", [128, 8, 512], BF16)
    x1af = S("d_x1af", [128, 8, 515])
    x2f = S("d_x2f", [128, 8, 512])
    x2b = S("d_x2b", [128, 8, 512], BF16)
    gx = S("d_gx", [128, 8, 512])
    BA = S("d_ba", [128, 8, 512])
    BM = S("d_bm", [128, 8, 512])
    yb = S("d_yb", [128, 8, 512], BF16)
    Aa = x1af
    wld = [S("d_wld0", [128, 8, 512]), S("d_wld1", [128, 8, 512])]
    wbf = [S("d_wbf0", [128, 8, 512], BF16), S("d_wbf1", [128, 8, 512], BF16)]
    mb = S("d_mb", [128, 512]); rvb = S("d_rvb", [128, 512])
    m_row = S("d_mrow", [1, 512]); v_row = S("d_vrow", [1, 512]); rv_row = S("d_rvrow", [1, 512])
    ones_c = S("d_onesc", [128, 1]); ones_r = S("d_onesr", [1, 128])
    one_col = S("d_onecol", [128, 1])
    zeros512 = S("d_z512", [128, 512])
    carry_h = S("d_carryh", [128, 8, 3])
    carry_s = S("d_carrys", [128, 8])
    par = {}
    for nm in ("l1_b_x", "l1_b_y", "l1_conv_b", "l1_gi_b", "l1_ga_b", "l1_a_param"):
        par[nm] = S("d_" + nm, [128, 8])
    conv1_w = S("d_conv1w", [128, 8, 4])
    msp = S("d_msp", [128, 8])
    t8a = S("d_t8a", [128, 8]); t8b = S("d_t8b", [128, 8])
    pms = [P("d_pm0", [1, 512]), P("d_pm1", [1, 512])]
    pbc = [P("d_pbc0", [128, 512]), P("d_pbc1", [128, 512])]
    pmm = [P("d_pmm0", [128, 512]), P("d_pmm1", [128, 512])]

    def Y(v):
        ec.selfsync("v", v)

    def q_setup(q):
        for nm in par:
            em("q", q.dma_start(out=par[nm][:], in_=io[nm]), dma=True)
        em("q", q.dma_start(out=conv1_w[:], in_=io["conv1_w"]), dma=True)
    block.sync(q_setup)

    def g_setup(g):
        em("g", g.memset(ones_c[:], 1.0))
        em("g", g.memset(ones_r[:], 1.0))
        em("g", g.memset(one_col[:], 1.0))
        em("g", g.memset(zeros512[:], 0.0))
        em("g", g.memset(carry_h[:], 0.0))
        em("g", g.memset(carry_s[:], 0.0))
    block.gpsimd(g_setup)

    def s_sp(s):
        ec.fence(s, ["q", "g"])
        em("s", s.activation(out=t8a[:], in_=par["l1_a_param"][:], func=AF.Abs))
    block.scalar(s_sp)

    def s_sp2(s):
        em("s", s.activation(out=t8a[:], in_=t8a[:], func=AF.Exp, scale=-1.0))
    block.scalar(s_sp2)

    def s_sp3(s):
        em("s", s.activation(out=t8a[:], in_=t8a[:], func=AF.Ln, bias=one_col[:]))
    block.scalar(s_sp3)

    def v_sp(v):
        ec.fence(v, ["s"])
        em("v", v.tensor_scalar_max(out=t8b[:], in0=par["l1_a_param"][:], scalar1=0.0)); Y(v)
        em("v", v.tensor_add(out=msp[:], in0=t8a[:], in1=t8b[:])); Y(v)
        em("v", v.tensor_scalar_mul(out=msp[:], in0=msp[:], scalar1=-8.0)); Y(v)
    block.vector(v_sp)

    def stats(src):
        def s_sq(s):
            ec.fence(s, ["q", "t", "v"])
            em("s", s.activation(out=BS[:], in_=src[:], func=AF.Square))
        block.scalar(s_sq)

        def t_stat(t):
            ec.fence(t, ["s", "v"])
            for ci in range(8):
                em("t", t.matmul(pms[0][:], lhsT=ones_c[:], rhs=src[:, ci, :],
                                 start=(ci == 0), stop=(ci == 7)))
            for ci in range(8):
                em("t", t.matmul(pms[1][:], lhsT=ones_c[:], rhs=BS[:, ci, :],
                                 start=(ci == 0), stop=(ci == 7)))
        block.tensor(t_stat)

        def v_stat(v):
            ec.fence(v, ["t"])
            em("v", v.tensor_scalar_mul(out=m_row[:], in0=pms[0][:], scalar1=1.0 / D)); Y(v)
            em("v", v.tensor_mul(out=v_row[:], in0=m_row[:], in1=m_row[:])); Y(v)
            em("v", v.scalar_tensor_tensor(out=v_row[:], in0=pms[1][:], scalar=1.0 / D,
                                           in1=v_row[:], op0=ALU.mult, op1=ALU.subtract)); Y(v)
            em("v", v.tensor_scalar_add(out=v_row[:], in0=v_row[:], scalar1=1e-5))
        block.vector(v_stat)

        def s_r(s):
            ec.fence(s, ["v"])
            em("s", s.activation(out=rv_row[:], in_=v_row[:], func=AF.Sqrt))
        block.scalar(s_r)

        def v_r(v):
            ec.fence(v, ["s"])
            em("v", v.reciprocal(out=rv_row[:], in_=rv_row[:]))
        block.vector(v_r)

        def t_b(t):
            ec.fence(t, ["v"])
            em("t", t.matmul(pbc[0][:], lhsT=ones_r[:], rhs=m_row[:], start=True, stop=True))
            em("t", t.matmul(pbc[1][:], lhsT=ones_r[:], rhs=rv_row[:], start=True, stop=True))
        block.tensor(t_b)

        def v_b(v):
            ec.fence(v, ["t"])
            em("v", v.tensor_copy(out=mb[:], in_=pbc[0][:]))
            em("v", v.tensor_copy(out=rvb[:], in_=pbc[1][:])); Y(v)
        block.vector(v_b)

    def proj(wkey, rhs_tile, evac, bias_key=None):
        for ob in range(2):
            wb = wld[ob % 2]; wf = wbf[ob % 2]

            def q_w(q, wkey=wkey, ob=ob, wb=wb):
                ec.fence(q, ["g"])
                em("q", q.dma_start(out=wb[:], in_=io[wkey][:, ob * 512:(ob + 1) * 512].rearrange(
                    "(c p) o -> p c o", p=128)), dma=True)
            block.sync(q_w)

            def g_cast(g, wb=wb, wf=wf):
                ec.fence(g, ["q", "t"])
                em("g", g.tensor_copy(out=wf[:], in_=wb[:]))
            block.gpsimd(g_cast)

            for oc in range(4):
                pp = pmm[oc % 2]

                def t_p(t, wf=wf, oc=oc, pp=pp):
                    ec.fence(t, ["g", "v", "s"])
                    for ci in range(8):
                        em("t", t.matmul(pp[:], lhsT=wf[:, ci, oc * 128:(oc + 1) * 128],
                                         rhs=rhs_tile[:, ci, :], start=(ci == 0), stop=(ci == 7)))
                block.tensor(t_p)

                evac(ob * 4 + oc, pp)

    for g_i in range(NG):
        t0 = g_i * 512

        def q_xg(q, t0=t0):
            ec.fence(q, ["v", "s", "g"])
            em("q", q.dma_start(out=xg[:], in_=io["acc1"][:, t0:t0 + 512].rearrange(
                "(c p) t -> p c t", p=128)), dma=True)
        block.sync(q_xg)

        stats(xg)

        def v_xn(v, t0=t0):
            for ci in range(8):
                em("v", v.tensor_sub(out=BS[:, ci, :], in0=xg[:, ci, :], in1=mb[:]))
            Y(v)
            for ci in range(8):
                em("v", v.tensor_mul(out=BS[:, ci, :], in0=BS[:, ci, :], in1=rvb[:]))
            Y(v)
            for ci in range(8):
                em("v", v.tensor_copy(out=xn1b[:, ci, :], in_=BS[:, ci, :]))
            Y(v)
        block.vector(v_xn)

        def g_xn1(g, t0=t0):
            ec.fence(g, ["v"])
            em("g", g.dma_start(out=io["xn1f_d"][:, t0:t0 + 512].rearrange(
                "(c p) t -> p c t", p=128), in_=BS[:]), dma=True)
            em("g", g.dma_start(out=io["xn1b_d"][:, t0:t0 + 512].rearrange(
                "(c p) t -> p c t", p=128), in_=xn1b[:]), dma=True)
        block.gpsimd(g_xn1)

        def ev_x1(occ, pp):
            def v_e(v, occ=occ, pp=pp):
                ec.fence(v, ["t"])
                em("v", v.tensor_scalar(out=x1af[:, occ, 3:515], in0=pp[:],
                                        scalar1=par["l1_b_x"][:, occ:occ + 1], scalar2=None,
                                        op0=ALU.add))
            block.vector(v_e)
        proj("w_x", xn1b, ev_x1)

        def v_conv(v, g_i=g_i):
            for ci in range(8):
                em("v", v.tensor_copy(out=x1af[:, ci, 0:3], in_=carry_h[:, ci, :]))
            Y(v)
            for ci in range(8):
                em("v", v.tensor_copy(out=carry_h[:, ci, :], in_=x1af[:, ci, 512:515]))
            Y(v)
            for ci in range(8):
                em("v", v.tensor_scalar_mul(out=x2f[:, ci, :], in0=x1af[:, ci, 0:512],
                                            scalar1=conv1_w[:, ci, 0:1]))
            Y(v)
            for i in range(1, 4):
                for ci in range(8):
                    em("v", v.scalar_tensor_tensor(out=x2f[:, ci, :], in0=x1af[:, ci, i:i + 512],
                                                   scalar=conv1_w[:, ci, i:i + 1],
                                                   in1=x2f[:, ci, :], op0=ALU.mult, op1=ALU.add))
                Y(v)
            for ci in range(8):
                em("v", v.tensor_scalar(out=x2f[:, ci, :], in0=x2f[:, ci, :],
                                        scalar1=par["l1_conv_b"][:, ci:ci + 1], scalar2=None,
                                        op0=ALU.add))
            Y(v)
            for ci in range(8):
                em("v", v.tensor_copy(out=x2b[:, ci, :], in_=x2f[:, ci, :]))
            Y(v)
        block.vector(v_conv)

        def ev_gx(occ, pp):
            def s_e(s, occ=occ, pp=pp):
                ec.fence(s, ["t"])
                em("s", s.activation(out=gx[:, occ, :], in_=pp[:], func=AF.Sigmoid,
                                     bias=par["l1_gi_b"][:, occ:occ + 1]))
            block.scalar(s_e)
        proj("gi_w", x2b, ev_gx)

        def ev_ga(occ, pp):
            def s_e(s, occ=occ, pp=pp):
                ec.fence(s, ["t"])
                em("s", s.activation(out=BA[:, occ, :], in_=pp[:], func=AF.Sigmoid,
                                     bias=par["l1_ga_b"][:, occ:occ + 1]))
            block.scalar(s_e)
        proj("ga_w", x2b, ev_ga)

        def v_la(v):
            ec.fence(v, ["s"])
            for ci in range(8):
                em("v", v.tensor_scalar_mul(out=BA[:, ci, :], in0=BA[:, ci, :],
                                            scalar1=msp[:, ci:ci + 1]))
            Y(v)
        block.vector(v_la)

        def s_a(s):
            ec.fence(s, ["v"])
            for ci in range(8):
                em("s", s.activation(out=Aa[:, ci, 0:512], in_=BA[:, ci, :], func=AF.Exp))
            for ci in range(8):
                em("s", s.activation(out=BM[:, ci, :], in_=BA[:, ci, :], func=AF.Exp,
                                     scale=2.0))
        block.scalar(s_a)

        def v_mult(v, g_i=g_i):
            ec.fence(v, ["s"])
            for ci in range(8):
                em("v", v.tensor_scalar(out=BM[:, ci, :], in0=BM[:, ci, :], scalar1=-1.0,
                                        scalar2=1.0, op0=ALU.mult, op1=ALU.add))
            Y(v)
            for ci in range(8):
                em("v", v.tensor_scalar_max(out=BM[:, ci, :], in0=BM[:, ci, :], scalar1=0.0))
            Y(v)
        block.vector(v_mult)

        def s_sqrt(s):
            ec.fence(s, ["v"])
            for ci in range(8):
                em("s", s.activation(out=BM[:, ci, :], in_=BM[:, ci, :], func=AF.Sqrt))
        block.scalar(s_sqrt)

        if g_i == 0:
            def g_t0(g):
                ec.fence(g, ["s"])
                em("g", g.memset(BM[:, :, 0:1], 1.0))
                em("g", g.memset(Aa[:, :, 0:1], 0.0))
            block.gpsimd(g_t0)

        def v_u(v):
            ec.fence(v, ["s", "g"])
            for ci in range(8):
                em("v", v.tensor_mul(out=gx[:, ci, :], in0=gx[:, ci, :], in1=x2f[:, ci, :]))
            Y(v)
            for ci in range(8):
                em("v", v.tensor_mul(out=gx[:, ci, :], in0=gx[:, ci, :], in1=BM[:, ci, :]))
            Y(v)
            for ci in range(8):
                em("v", v.tensor_tensor_scan(out=BA[:, ci, :], data0=Aa[:, ci, 0:512],
                                             data1=gx[:, ci, :],
                                             initial=carry_s[:, ci:ci + 1],
                                             op0=ALU.mult, op1=ALU.add))
                ec.selfsync("v", v)
                em("v", v.tensor_copy(out=carry_s[:, ci:ci + 1], in_=BA[:, ci, 511:512]))
                ec.selfsync("v", v)
        block.vector(v_u)

        def ev_y(occ, pp):
            def s_e(s, occ=occ, pp=pp):
                ec.fence(s, ["t"])
                em("s", s.activation(out=yb[:, occ, :], in_=pp[:], func=AF.Gelu,
                                     bias=par["l1_b_y"][:, occ:occ + 1]))
            block.scalar(s_e)
        proj("w_y", xn1b, ev_y)

        def v_out(v):
            ec.fence(v, ["s"])
            for ci in range(8):
                em("v", v.tensor_mul(out=BS[:, ci, :], in0=BA[:, ci, :], in1=yb[:, ci, :]))
            Y(v)
            for ci in range(8):
                em("v", v.tensor_add(out=xg[:, ci, :], in0=xg[:, ci, :], in1=BS[:, ci, :]))
            Y(v)
        block.vector(v_out)

        def g_out(g, t0=t0):
            ec.fence(g, ["v"])
            em("g", g.dma_start(out=io["acc3"][:, t0:t0 + 512].rearrange(
                "(c p) t -> p c t", p=128), in_=xg[:]), dma=True)
        block.gpsimd(g_out)

        if g_i == P3A_DBG:
            def g_dbg(g):
                ec.fence(g, ["v", "s"])
                for off, tile in ((0, Aa), (512, BM), (1024, gx), (1536, BA), (2048, x2f), (2560, BS)):
                    src = tile[:, :, 0:512] if tile is Aa else tile[:]
                    em("g", g.dma_start(out=io["dbgA"][:, off:off + 512].rearrange(
                        "(c p) t -> p c t", p=128), in_=src), dma=True)
            block.gpsimd(g_dbg)


def build_p3b(nc, ec, block, io, st):
    S = lambda name, shape, dt=F32: st.enter_context(nc.sbuf_tensor(name, shape, dt))
    em = ec.em

    def Y(v):
        ec.selfsync("v", v)

    with contextlib.ExitStack() as st1:
        S1 = lambda name, shape, dt=F32: st1.enter_context(nc.sbuf_tensor(name, shape, dt))
        P1_ = lambda name, shape: st1.enter_context(nc.psum_tensor(name, shape, F32))
        xn1f = S1("e_xn1f", [128, 8, 1024])
        gw = S1("e_gw", [128, 8, 8])
        gb8 = S1("e_gb8", [8, 1])
        lg_sb = S1("e_lg", [8, 1024])
        lt = S1("e_lt", [128, 8, 8])
        probs = S1("e_probs", [128, 8, 8])
        pr = S1("e_pr", [128, 8, 8])
        k1 = S1("e_k1", [128, 8, 8])
        k2 = S1("e_k2", [128, 8, 8])
        red = S1("e_red", [128, 8])
        iota8 = S1("e_iota8", [128, 8])
        comb = S1("e_comb", [128, 8, 8])
        crow = S1("e_crow", [8, 1024])
        identf = S1("e_id", [128, 128])
        plg = P1_("e_plg", [8, 512])
        ptr = P1_("e_ptr", [128, 8])
        ptc = P1_("e_ptc", [8, 128])

        def redb(n):
            return red[:].rearrange("p (s o) -> p s o", o=1).broadcast_to([128, 8, n])

        def q_setup(q):
            em("q", q.dma_start(out=gw[:], in_=io["gate1_w"].rearrange("(c p) e -> p c e", p=128)), dma=True)
            em("q", q.dma_start(out=gb8[:], in_=io["gate1_bc"]), dma=True)
            em("q", q.dma_start(out=identf[:], in_=io["identf"]), dma=True)
        block.sync(q_setup)

        def g_setup(g):
            em("g", g.iota(iota8[:], pattern=[[1, 8]], base=0, channel_multiplier=0,
                           allow_small_or_imprecise_dtypes=True))
        block.gpsimd(g_setup)

        for g_i in range(4):
            t0 = g_i * 1024

            def q_ld(q, t0=t0):
                ec.fence(q, ["v", "t", "g"])
                em("q", q.dma_start(out=xn1f[:], in_=io["xn1f_d"][:, t0:t0 + 1024].rearrange(
                    "(c p) t -> p c t", p=128)), dma=True)
            block.sync(q_ld)

            for t5 in range(2):
                def t_lg(t, t5=t5):
                    ec.fence(t, ["q", "v", "g"])
                    for ci in range(8):
                        em("t", t.matmul(plg[:], lhsT=gw[:, ci, :],
                                         rhs=xn1f[:, ci, t5 * 512:(t5 + 1) * 512],
                                         start=(ci == 0), stop=(ci == 7)))
                block.tensor(t_lg)

                def v_lg(v, t5=t5):
                    ec.fence(v, ["t"])
                    em("v", v.tensor_scalar(out=lg_sb[:, t5 * 512:(t5 + 1) * 512], in0=plg[:],
                                            scalar1=gb8[:], scalar2=None, op0=ALU.add))
                block.vector(v_lg)

            for tc in range(8):
                def t_tr(t, tc=tc):
                    ec.fence(t, ["v"])
                    em("t", t.transpose(ptr[:], in_=lg_sb[:, tc * 128:(tc + 1) * 128],
                                        identity=identf[0:8, 0:8]))
                block.tensor(t_tr)

                def v_tr(v, tc=tc):
                    ec.fence(v, ["t"])
                    em("v", v.tensor_copy(out=lt[:, tc, :], in_=ptr[:]))
                block.vector(v_tr)

            def v_sm(v):
                Y(v)
                em("v", v.reduce_max(out=red[:], in_=lt[:], axis=AX.X)); Y(v)
                em("v", v.tensor_sub(out=pr[:], in0=lt[:], in1=redb(8))); Y(v)
            block.vector(v_sm)

            def s_exp(s):
                ec.fence(s, ["v"])
                em("s", s.activation(out=pr[:], in_=pr[:], func=AF.Exp))
            block.scalar(s_exp)

            def v_sm2(v):
                ec.fence(v, ["s"])
                em("v", v.reduce_sum(out=red[:], in_=pr[:], axis=AX.X)); Y(v)
                em("v", v.reciprocal(out=red[:], in_=red[:])); Y(v)
                em("v", v.tensor_mul(out=probs[:], in0=pr[:], in1=redb(8))); Y(v)
                em("v", v.scalar_tensor_tensor(out=pr[:], in0=iota8[:].rearrange(
                    "p (o e) -> p o e", o=1).broadcast_to([128, 8, 8]), scalar=-TIE,
                    in1=probs[:], op0=ALU.mult, op1=ALU.add)); Y(v)
                em("v", v.reduce_max(out=red[:], in_=pr[:], axis=AX.X)); Y(v)
                em("v", v.tensor_tensor(out=k1[:], in0=pr[:], in1=redb(8), op=ALU.is_ge)); Y(v)
                em("v", v.scalar_tensor_tensor(out=pr[:], in0=k1[:], scalar=-1e4, in1=pr[:],
                                               op0=ALU.mult, op1=ALU.add)); Y(v)
                em("v", v.reduce_max(out=red[:], in_=pr[:], axis=AX.X)); Y(v)
                em("v", v.tensor_tensor(out=k2[:], in0=pr[:], in1=redb(8), op=ALU.is_ge)); Y(v)
                em("v", v.tensor_add(out=k1[:], in0=k1[:], in1=k2[:])); Y(v)
                em("v", v.tensor_mul(out=comb[:], in0=k1[:], in1=probs[:])); Y(v)
                em("v", v.reduce_sum(out=red[:], in_=comb[:], axis=AX.X)); Y(v)
                em("v", v.reciprocal(out=red[:], in_=red[:])); Y(v)
                em("v", v.tensor_mul(out=comb[:], in0=comb[:], in1=redb(8))); Y(v)
            block.vector(v_sm2)

            for tc in range(8):
                def t_tc(t, tc=tc):
                    ec.fence(t, ["v"])
                    em("t", t.transpose(ptc[:], in_=comb[:, tc, :], identity=identf[:]))
                block.tensor(t_tc)

                def v_tc(v, tc=tc):
                    ec.fence(v, ["t"])
                    em("v", v.tensor_copy(out=crow[:, tc * 128:(tc + 1) * 128], in_=ptc[:]))
                block.vector(v_tc)

            def g_crow(g, t0=t0):
                ec.fence(g, ["v"])
                em("g", g.dma_start(out=io["crow_d"][:, t0:t0 + 1024], in_=crow[:]), dma=True)
            block.gpsimd(g_crow)

    with contextlib.ExitStack() as st2:
        S2 = lambda name, shape, dt=F32: st2.enter_context(nc.sbuf_tensor(name, shape, dt))
        P2_ = lambda name, shape: st2.enter_context(nc.psum_tensor(name, shape, F32))
        xn1b = S2("e_xn1b", [128, 8, 1024], BF16)
        acc = S2("e_acc", [128, 8, 1024])
        m = S2("e_m", [128, 32, 1024], BF16)
        wld = [S2("e_wld0", [128, 8, 512]), S2("e_wld1", [128, 8, 512]),
               S2("e_wld2", [128, 8, 512])]
        wbf = [S2("e_wbf0", [128, 8, 512], BF16), S2("e_wbf1", [128, 8, 512], BF16),
               S2("e_wbf2", [128, 8, 512], BF16)]
        crow1 = S2("e_crow1", [1, 1024])
        bcb = S2("e_bcb", [128, 2, 512], BF16)
        dtmp = S2("e_dtmp", [128, 512])
        ones_r = S2("e_onesr", [1, 128])
        pps = [P2_("e_pp%d" % i, [128, 512]) for i in range(8)]

        def g_setup2(g):
            ec.fence(g, ["g", "v", "q"])
            em("g", g.memset(ones_r[:], 1.0))
        block.gpsimd(g_setup2)

        cast_mark = {}
        mm_done = {}
        evac_mark = {}
        NBUF = 3

        def wblock(kslot, wkey_ap, mm_emit, cast_en="v"):
            b = kslot[0] % NBUF
            wb, wf = wld[b], wbf[b]

            def q_w(q, b=b, wkey_ap=wkey_ap):
                ec.wait(q, [cast_mark.get(b, ("v", 0))])
                em("q", q.dma_start(out=wb[:], in_=wkey_ap), dma=True)
            block.sync(q_w)
            ldm = ec.mark("q")

            def v_cast(v, b=b, ldm=ldm):
                ec.wait(v, [ldm, mm_done.get(b, ("t", 0))])
                em("v" if cast_en == "v" else cast_en,
                   (v.tensor_copy(out=wf[:], in_=wb[:])))
            if cast_en == "v":
                block.vector(v_cast)
                cast_mark[b] = ec.mark("v")
            else:
                def s_cast(sx, b=b, ldm=ldm):
                    ec.wait(sx, [ldm, mm_done.get(b, ("t", 0))])
                    em("s", sx.tensor_copy(out=wf[:], in_=wb[:])) if False else em(
                        "s", sx.copy(out=wf[:], in_=wb[:]))
                block.scalar(s_cast)
                cast_mark[b] = ec.mark("s")
            mm_emit(wf, b)
            mm_done[b] = ec.mark("t")
            kslot[0] += 1

        kslot = [0]
        ldx_mark = [None]

        for g_i in range(4):
            t0 = g_i * 1024

            def q_ld(q, t0=t0):
                ec.fence(q, ["v", "t", "g", "s", "q"])
                em("q", q.dma_start(out=xn1b[:], in_=io["xn1b_d"][:, t0:t0 + 1024].rearrange(
                    "(c p) t -> p c t", p=128)), dma=True)
                em("q", q.dma_start(out=acc[:], in_=io["acc3"][:, t0:t0 + 1024].rearrange(
                    "(c p) t -> p c t", p=128)), dma=True)
            block.sync(q_ld)
            ldx = ec.mark("q")

            for e in range(8):
                def q_cr(q, e=e, t0=t0):
                    ec.wait(q, [evac_mark.get(100, ("v", 0))])
                    ec.fence(q, ["v"]) if False else None
                    em("q", q.dma_start(out=crow1[:], in_=io["crow_d"][e:e + 1, t0:t0 + 1024]), dma=True)
                block.sync(q_cr)
                crm = ec.mark("q")

                for t5 in range(2):
                    def t_bc(t, t5=t5, crm=crm):
                        ec.wait(t, [crm, evac_mark.get(4 + t5, ("v", 0))])
                        em("t", t.matmul(pps[4 + t5][:], lhsT=ones_r[:],
                                         rhs=crow1[0:1, t5 * 512:(t5 + 1) * 512],
                                         start=True, stop=True))
                    block.tensor(t_bc)
                    bm = ec.mark("t")

                    def v_bc(v, t5=t5, bm=bm):
                        ec.wait(v, [bm])
                        em("v", v.tensor_copy(out=bcb[:, t5, :], in_=pps[4 + t5][:]))
                    block.vector(v_bc)
                    evac_mark[4 + t5] = ec.mark("v")

                silu_marks = {}
                for fb in range(8):
                    for which, wkey in ((0, "w1f"), (1, "w3f")):
                        ap_w = io[wkey][e][:, fb * 512:(fb + 1) * 512].rearrange(
                            "(c p) o -> p c o", p=128)

                        def mm_gateup(wf, b, fb=fb, which=which, e=e):
                            for fc in range(4):
                                for t5 in range(2):
                                    ppi = which * 2 + (fc * 2 + t5) % 2
                                    pp = pps[ppi]

                                    def t_p(t, wf=wf, fc=fc, t5=t5, pp=pp, ppi=ppi):
                                        ec.wait(t, [cast_mark[b], ldx,
                                                    evac_mark.get(ppi, ("v", 0))])
                                        for ci in range(8):
                                            em("t", t.matmul(pp[:], lhsT=wf[:, ci, fc * 128:(fc + 1) * 128],
                                                             rhs=xn1b[:, ci, t5 * 512:(t5 + 1) * 512],
                                                             start=(ci == 0), stop=(ci == 7)))
                                    block.tensor(t_p)
                                    gm = ec.mark("t")

                                    fcg = fb * 4 + fc
                                    if which == 0:
                                        def s_silu(s, pp=pp, fcg=fcg, t5=t5, gm=gm):
                                            ec.wait(s, [gm])
                                            em("s", s.activation(out=m[:, fcg, t5 * 512:(t5 + 1) * 512],
                                                                 in_=pp[:], func=AF.Silu))
                                        block.scalar(s_silu)
                                        evac_mark[ppi] = ec.mark("s")
                                        silu_marks[(fcg, t5)] = evac_mark[ppi]
                                    else:
                                        def v_mul(v, pp=pp, fcg=fcg, t5=t5, gm=gm):
                                            ec.wait(v, [gm, silu_marks[(fcg, t5)]])
                                            em("v", v.tensor_mul(out=m[:, fcg, t5 * 512:(t5 + 1) * 512],
                                                                 in0=m[:, fcg, t5 * 512:(t5 + 1) * 512],
                                                                 in1=pp[:]))
                                        block.vector(v_mul)
                                        evac_mark[ppi] = ec.mark("v")

                        wblock(kslot, ap_w, mm_gateup, cast_en="s" if which == 0 else "v")

                for db in range(2):
                    for sub in range(4):
                        ap_w = io["w2f"][e][sub * 1024:(sub + 1) * 1024,
                                            db * 512:(db + 1) * 512].rearrange(
                            "(c p) o -> p c o", p=128)

                        def mm_down(wf, b, db=db, sub=sub, e=e):
                            for dc in range(4):
                                for t5 in range(2):
                                    ppi = dc * 2 + t5
                                    pp = pps[ppi]

                                    def t_p(t, wf=wf, dc=dc, t5=t5, pp=pp, ppi=ppi, sub=sub):
                                        w8 = [cast_mark[b]]
                                        if sub == 0:
                                            w8.append(evac_mark.get(ppi, ("v", 0)))
                                        ec.wait(t, w8)
                                        for kf in range(8):
                                            em("t", t.matmul(pp[:], lhsT=wf[:, kf, dc * 128:(dc + 1) * 128],
                                                             rhs=m[:, sub * 8 + kf, t5 * 512:(t5 + 1) * 512],
                                                             start=(kf == 0 and sub == 0),
                                                             stop=(kf == 7 and sub == 3)))
                                    block.tensor(t_p)

                                    if sub == 3:
                                        gm = ec.mark("t")
                                        dcg = db * 4 + dc

                                        def v_acc(v, pp=pp, dcg=dcg, t5=t5, gm=gm):
                                            ec.wait(v, [gm])
                                            em("v", v.tensor_mul(out=dtmp[:], in0=pp[:], in1=bcb[:, t5, :]))
                                            ec.selfsync("v", v)
                                            em("v", v.tensor_add(out=acc[:, dcg, t5 * 512:(t5 + 1) * 512],
                                                                 in0=acc[:, dcg, t5 * 512:(t5 + 1) * 512],
                                                                 in1=dtmp[:]))
                                        block.vector(v_acc)
                                        evac_mark[ppi] = ec.mark("v")

                        wblock(kslot, ap_w, mm_down, cast_en="v")
                evac_mark[100] = ec.mark("v")

            def g_store(g, t0=t0):
                ec.fence(g, ["v"])
                em("g", g.dma_start(out=io["outT"][:, t0:t0 + 1024].rearrange(
                    "(c p) t -> p c t", p=128), in_=acc[:]), dma=True)
            block.gpsimd(g_store)


def phase_barrier(ec, block):
    for dst in EngCtx.NAMES:
        def body(q, dst=dst):
            ec.fence(q, list(EngCtx.NAMES))
        getattr(block, _BLK[dst])(body)


def run_phases(build_fns, which, in_maps, n_cores=8, trace=False):
    nc = bass.Bass("TRN2", target_bir_lowering=False)
    io = dram_io(nc, which)
    with contextlib.ExitStack() as st:
        ec = EngCtx(nc, st)
        block = st.enter_context(nc.Block())
        for i, fn in enumerate(build_fns):
            if i:
                phase_barrier(ec, block)
            with contextlib.ExitStack() as pst:
                fn(nc, ec, block, io, pst)

        def g_done(g):
            ec.fence(g, ["q", "s", "v", "t"])
        block.gpsimd(g_done)
    return run_bass_kernel_spmd(nc, in_maps, core_ids=list(range(n_cores)), trace=trace)


INPUT_NAMES = [
    "xT", "xaT", "lhsA", "bias_row", "identf", "mask0",
    "qk_wT", "v_wT", "g_wT", "o_wT", "gate_wT", "up_wT", "down_wT",
    "ln1_w", "ln1_b", "conv_b", "g_b", "onorm_w", "onorm_b", "ln2_w", "ln2_b",
    "gamma1", "gamma2", "conv_w", "gammaA", "freqs", "headsel", "headexp",
    "sw1T", "sw2T", "sw3T", "w1", "w3", "w2",
    "w_x", "w_y", "gi_w", "ga_w", "l1_b_x", "l1_b_y", "l1_conv_b",
    "l1_gi_b", "l1_ga_b", "l1_a_param", "conv1_w",
    "gate1_w", "gate1_bc", "w1f", "w3f", "w2f",
]
INTER_NAMES = [
    "jslot", "wslot", "offs_d", "xn8T_d", "mrow_d", "u1tab", "u3tab", "hs_d",
    "acc1", "acc3", "h2_d", "xnb_d", "xn1f_d", "xn1b_d", "crow_d",
]
ALL_PHASES = None


def kernel(**inputs):
    phases = [build_p0, build_p1a, build_p1b, build_p1c, build_p2,
              build_p3a, build_p3b]
    which = {n: "in" for n in INPUT_NAMES}
    which.update({n: "int" for n in INTER_NAMES})
    which["outT"] = "out"
    in_maps = []
    for core in range(8):
        hi = build_host_inputs(inputs, core)
        in_maps.append({n: hi[n] for n in INPUT_NAMES})
    res = run_phases(phases, which, in_maps, n_cores=8)
    out = np.stack([np.asarray(res.results[c]["outT"]).T for c in range(8)], axis=0)
    return np.ascontiguousarray(out).astype(np.float32)
